# version 29
# baseline (speedup 1.0000x reference)
import contextlib
import numpy as np
import ml_dtypes
import concourse.bass as bass
import concourse.mybir as mybir
from concourse.bass_utils import run_bass_kernel_spmd

F32 = mybir.dt.float32
BF16 = mybir.dt.bfloat16
AX = mybir.AxisListType.X
ALU = mybir.AluOpType
ACT = mybir.ActivationFunctionType

D = 1024
E2 = 2048
FI = 2816
def F(v):
    return float(np.float32(v))


AMAX = F(0.65625)
C1_2PI = F(6.2831855)
C2_2PI = F(2 * np.pi - 6.283185482025146484375)
PI = F(np.pi)
PI3 = F(3 * np.pi)
HALF_PI = F(np.pi / 2)

_QOPS = {}


def _ensure_qops():
    if _QOPS:
        return _QOPS
    from concourse.dve_spec import (Spec, Src0, Src1, C0, C1, C2, Zero, One,
                                    lower, AluOp, select, Bin)
    from concourse.dve_op import DveOp
    import concourse.dve_ops as dmod

    replaceable = ["BITWISE_XOR_TT_BV", "SHIFT_RIGHT_TT_BV"]

    def mk(name, spec, ref):
        for o in dmod.OPS:
            if o.name == name:
                return o
        shas = {}
        for ver in ("v3", "v4"):
            try:
                shas[ver] = lower(spec, ver=ver, perf_en=False, name=name).sha(ver)
            except Exception:
                pass
        op = DveOp(name, spec, subdim=False, uops_sha=shas, reference=ref)
        if dmod._CUSTOM_DVE_ROW_BASE + len(dmod.OPS) <= 31:
            dmod.OPS.append(op)
            row = dmod._CUSTOM_DVE_ROW_BASE + len(dmod.OPS) - 1
        else:
            victim = replaceable.pop()
            idx = next(i for i, o in enumerate(dmod.OPS) if o.name == victim)
            dmod.OPS[idx] = op
            row = dmod._CUSTOM_DVE_ROW_BASE + idx
            dmod._SUB_OPCODE_FOR_NAME.pop(victim, None)
        dmod._SUB_OPCODE_FOR_NAME[name] = row
        dmod.CUSTOM_DVE_SPECS[name] = op.spec
        dmod.CUSTOM_DVE_REFS[name] = op.ref
        return op

    prod = Src0 * Src1
    _QOPS["u"] = mk("OPQ_U_ANT", Spec(body=Bin(AluOp.ABSOLUTE_VALUE, prod, prod)),
                    lambda in0, in1, s0, s1, imm2: np.abs(in0 * in1))
    u = Src0
    um1 = u - One
    low = ((u > C0) + (u > C1)) + ((um1 > C0) + (um1 > C1))

    def _ref_low(in0, in1, s0, s1, imm2):
        dt_ = in0.dtype
        um = in0 - dt_.type(1.0)
        return ((in0 > s0).astype(dt_) + (in0 > s1).astype(dt_)
                + (um > s0).astype(dt_) + (um > s1).astype(dt_))
    _QOPS["low"] = mk("OPQ_LOW_ANT", Spec(body=low), _ref_low)
    ch1 = u > C0
    ch2 = u > C1
    ch3 = u > C2
    chc = (ch1 + ch2) + (ch3 + ch3)
    high = Src1 + (chc + chc)

    def _ref_high(in0, in1, s0, s1, imm2):
        dt_ = in0.dtype
        c = ((in0 > s0).astype(dt_) + (in0 > s1).astype(dt_)
             + dt_.type(2.0) * (in0 > imm2).astype(dt_))
        return in1 + dt_.type(2.0) * c
    _QOPS["high"] = mk("OPQ_HIGH_ANT", Spec(body=high), _ref_high)
    sgn = select(Src0 < Zero, Zero - Src1, Src1)
    _QOPS["sign"] = mk("OPQ_SIGN_ANT", Spec(body=sgn),
                       lambda in0, in1, s0, s1, imm2: np.where(in0 < 0, -in1, in1))
    return _QOPS


def build_program(tpc=4096, seq_tiles=16):
    nt = tpc // 128
    ng = nt // 2
    assert nt % 2 == 0 and seq_tiles % 2 == 0

    nc = bass.Bass("TRN2", target_bir_lowering=False)

    x_d = nc.dram_tensor("x", [tpc, D], F32, kind="ExternalInput").ap()
    wpsiT_d = nc.dram_tensor("wpsiT", [D, E2], F32, kind="ExternalInput").ap()
    wphiT_d = nc.dram_tensor("wphiT", [D, E2], F32, kind="ExternalInput").ap()
    wgateT_d = nc.dram_tensor("wgateT", [D, D], F32, kind="ExternalInput").ap()
    bpsi_d = nc.dram_tensor("bpsi", [1, E2], F32, kind="ExternalInput").ap()
    bphi_d = nc.dram_tensor("bphi", [1, E2], F32, kind="ExternalInput").ap()
    bgate_d = nc.dram_tensor("bgate", [1, D], F32, kind="ExternalInput").ap()
    omega_d = nc.dram_tensor("omega", [1, D], F32, kind="ExternalInput").ap()
    lgam_d = nc.dram_tensor("lgam", [1, D], F32, kind="ExternalInput").ap()
    dt_d = nc.dram_tensor("dt", [1, 1], F32, kind="ExternalInput").ap()
    wg_d = nc.dram_tensor("wg", [FI, D], F32, kind="ExternalInput").ap()
    wu_d = nc.dram_tensor("wu", [FI, D], F32, kind="ExternalInput").ap()
    wd_d = nc.dram_tensor("wd", [D, FI], F32, kind="ExternalInput").ap()
    out_d = nc.dram_tensor("out", [tpc, D], F32, kind="ExternalOutput").ap()
    sc_g = nc.dram_tensor("sc_g", [128, 8 * FI], BF16, kind="Internal").ap()
    sc_u = nc.dram_tensor("sc_u", [128, 8 * FI], BF16, kind="Internal").ap()
    sc_d = nc.dram_tensor("sc_d", [128, 22 * D], BF16, kind="Internal").ap()
    sc_w = nc.dram_tensor("sc_w", [128, 16 * E2], BF16, kind="Internal").ap()

    identN = np.eye(128, dtype=ml_dtypes.bfloat16)
    ltN = np.triu(np.ones((128, 128), dtype=np.float32)).astype(ml_dtypes.bfloat16)
    e127N = np.zeros((128, 128), dtype=np.float32)
    e127N[127, :] = 1.0
    e127N = e127N.astype(ml_dtypes.bfloat16)
    ident_d = nc.inline_tensor(identN, "identc").ap()
    lt_d = nc.inline_tensor(ltN, "ltc").ap()
    e127_d = nc.inline_tensor(e127N, "e127c").ap()
    ones1b_d = nc.inline_tensor(np.ones((1, 128), dtype=ml_dtypes.bfloat16), "o1b").ap()
    ones1f_d = nc.inline_tensor(np.ones((1, 128), dtype=np.float32), "o1f").ap()
    tp1_d = nc.inline_tensor(np.arange(1, 129, dtype=np.float32).reshape(128, 1),
                             "tp1").ap()
    tn1_d = nc.inline_tensor(-np.arange(1, 129, dtype=np.float32).reshape(128, 1),
                             "tn1").ap()

    ctx = contextlib.ExitStack()

    def sb(name, shape, dt):
        return ctx.enter_context(nc.sbuf_tensor(name, shape, dt))

    class Sem:
        def __init__(self, name):
            self.h = ctx.enter_context(nc.semaphore(name))
            self.v = 0

    sT, sV, sS = Sem("sT"), Sem("sV"), Sem("sS")
    sX, sF, sO, sG = Sem("sX"), Sem("sF"), Sem("sO"), Sem("sG")
    sW = Sem("sW")
    sC = Sem("sC")
    sFg = [Sem("sFg0"), Sem("sFg1"), Sem("sFg2")]
    sFd = [Sem("sFd0"), Sem("sFd1")]
    sD = Sem("sD")

    def inc(instr, s, n=1):
        instr.then_inc(s.h, n)
        s.v += n

    def wge(eng, s, v=None):
        eng.wait_ge(s.h, s.v if v is None else v)

    wgat_b = sb("wgat_b", [128, 8 * D], BF16)
    bpsi_b = sb("bpsi_b", [1, E2], BF16)
    bphi_b = sb("bphi_b", [1, E2], BF16)
    bgat_b = sb("bgat_b", [1, D], BF16)
    identb = sb("identb", [128, 128], BF16)
    ltb = sb("ltb", [128, 128], BF16)
    e127b = sb("e127b", [128, 128], BF16)
    ones1b = sb("ones1b", [1, 128], BF16)
    ones1f = sb("ones1f", [1, 128], F32)
    tp1c = sb("tp1c", [128, 1], F32)
    tn1c = sb("tn1c", [128, 1], F32)
    wm_re = sb("wm_re", [128, D], BF16)
    wm_im = sb("wm_im", [128, D], BF16)
    wp_re = sb("wp_re", [128, D], BF16)
    wp_im = sb("wp_im", [128, D], BF16)
    stat = sb("stat", [128, 16], F32)
    wbuf = sb("wbuf", [128, 8 * E2], BF16)

    PS_A = ctx.enter_context(nc.psum_tensor("PS_A", [128, 2048], F32))
    PS_B = ctx.enter_context(nc.psum_tensor("PS_B", [128, 2048], F32))
    PS_Tv = PS_B[:].bitcast(BF16)

    with nc.Block() as blk:
        def _sy(e):
            for dst, src in [(identb, ident_d), (ltb, lt_d), (e127b, e127_d),
                             (ones1b, ones1b_d), (ones1f, ones1f_d),
                             (tp1c, tp1_d), (tn1c, tn1_d)]:
                inc(e.dma_start(out=dst[:], in_=src), sX, 16)
        blk.sync(_sy)



    with contextlib.ExitStack() as tctx:
        tsb = lambda n, s, d: tctx.enter_context(nc.sbuf_tensor(n, s, d))
        tbl_sc = tsb("tbl_sc", [128, 2 * D], F32)
        tbl_e = tsb("tbl_e", [128, D], F32)
        tbl_p = tsb("tbl_p", [128, D], F32)
        tbl_m = tsb("tbl_m", [128, D], F32)
        tbl_s = tsb("tbl_s", [128, D], F32)
        tbl_c = tsb("tbl_c", [128, D], F32)
        tbl_k = tsb("tbl_k", [128, D], F32)
        rowa = tsb("rowa", [1, D], F32)
        roww = tsb("roww", [1, D], F32)
        dtab = tsb("dtab", [1, 1], F32)
        lgam_r = tsb("lgam_r", [1, D], F32)
        omega_r = tsb("omega_r", [1, D], F32)
        dt_r = tsb("dt_r", [1, 1], F32)

        with nc.Block() as blk:
            def _sy0(e):
                for dst, src_ in [(lgam_r, lgam_d), (omega_r, omega_d),
                                  (dt_r, dt_d)]:
                    inc(e.dma_start(out=dst[:], in_=src_), sX, 16)
            blk.sync(_sy0)

            def _sc(e):
                wge(e, sX)
                e.activation(dtab[:], dt_r[:], ACT.Abs)
                i = e.activation(rowa[:], lgam_r[:], ACT.Exp)
                inc(i, sS)
            blk.scalar(_sc)

            def _v(e):
                wge(e, sS)
                e.tensor_scalar(rowa[:], rowa[:], dtab[0:1, 0:1], AMAX,
                                ALU.mult, ALU.min)
                e.tensor_scalar(roww[:], omega_r[:], dtab[0:1, 0:1], None, ALU.mult)
                inc(e.drain(), sV)
            blk.vector(_v)

            def _t(e):
                wge(e, sV)
                for j in range(2):
                    e.matmul(PS_A[:, 512 * j:512 * (j + 1)], ones1f[:],
                             rowa[0:1, 512 * j:512 * (j + 1)], start=True, stop=True)
                for j in range(2):
                    i = e.matmul(PS_A[:, 1024 + 512 * j:1536 + 512 * j], ones1f[:],
                                 roww[0:1, 512 * j:512 * (j + 1)], start=True,
                                 stop=True)
                inc(i, sT)
            blk.tensor(_t)

            def _v2(e):
                wge(e, sT)
                e.tensor_copy(tbl_sc[:], PS_A[:])
                inc(e.drain(), sV)
            blk.vector(_v2)

        a_bc = tbl_sc[:, 0:D]
        w_bc = tbl_sc[:, D:2 * D]

        def emit_table(sgn, ecol, o_re, o_im):
            with nc.Block() as blk:
                def _v(e):
                    wge(e, sV)
                    e.tensor_scalar(tbl_e[:], a_bc, ecol[:, 0:1], None, ALU.mult)
                    e.tensor_scalar(tbl_p[:], w_bc, tp1c[:, 0:1], None, ALU.mult)
                    e.drain()
                    e.tensor_scalar(tbl_k[:], tbl_p[:], PI, None, ALU.is_gt)
                    e.drain()
                    e.scalar_tensor_tensor(tbl_k[:], tbl_p[:], PI3, tbl_k[:],
                                           ALU.is_gt, ALU.add)
                    e.drain()
                    e.scalar_tensor_tensor(tbl_p[:], tbl_k[:], F(-C1_2PI), tbl_p[:],
                                           ALU.mult, ALU.add)
                    e.drain()
                    e.scalar_tensor_tensor(tbl_p[:], tbl_k[:], F(-C2_2PI), tbl_p[:],
                                           ALU.mult, ALU.add)
                    inc(e.drain(), sV)
                blk.vector(_v)

                def _s(e):
                    wge(e, sV)
                    e.activation(tbl_m[:], tbl_e[:], ACT.Exp)
                    i = e.activation(tbl_s[:], tbl_p[:], ACT.Sin)
                    inc(i, sS)
                blk.scalar(_s)

                def _v2(e):
                    wge(e, sS)
                    e.tensor_scalar(tbl_c[:], tbl_p[:], F(-1.0), HALF_PI,
                                    ALU.mult, ALU.add)
                    e.drain()
                    e.tensor_scalar(tbl_k[:], tbl_c[:], PI, None, ALU.is_gt)
                    e.drain()
                    e.scalar_tensor_tensor(tbl_c[:], tbl_k[:], F(-C1_2PI), tbl_c[:],
                                           ALU.mult, ALU.add)
                    e.drain()
                    e.scalar_tensor_tensor(tbl_c[:], tbl_k[:], F(-C2_2PI), tbl_c[:],
                                           ALU.mult, ALU.add)
                    inc(e.drain(), sV)
                blk.vector(_v2)

                def _s2(e):
                    wge(e, sV)
                    i = e.activation(tbl_c[:], tbl_c[:], ACT.Sin)
                    inc(i, sS)
                blk.scalar(_s2)

                def _v3(e):
                    wge(e, sS)
                    e.tensor_tensor(o_re[:], tbl_m[:], tbl_c[:], ALU.mult)
                    e.scalar_tensor_tensor(o_im[:], tbl_m[:], F(sgn), tbl_s[:],
                                           ALU.mult, ALU.mult)
                    inc(e.drain(), sV)
                blk.vector(_v3)

        emit_table(-1.0, tp1c, wm_re, wm_im)
        emit_table(+1.0, tn1c, wp_re, wp_im)

    def bc16(ap2, nblk):
        return ap2.unsqueeze(2).broadcast_to([128, nblk, 16])

    def emit_quant(ve, z_ap, n, out_bf, QA, QB, AZ, bm, r6, sdq,
                   sgn_ap=None, az_mark=None, fuse=False, az_sem=None,
                   az_sep=False, early_free=False):
        nb = n // 16
        zv = z_ap.rearrange("p (b k) -> p b k", k=16)
        az3 = AZ.rearrange("p (b k) -> p b k", k=16)

        def E(i):
            inc(i, sD)
            return sD.v

        def W(m):
            ve.wait_ge(sD.h, m)

        if az_sem is None:
            az_sem = sS
        if sgn_ap is not None and not fuse:
            ve.wait_ge(az_sem.h, az_mark)
        m_red = E(ve.reduce_max(bm, zv, axis=AX, apply_absolute_value=True))
        if fuse:
            W(m_red)
            if not az_sep:
                ve.wait_ge(az_sem.h, az_mark)
            i_az = ve.scalar_tensor_tensor(AZ, z_ap, F(-1.0), z_ap,
                                           ALU.mult, ALU.max)
            if early_free:
                inc(i_az, sV)
                m_az = ("V", sV.v)
            else:
                inc(i_az, sD)
                m_az = ("D", sD.v)
        if sgn_ap is None:
            m_az = E(ve.scalar_tensor_tensor(AZ, z_ap, F(-1.0), z_ap, ALU.mult,
                                             ALU.max))
        W(m_red)
        E(ve.reciprocal(r6, bm))
        W(sD.v)
        m_r6 = E(ve.tensor_scalar(r6, r6, F(6.0), F(1e12), ALU.mult, ALU.min))
        W(m_r6)
        m_sd = E(ve.reciprocal(sdq, r6))
        W(m_sd)
        m_sdh = E(ve.tensor_scalar(sdq, sdq, F(0.5), None, ALU.mult))
        if sgn_ap is None:
            W(m_az)
        if fuse:
            if m_az[0] == "V":
                ve.wait_ge(sV.h, m_az[1])
            else:
                W(m_az[1])
        m_u = E(ve.tensor_tensor(az3, az3, bc16(r6, nb), ALU.mult))
        W(m_u)
        a = E(ve.tensor_scalar(QA, AZ, F(0.25), None, ALU.is_gt))
        c = E(ve.tensor_scalar(QB, AZ, F(2.5), None, ALU.is_gt))
        for thr_a, thr_c in ((0.75, 3.5), (1.25, 5.0), (1.75, 5.0)):
            W(a)
            a = E(ve.scalar_tensor_tensor(QA, AZ, F(thr_a), QA, ALU.is_gt,
                                          ALU.add))
            W(c)
            c = E(ve.scalar_tensor_tensor(QB, AZ, F(thr_c), QB, ALU.is_gt,
                                          ALU.add))
        W(a)
        W(c)
        q2 = E(ve.scalar_tensor_tensor(QB, QB, F(2.0), QA, ALU.mult, ALU.add))
        W(q2)
        if sgn_ap is None:
            s1 = E(ve.scalar_tensor_tensor(QA, z_ap, F(0.0), QB, ALU.is_gt,
                                           ALU.mult))
            s2 = E(ve.scalar_tensor_tensor(AZ, z_ap, F(0.0), QB, ALU.is_lt,
                                           ALU.mult))
            W(s1)
            W(s2)
            qs = E(ve.tensor_tensor(QA, QA, AZ, ALU.subtract))
            W(qs)
            W(m_sdh)
            E(ve.tensor_tensor(out_bf, QA.rearrange("p (b k) -> p b k", k=16),
                               bc16(sdq, nb), ALU.mult))
        else:
            W(m_sdh)
            t = E(ve.tensor_tensor(out_bf,
                                   QB.rearrange("p (b k) -> p b k", k=16),
                                   bc16(sdq, nb), ALU.mult))
            W(t)
            if fuse and az_sep:
                ve.wait_ge(az_sem.h, az_mark)
            E(ve.tensor_tensor(out_bf, out_bf, sgn_ap, ALU.mult))

    psiw_mark = [0]
    with contextlib.ExitStack() as qctx:
        qsb = lambda n, s, d: qctx.enter_context(nc.sbuf_tensor(n, s, d))
        slab_in = qsb("slab_in", [128, 3 * FI], F32)
        sAZ = qsb("sAZ", [128, FI], F32)
        sQA = qsb("sQA", [128, FI], F32)
        sQB = qsb("sQB", [128, FI], F32)
        sbm = qsb("sbm", [128, FI // 16], F32)
        sr6 = qsb("sr6", [128, FI // 16], F32)
        ssd = qsb("ssd", [128, FI // 16], F32)
        qout = qsb("qout", [128, 3 * FI], BF16)
        bigT = qsb("bigT", [128, 8 * FI], BF16)

        sSg = qsb("sSg", [128, 2 * FI], BF16)
        a_items = ([("psi", c, E2) for c in range(8)]
                   + [("phi", c, E2) for c in range(8)]
                   + [("gate", c, D) for c in range(8)])
        q_items = ([("wg", c, D) for c in range(22)]
                   + [("wu", c, D) for c in range(22)]
                   + [("wd", c, FI) for c in range(8)])
        items = []
        for k in range(len(q_items)):
            if k < len(a_items):
                items.append(a_items[k])
            items.append(q_items[k])
        items.append(("bias", 0, 0))
        SRCD = {"psi": wpsiT_d, "phi": wphiT_d, "gate": wgateT_d,
                "wg": wg_d, "wu": wu_d, "wd": wd_d}
        NCH = {"wg": 22, "wu": 22, "wd": 8}
        SCRD = {"wg": (sc_g, 8 * FI), "wu": (sc_u, 8 * FI), "wd": (sc_d, 22 * D)}
        HBASE = {"psi": 0, "phi": 8 * E2}
        QK = ("wg", "wu", "wd")
        vm = {}
        slab_free = {}
        slab_mark = {}
        sign_mark = {}
        evac_mark = {}
        qslot = {0: None, 1: None, 2: None}
        ev_last = [0]

        with nc.Block() as blk:
            def emit_dma(i):
                kind, c, w = items[i]

                def _sy(e, i=i, kind=kind, c=c, w=w):
                    if i >= 3:
                        wge(e, sV, slab_free[i - 3])
                        if (i - 3) in sign_mark:
                            wge(e, sG, sign_mark[i - 3])
                    for jb in (i - 1, i - 2):
                        if jb >= 0 and items[jb][0] == "bias":
                            wge(e, sV, vm[jb])
                    if kind == "bias":
                        wge(e, sV, vm[i - 1])
                        wge(e, sV, vm[i - 2])
                        inc(e.dma_start(out=slab_in[0:1, 0:E2], in_=bpsi_d),
                            sX, 16)
                        inc(e.dma_start(out=slab_in[0:1, E2:2 * E2], in_=bphi_d),
                            sX, 16)
                        inc(e.dma_start(out=slab_in[0:1, 2 * E2:2 * E2 + D],
                                        in_=bgate_d), sX, 16)
                        slab_mark[i] = sX.v
                    else:
                        inc(e.dma_start(
                            out=slab_in[:, (i % 3) * FI:(i % 3) * FI + w],
                            in_=SRCD[kind][128 * c:128 * (c + 1), :]),
                            sFg[i % 3], 16)
                        slab_mark[i] = sFg[i % 3].v
                blk.sync(_sy)

            def emit_sign(i):
                kind, c, w = items[i]

                def _s(e, i=i, w=w):
                    wge(e, sG)
                    if i >= 2:
                        wge(e, sV, vm[i - 2])
                    e.wait_ge(sFg[i % 3].h, slab_mark[i])
                    i_ = e.activation(sSg[:, (i % 2) * FI:(i % 2) * FI + w],
                                      slab_in[:, (i % 3) * FI:(i % 3) * FI + w],
                                      ACT.Sign)
                    inc(i_, sG)
                    sign_mark[i] = sG.v
                blk.scalar(_s)

            def wait_qslot(e, spec):
                if spec is None:
                    return
                kind_, val = spec
                if kind_ == "d":
                    e.wait_ge(val[0].h, val[1])
                else:
                    wge(e, sT, val)

            def emit_consume(i):
                kind, c, w = items[i]
                if kind == "bias":
                    def _vb(e, i=i):
                        e.wait_ge(sX.h, slab_mark[i])
                        e.tensor_copy(bpsi_b[:], slab_in[0:1, 0:E2])
                        e.tensor_copy(bphi_b[:], slab_in[0:1, E2:2 * E2])
                        e.tensor_copy(bgat_b[:], slab_in[0:1, 2 * E2:2 * E2 + D])
                        inc(e.drain(), sV)
                        vm[i] = sV.v
                        slab_free[i] = sV.v
                    blk.vector(_vb)
                    return
                if kind in ("psi", "phi", "gate"):
                    spec = qslot[i % 3] if kind != "gate" else None

                    def _v(e, i=i, kind=kind, c=c, w=w, spec=spec):
                        e.wait_ge(sFg[i % 3].h, slab_mark[i])
                        wait_qslot(e, spec)
                        dst = (wgat_b[:, D * c:D * (c + 1)] if kind == "gate"
                               else qout[:, (i % 3) * FI:(i % 3) * FI + w])
                        e.tensor_copy(dst,
                                      slab_in[:, (i % 3) * FI:(i % 3) * FI + w])
                        inc(e.drain(), sV)
                        vm[i] = sV.v
                        slab_free[i] = sV.v
                    blk.vector(_v)
                    if kind != "gate":
                        osem = (sFd[0], sFd[1], sC)[i % 3]

                        def _sy2(e, i=i, kind=kind, c=c, w=w, osem=osem):
                            wge(e, sV, vm[i])
                            inc(e.dma_start(
                                out=sc_w[:, HBASE[kind] + w * c:
                                         HBASE[kind] + w * (c + 1)],
                                in_=qout[:, (i % 3) * FI:(i % 3) * FI + w]),
                                osem, 16)
                        blk.sync(_sy2)
                        qslot[i % 3] = ("d", (osem, osem.v))
                    return
                n_tb = w // 128
                spec = qslot[i % 3]

                def _v(e, i=i, w=w, spec=spec):
                    e.wait_ge(sFg[i % 3].h, slab_mark[i])
                    wait_qslot(e, spec)
                    sl = slab_in[:, (i % 3) * FI:(i % 3) * FI + w]
                    emit_quant(e, sl, w,
                               qout[:, (i % 3) * FI:(i % 3) * FI + w],
                               sQA[:, 0:w], sQB[:, 0:w], sAZ[:, 0:w],
                               sbm[:, 0:w // 16], sr6[:, 0:w // 16],
                               ssd[:, 0:w // 16],
                               sgn_ap=sSg[:, (i % 2) * FI:(i % 2) * FI + w],
                               az_mark=sign_mark[i], fuse=True, az_sem=sG,
                               az_sep=True, early_free=True)
                    slab_free[i] = sV.v
                    inc(e.drain(), sV)
                    vm[i] = sV.v
                blk.vector(_v)
                n_waves = (n_tb + 7) // 8
                wv_marks = []
                ev_base = ev_last[0]
                for wv in range(n_waves):
                    lo, hi = 8 * wv, min(8 * (wv + 1), n_tb)

                    def _t(e, i=i, lo=lo, hi=hi, wv=wv, ev_base=ev_base):
                        wge(e, sS, ev_base + wv)
                        if wv == 0:
                            wge(e, sV, vm[i])
                        for b in range(lo, hi):
                            i_ = e.transpose(
                                PS_Tv[:, 128 * (b - lo):128 * (b - lo + 1)],
                                qout[:, (i % 3) * FI + 128 * b:
                                     (i % 3) * FI + 128 * (b + 1)], identb[:])
                        inc(i_, sT)
                        wv_marks.append(sT.v)
                    blk.tensor(_t)

                def _s(e, i=i, kind=kind, c=c, n_tb=n_tb, n_waves=n_waves,
                       wv_marks=wv_marks):
                    nch = NCH[kind]
                    for wv in range(n_waves):
                        lo, hi = 8 * wv, min(8 * (wv + 1), n_tb)
                        wge(e, sT, wv_marks[wv])
                        if c == 0 and wv == 0:
                            wge(e, sF)
                        bt = bigT[:].rearrange("p (b cc) -> p b cc",
                                               cc=nch * 128)
                        i_ = e.activation(
                            bt[:, lo:hi, 128 * c:128 * (c + 1)],
                            PS_Tv.rearrange("p (b cc) -> p b cc", cc=128)
                            [:, 0:hi - lo, :], ACT.Copy)
                        inc(i_, sS)
                        ev_last[0] = sS.v
                    evac_mark[i] = sS.v
                blk.scalar(_s)
                qslot[i % 3] = ("t", wv_marks[-1])
                if c == NCH[kind] - 1:
                    scr, scr_cols = SCRD[kind]

                    def _syw(e, i=i, scr=scr, scr_cols=scr_cols):
                        wge(e, sS, evac_mark[i])
                        inc(e.dma_start(out=scr[:, 0:scr_cols],
                                        in_=bigT[:, 0:scr_cols]), sF, 16)
                    blk.sync(_syw)

            nit = len(items)
            for i in range(nit - 1):
                emit_dma(i)
                if items[i][0] in QK:
                    emit_sign(i)
                if i >= 1:
                    emit_consume(i - 1)
            emit_consume(nit - 2)
            emit_dma(nit - 1)
            emit_consume(nit - 1)

            def _sy_wb(e):
                wge(e, sFd[0])
                wge(e, sFd[1])
                wge(e, sC)
                inc(e.dma_start(out=wbuf[:], in_=sc_w[:, 0:8 * E2]), sW, 16)
                psiw_mark[0] = sW.v
            blk.sync(_sy_wb)

    with nc.Block() as blk:
        def _fence(e):
            wge(e, sF)
            wge(e, sX)
            wge(e, sFd[0])
            wge(e, sFd[1])
            wge(e, sC)
            wge(e, sG)
        blk.sync(_fence)

    xbuf = sb("xbuf", [128, 2 * E2], F32)
    xn = sb("xn", [128, E2], F32)
    xnT = sb("xnT", [128, 2 * D], BF16)
    psi = sb("psi", [128, 2 * E2], BF16)
    phi = sb("phi", [128, 2 * E2], BF16)
    gat = sb("gat", [128, 2 * D], BF16)
    scr2k = sb("scr2k", [128, E2], BF16)
    xnbf = scr2k
    qsA = sb("qsA", [128, D], F32)
    qsB = sb("qsB", [128, D], F32)
    hfin = sb("hfin", [128, 2 * E2], BF16)
    bmF = sb("bmF", [128, 64], F32)
    r6F = sb("r6F", [128, 64], F32)
    sdF = sb("sdF", [128, 64], F32)
    xqT = sb("xqT", [128, E2], BF16)
    hsl = sb("hsl", [128, 1024], F32)
    silu_t = sb("silu_t", [128, 1024], F32)
    hq = sb("hq", [128, 2 * FI], BF16)
    hqT = sb("hqT", [128, 2 * FI], BF16)
    gusl = sb("gusl", [128, 3 * 4096], BF16)
    dsl = sb("dsl", [128, 2 * E2], BF16)
    sgnb = sb("sgnb", [128, 1024], BF16)

    ISLW = [512, 512, 512, 512, 512, 256]

    fin_mark = [0]
    out_mark = [0]
    phi_mark = [0]
    phiw_mark = [0]
    xqT_mark = [0]
    PS_Abf = PS_A[:].bitcast(BF16)
    PS_Bbf = PS_B[:].bitcast(BF16)
    silu_bf = silu_t[:].bitcast(BF16)
    hqTf = hqT[:].bitcast(F32)

    def em_sy_head(g):
        xsl = E2 * (g % 2)

        def _sy(e):
            wge(e, sO, out_mark[0])
            for i in range(2):
                r = 128 * (2 * g + i)
                inc(e.dma_start(out=xbuf[:, xsl + 1024 * i:xsl + 1024 * (i + 1)],
                                in_=x_d[r:r + 128, :]), sX, 16)
            if g > 0:
                wge(e, sT, phi_mark[0])
                inc(e.dma_start(out=wbuf[:], in_=sc_w[:, 0:8 * E2]), sW, 16)
                psiw_mark[0] = sW.v
        blk.sync(_sy)

    def em_v_ln(g):
        xsl = E2 * (g % 2)

        def _v(e):
            wge(e, sX)
            x3 = xbuf[:, xsl:xsl + E2].rearrange("p (t c) -> p t c", c=1024)
            e.reduce_sum(stat[:, 0:2], x3, axis=AX)
            inc(e.drain(), sD)
            e.tensor_scalar(stat[:, 0:2], stat[:, 0:2], F(1.0 / 1024.0), None,
                            ALU.mult)
            inc(e.drain(), sV)
        blk.vector(_v)
        return sV.v

    def em_s_ln(g):
        xsl = E2 * (g % 2)

        def _s(e):
            wge(e, sS)
            wge(e, sX)
            wge(e, sV)
            for ti in range(2):
                e.activation(scr2k[:, 1024 * ti:1024 * (ti + 1)],
                             xbuf[:, xsl + 1024 * ti:xsl + 1024 * (ti + 1)],
                             ACT.Square, accum_out=stat[:, 4 + ti:5 + ti])
            inc(e.drain(), sS)
        blk.scalar(_s)
        return sS.v

    def em_v_ln2(g, sq_mark):
        xsl = E2 * (g % 2)

        def _v(e):
            wge(e, sS, sq_mark)
            var, y, t, sq = stat[:, 6:8], stat[:, 8:10], stat[:, 10:12], stat[:, 2:4]
            e.tensor_tensor(sq, stat[:, 0:2], stat[:, 0:2], ALU.mult)
            inc(e.drain(), sD)
            e.scalar_tensor_tensor(var, stat[:, 4:6], F(1.0 / 1024.0), sq,
                                   ALU.mult, ALU.subtract)
            inc(e.drain(), sD)
            e.tensor_scalar(var, var, F(1e-5), None, ALU.add)
            inc(e.drain(), sD)
            e.tensor_scalar(y, var, F(-0.5), F(1.5), ALU.mult, ALU.add)
            inc(e.drain(), sD)
            for _ in range(4):
                e.tensor_tensor(t, y, y, ALU.mult)
                inc(e.drain(), sD)
                e.tensor_tensor(t, t, var, ALU.mult)
                inc(e.drain(), sD)
                e.tensor_scalar(t, t, F(-0.5), F(1.5), ALU.mult, ALU.add)
                inc(e.drain(), sD)
                e.tensor_tensor(y, y, t, ALU.mult)
                inc(e.drain(), sD)
            e.scalar_tensor_tensor(stat[:, 10:12], stat[:, 0:2], F(-1.0),
                                   stat[:, 8:10], ALU.mult, ALU.mult)
            inc(e.drain(), sV)
        blk.vector(_v)
        return sV.v

    def em_s_ln2a(g, ln2_v):
        xsl = E2 * (g % 2)

        def _s(e):
            wge(e, sS)
            wge(e, sV, ln2_v)
            for ti in range(2):
                xb = xbuf[:, xsl + 1024 * ti:xsl + 1024 * (ti + 1)]
                e.activation(xn[:, 1024 * ti:1024 * (ti + 1)], xb,
                             ACT.Identity,
                             scale=stat[:, 8 + ti:9 + ti],
                             bias=stat[:, 10 + ti:11 + ti])
                e.activation(xnbf[:, D * ti:D * (ti + 1)], xb, ACT.Identity,
                             scale=stat[:, 8 + ti:9 + ti],
                             bias=stat[:, 10 + ti:11 + ti])
            inc(e.drain(), sS)
        blk.scalar(_s)
        return sS.v

    def em_t_tr(g, ti, lnc_s, bank_free_s):
        def _t(e):
            e.wait_ge(sS.h, lnc_s)
            e.wait_ge(sS.h, bank_free_s)
            for k in range(8):
                i = e.transpose(PS_Abf[:, 128 * k:128 * (k + 1)],
                                xnbf[:, D * ti + 128 * k:D * ti + 128 * (k + 1)],
                                identb[:])
            inc(i, sT)
        blk.tensor(_t)
        return sT.v

    def em_s_tr(g, ti, t_mark):
        def _s(e):
            wge(e, sT, t_mark)
            i = e.activation(xnT[:, D * ti:D * (ti + 1)], PS_Abf[:, 0:1024],
                             ACT.Copy)
            inc(i, sS)
        blk.scalar(_s)
        return sS.v

    def em_t_psi(g, ti, s_mark):
        def _t(e):
            wge(e, sS, s_mark)
            e.wait_ge(sW.h, psiw_mark[0])
            for k in range(8):
                for j in range(4):
                    e.matmul(PS_A[:, 512 * j:512 * (j + 1)],
                             xnT[:, D * ti + 128 * k:D * ti + 128 * (k + 1)],
                             wbuf[:, E2 * k + 512 * j:E2 * k + 512 * (j + 1)],
                             start=(k == 0), stop=False)
            for j in range(4):
                i = e.matmul(PS_A[:, 512 * j:512 * (j + 1)], ones1b[:],
                             bpsi_b[0:1, 512 * j:512 * (j + 1)],
                             start=False, stop=True)
            inc(i, sT)
        blk.tensor(_t)
        return sT.v

    def em_s_psi(g, ti, t_mark):
        def _s(e):
            wge(e, sT, t_mark)
            i = e.activation(psi[:, E2 * ti:E2 * (ti + 1)], PS_A[:], ACT.Copy)
            inc(i, sS)
        blk.scalar(_s)
        return sS.v

    def em_t_gat(g, ti, s_mark):
        def _t(e):
            wge(e, sS, s_mark)
            for k in range(8):
                for j in range(2):
                    e.matmul(PS_A[:, 512 * j:512 * (j + 1)],
                             xnT[:, D * ti + 128 * k:D * ti + 128 * (k + 1)],
                             wgat_b[:, D * k + 512 * j:D * k + 512 * (j + 1)],
                             start=(k == 0), stop=False)
            for j in range(2):
                i = e.matmul(PS_A[:, 512 * j:512 * (j + 1)], ones1b[:],
                             bgat_b[0:1, 512 * j:512 * (j + 1)],
                             start=False, stop=True)
            inc(i, sT)
        blk.tensor(_t)
        return sT.v

    def em_s_gat(g, ti, t_mark):
        def _s(e):
            wge(e, sT, t_mark)
            i = e.activation(gat[:, D * ti:D * (ti + 1)], PS_A[:, 0:1024],
                             ACT.Sigmoid)
            inc(i, sS)
        blk.scalar(_s)
        return sS.v

    def em_sy_phi(g, psi1_t):
        def _sy(e):
            wge(e, sT, psi1_t)
            inc(e.dma_start(out=wbuf[:], in_=sc_w[:, 8 * E2:16 * E2]), sW, 16)
            phiw_mark[0] = sW.v
        blk.sync(_sy)

    def em_t_phi(g, ti, s_mark):
        def _t(e):
            wge(e, sS, s_mark)
            e.wait_ge(sW.h, phiw_mark[0])
            for k in range(8):
                for j in range(4):
                    e.matmul(PS_A[:, 512 * j:512 * (j + 1)],
                             xnT[:, D * ti + 128 * k:D * ti + 128 * (k + 1)],
                             wbuf[:, E2 * k + 512 * j:E2 * k + 512 * (j + 1)],
                             start=(k == 0), stop=False)
            for j in range(4):
                i = e.matmul(PS_A[:, 512 * j:512 * (j + 1)], ones1b[:],
                             bphi_b[0:1, 512 * j:512 * (j + 1)],
                             start=False, stop=True)
            inc(i, sT)
            if ti == 1:
                phi_mark[0] = sT.v
        blk.tensor(_t)
        return sT.v

    def em_s_phi(g, ti, t_mark):
        def _s(e):
            wge(e, sT, t_mark)
            i = e.activation(phi[:, E2 * ti:E2 * (ti + 1)], PS_A[:], ACT.Copy)
            inc(i, sS)
        blk.scalar(_s)
        return sS.v

    def em_v_a(g, ti, s_mark, t_mark, dst):
        def _v(e):
            wge(e, sS, s_mark)
            wge(e, sT, t_mark)
            pr = psi[:, E2 * ti:E2 * ti + D]
            pim = psi[:, E2 * ti + D:E2 * (ti + 1)]
            g_ = gat[:, D * ti:D * (ti + 1)]
            t1, t2 = xqT[:, 0:D], xqT[:, D:2 * D]
            e.tensor_tensor(t1, pr, wm_re[:], ALU.mult)
            i = e.tensor_tensor(t2, pim, wm_im[:], ALU.mult)
            inc(i, sD)
            e.wait_ge(sD.h, sD.v)
            i = e.tensor_tensor(dst[:, 0:D], t1, t2, ALU.subtract)
            inc(i, sD)
            e.wait_ge(sD.h, sD.v)
            e.tensor_tensor(t1, pr, wm_im[:], ALU.mult)
            i = e.tensor_tensor(t2, pim, wm_re[:], ALU.mult)
            inc(i, sD)
            e.wait_ge(sD.h, sD.v)
            i = e.tensor_tensor(dst[:, D:E2], t1, t2, ALU.add)
            inc(i, sD)
            e.wait_ge(sD.h, sD.v)
            e.tensor_tensor(dst[:, 0:D], dst[:, 0:D], g_, ALU.mult)
            e.tensor_tensor(dst[:, D:E2], dst[:, D:E2], g_, ALU.mult)
            inc(e.drain(), sV)
        blk.vector(_v)
        return sV.v

    def em_t_c(g, ti, v_mark, s_mark, src):
        tt = 2 * g + ti

        def _t(e):
            wge(e, sV, v_mark)
            wge(e, sS, s_mark)
            first = (tt % seq_tiles == 0)
            prev = E2 * ((tt + 1) % 2)
            for j in range(4):
                i = e.matmul(PS_A[:, 512 * j:512 * (j + 1)], ltb[:],
                             src[:, 512 * j:512 * (j + 1)],
                             start=True, stop=first)
            if not first:
                for j in range(4):
                    i = e.matmul(PS_A[:, 512 * j:512 * (j + 1)], e127b[:],
                                 hfin[:, prev + 512 * j:prev + 512 * (j + 1)],
                                 start=False, stop=True)
            inc(i, sT)
        blk.tensor(_t)
        return sT.v

    def em_s_pv(g, ti, t_mark, dst):
        def _s(e):
            wge(e, sT, t_mark)
            i = e.activation(dst[:], PS_A[:], ACT.Copy)
            inc(i, sS)
        blk.scalar(_s)
        return sS.v

    def em_v_p(g, ti, s_mark, src):
        tt = 2 * g + ti

        def _v(e):
            wge(e, sS, s_mark)
            cur = E2 * (tt % 2)
            t1, t2 = xqT[:, 0:D], xqT[:, D:2 * D]
            e.tensor_tensor(t1, src[:, 0:D], wp_re[:], ALU.mult)
            i = e.tensor_tensor(t2, src[:, D:E2], wp_im[:], ALU.mult)
            inc(i, sD)
            e.wait_ge(sD.h, sD.v)
            i = e.tensor_tensor(hfin[:, cur:cur + D], t1, t2, ALU.subtract)
            inc(i, sD)
            e.wait_ge(sD.h, sD.v)
            e.tensor_tensor(t1, src[:, 0:D], wp_im[:], ALU.mult)
            i = e.tensor_tensor(t2, src[:, D:E2], wp_re[:], ALU.mult)
            inc(i, sD)
            e.wait_ge(sD.h, sD.v)
            i = e.tensor_tensor(hfin[:, cur + D:cur + E2], t1, t2, ALU.add)
            inc(e.drain(), sV)
        blk.vector(_v)
        return sV.v

    def em_v_pr(g):
        def _v(e):
            h3 = hfin[:].rearrange("p (q c) -> p q c", c=1024)
            e.reduce_sum(stat[:, 0:4], h3, axis=AX)
            inc(e.drain(), sD)
            e.tensor_scalar(stat[:, 0:4], stat[:, 0:4], F(1.0 / 1024.0), None,
                            ALU.mult)
            inc(e.drain(), sV)
        blk.vector(_v)
        return sV.v

    def em_s_pr(g, p1_v):
        def _s(e):
            wge(e, sS)
            wge(e, sV, p1_v)
            dumps = [scr2k[:, 0:1024], scr2k[:, 1024:2048],
                     gat[:, 0:1024], gat[:, 1024:2048]]
            for q in range(4):
                i = e.activation(dumps[q], hfin[:, 1024 * q:1024 * (q + 1)],
                                 ACT.Square, accum_out=stat[:, 8 + q:9 + q])
            inc(i, sS)
        blk.scalar(_s)
        return sS.v

    def em_v_pr2(g, s_mark):
        def _v(e):
            wge(e, sS, s_mark)
            sq = stat[:, 4:8]
            e.tensor_tensor(sq, stat[:, 0:4], stat[:, 0:4], ALU.mult)
            inc(e.drain(), sD)
            e.scalar_tensor_tensor(stat[:, 8:12], stat[:, 8:12], F(1.0 / 1024.0),
                                   sq, ALU.mult, ALU.subtract)
            inc(e.drain(), sV)
        blk.vector(_v)
        return sV.v

    def em_s_pr2(g, v_mark):
        def _s(e):
            wge(e, sV, v_mark)
            i = e.sqrt(stat[:, 12:16], stat[:, 8:12])
            inc(i, sS)
        blk.scalar(_s)
        return sS.v

    def em_v_pr3(g, s_mark):
        xsl = E2 * (g % 2)

        def _v(e):
            wge(e, sS, s_mark)
            y, rr, var = stat[:, 12:16], stat[:, 4:8], stat[:, 8:12]
            for _ in range(2):
                e.reciprocal(rr, y)
                inc(e.drain(), sD)
                e.tensor_tensor(rr, var, rr, ALU.mult)
                inc(e.drain(), sD)
                e.tensor_tensor(y, y, rr, ALU.add)
                inc(e.drain(), sD)
                e.tensor_scalar(y, y, F(0.5), None, ALU.mult)
                inc(e.drain(), sD)
            e.tensor_scalar(y, y, F(1e-6), None, ALU.add)
            inc(e.drain(), sD)
            e.reciprocal(rr, y)
            inc(e.drain(), sD)
            for ti in range(2):
                for p_ in range(2):
                    q = 2 * ti + p_
                    e.tensor_scalar(psi[:, E2 * ti + 1024 * p_:
                                        E2 * ti + 1024 * (p_ + 1)],
                                    hfin[:, 1024 * q:1024 * (q + 1)],
                                    stat[:, q:q + 1], stat[:, 4 + q:5 + q],
                                    ALU.subtract, ALU.mult)
            inc(e.drain(), sD)
            e.wait_ge(sD.h, sD.v)
            for ti in range(2):
                e.tensor_tensor(psi[:, E2 * ti:E2 * ti + D],
                                psi[:, E2 * ti:E2 * ti + D],
                                phi[:, E2 * ti:E2 * ti + D], ALU.mult)
                e.tensor_tensor(psi[:, E2 * ti + D:E2 * (ti + 1)],
                                psi[:, E2 * ti + D:E2 * (ti + 1)],
                                phi[:, E2 * ti + D:E2 * (ti + 1)], ALU.mult)
            inc(e.drain(), sD)
            e.wait_ge(sD.h, sD.v)
            for ti in range(2):
                a_ap = xbuf[:, xsl + 1024 * ti:xsl + 1024 * (ti + 1)]
                e.tensor_tensor(a_ap, a_ap, psi[:, E2 * ti:E2 * ti + D], ALU.add)
                inc(e.drain(), sD)
                e.wait_ge(sD.h, sD.v)
                i = e.tensor_tensor(a_ap, a_ap,
                                    psi[:, E2 * ti + D:E2 * (ti + 1)], ALU.add)
                inc(i, sD)
            inc(e.drain(), sV)
        blk.vector(_v)
        return sV.v

    def em_s_xqp(g, a1_v):
        def _s(e):
            wge(e, sS)
            wge(e, sV, a1_v)
            i = e.activation(gat[:, 0:2 * D], xn[:, 0:2 * D], ACT.Sign)
            inc(i, sS)
        blk.scalar(_s)
        return sS.v

    def em_v_xq(g, ti, sgn_s, free_t, lnc_s):
        def _v(e):
            wge(e, sS, lnc_s)
            if free_t is not None:
                wge(e, sT, free_t)
            emit_quant(e, xn[:, 1024 * ti:1024 * (ti + 1)], D,
                       hq[:, 1024 * ti:1024 * (ti + 1)],
                       qsA[:], qsB[:], hsl[:],
                       bmF[:, 0:64], r6F[:, 0:64], sdF[:, 0:64],
                       sgn_ap=gat[:, D * ti:D * (ti + 1)], az_mark=sgn_s,
                       fuse=True, az_sep=True)
            inc(e.drain(), sV)
        blk.vector(_v)
        return sV.v

    def em_t_xqtr(g, v_mark, ps_free_s):
        def _t(e):
            wge(e, sV, v_mark)
            wge(e, sS, ps_free_s)
            for ti in range(2):
                dstp = PS_Abf[:, 1024 * ti:1024 * (ti + 1)]
                for k in range(8):
                    i = e.transpose(dstp[:, 128 * k:128 * (k + 1)],
                                    hq[:, 1024 * ti + 128 * k:
                                       1024 * ti + 128 * (k + 1)], identb[:])
            inc(i, sT)
        blk.tensor(_t)
        return sT.v

    def em_s_xqe(g, t_mark):
        def _s(e):
            wge(e, sT, t_mark)
            i = e.activation(xqT[:, 0:2 * D], PS_Abf[:, 0:2048], ACT.Copy)
            inc(i, sS)
            xqT_mark[0] = sS.v
        blk.scalar(_s)
        return sS.v

    def em_t_gu(g, s, usv, vb_h1, expect):
        w = ISLW[s]
        gj, uj = 2 * s, 2 * s + 1

        def _t(e):
            if s == 0:
                wge(e, sS, xqT_mark[0])
                wge(e, sV, fin_mark[0])
            else:
                e.wait_ge(sV.h, vb_h1[s - 1])
            e.wait_ge(sFg[gj % 3].h, usv[gj])
            e.wait_ge(sFg[uj % 3].h, usv[uj])
            gsl_ap = gusl[:, 4096 * (gj % 3):4096 * (gj % 3) + 8 * w]
            usl_ap = gusl[:, 4096 * (uj % 3):4096 * (uj % 3) + 8 * w]
            for k in range(8):
                for (ti, po) in ((0, 0), (1, 1024)):
                    xt = xqT[:, D * ti + 128 * k:D * ti + 128 * (k + 1)]
                    e.matmul(PS_B[:, po:po + w], xt,
                             gsl_ap[:, w * k:w * (k + 1)],
                             start=(k == 0), stop=(k == 7))
                    i = e.matmul(PS_B[:, po + 512:po + 512 + w], xt,
                                 usl_ap[:, w * k:w * (k + 1)],
                                 start=(k == 0), stop=(k == 7))
            inc(i, sT)
            assert sT.v == expect, (sT.v, expect)
        blk.tensor(_t)
        return sT.v

    def em_sy_gu(g, usv, gu_plan, j0):
        def _sy(e):
            for j in range(j0, 12):
                mat, s = j % 2, j // 2
                w = ISLW[s]
                scr = sc_u if mat else sc_g
                slot = gusl[:, 4096 * (j % 3):4096 * (j % 3) + 8 * w]
                if j >= 3:
                    wge(e, sT, gu_plan[(j - 3) // 2])
                src_ = scr[:].rearrange("p (k c) -> p k c", c=FI)[:, :,
                                                                 512 * s:512 * s + w]
                inc(e.dma_start(out=slot.rearrange("p (k c) -> p k c", c=w),
                                in_=src_), sFg[j % 3], 16)
                assert sFg[j % 3].v == usv[j]
        blk.sync(_sy)

    pre_usv = {}

    def em_sy_gu_pre(nxt, gu_plan_cur):
        def _sy(e):
            for j in range(3):
                mat, s = j % 2, j // 2
                w = ISLW[s]
                scr = sc_u if mat else sc_g
                slot = gusl[:, 4096 * j:4096 * j + 8 * w]
                wge(e, sT, gu_plan_cur[(j + 9) // 2])
                src_ = scr[:].rearrange("p (k c) -> p k c", c=FI)[:, :,
                                                                 512 * s:512 * s + w]
                inc(e.dma_start(out=slot.rearrange("p (k c) -> p k c", c=w),
                                in_=src_), sFg[j], 16)
                pre_usv[j] = sFg[j].v
        blk.sync(_sy)

    def em_s_sig(g, s, gu_t):
        w = ISLW[s]

        def _s(e):
            wge(e, sS)
            wge(e, sT, gu_t)
            ps3 = PS_B[:].rearrange("p (t c) -> p t c", c=1024)[:, :, 0:w]
            sl3 = silu_t[:, 0:2 * w].rearrange("p (t c) -> p t c", c=w)
            i = e.activation(sl3, ps3, ACT.Silu)
            inc(i, sS)
        blk.scalar(_s)
        return sS.v

    def em_v_h1(g, s, sig_s, expect):
        w = ISLW[s]

        def _v(e):
            wge(e, sS, sig_s)
            sl3 = silu_t[:, 0:2 * w].rearrange("p (t c) -> p t c", c=w)
            u3 = PS_B[:].rearrange("p (t c) -> p t c", c=1024)[:, :, 512:512 + w]
            h3 = hsl[:, 0:2 * w].rearrange("p (t c) -> p t c", c=w)
            e.tensor_tensor(h3, sl3, u3, ALU.mult)
            inc(e.drain(), sV)
            assert sV.v == expect, (sV.v, expect)
        blk.vector(_v)
        return sV.v

    def em_s_sgn(g, s, h1_v):
        w = ISLW[s]

        def _s(e):
            wge(e, sS)
            wge(e, sV, h1_v)
            i = e.activation(sgnb[:, 0:2 * w], hsl[:, 0:2 * w], ACT.Sign)
            inc(i, sS)
        blk.scalar(_s)
        return sS.v

    def em_v_h2(g, s, sgn_s, expect):
        w = ISLW[s]
        off = 512 * s

        def _v(e):
            nb2 = 2 * w // 16
            hq3 = hq[:].rearrange("p (t c) -> p t c", c=FI)[:, :, off:off + w]
            emit_quant(e, hsl[:, 0:2 * w], 2 * w, hq3,
                       qsA[:, 0:2 * w], qsB[:, 0:2 * w], hqTf[:, 0:2 * w],
                       bmF[:, 0:nb2], r6F[:, 0:nb2], sdF[:, 0:nb2],
                       sgn_ap=sgnb[:, 0:2 * w], az_mark=sgn_s, fuse=True,
                       az_sep=True)
            inc(e.drain(), sV)
            assert sV.v == expect, (sV.v, expect)
        blk.vector(_v)
        return sV.v

    HTR_WAVES = [(ti, 128 * 8 * wv, min(128 * 8 * (wv + 1), FI))
                 for ti in range(2) for wv in range(3)]

    def em_t_htrw(g, q5_v, ebase):
        htr_marks = []
        for widx, (ti, lo, hi) in enumerate(HTR_WAVES):
            def _t(e, ti=ti, lo=lo, hi=hi, widx=widx):
                e.wait_ge(sS.h, ebase + max(0, widx - 1))
                if widx == 0:
                    wge(e, sV, q5_v)
                dstp = PS_Bbf[:, 0:1024] if widx % 2 == 0 else PS_Bbf[:, 1024:2048]
                nb = (hi - lo) // 128
                for b in range(nb):
                    i = e.transpose(dstp[:, 128 * b:128 * (b + 1)],
                                    hq[:, FI * ti + lo + 128 * b:
                                       FI * ti + lo + 128 * (b + 1)], identb[:])
                inc(i, sT)
                htr_marks.append(sT.v)
            blk.tensor(_t)
        return htr_marks

    def em_s_htr(g, htr_marks, ebase):
        def _s(e):
            for widx, (ti, lo, hi) in enumerate(HTR_WAVES):
                wge(e, sT, htr_marks[widx])
                srcp = PS_Bbf[:, 0:1024] if widx % 2 == 0 else PS_Bbf[:, 1024:2048]
                i = e.activation(hqT[:, FI * ti + lo:FI * ti + hi],
                                 srcp[:, 0:hi - lo], ACT.Copy)
                inc(i, sS)
                assert sS.v == ebase + widx + 1, (sS.v, ebase, widx)
        blk.scalar(_s)
        return sS.v

    def em_t_dn(g, s_mark, v_mark):
        dpair_freed = {}
        dpair_sem_val = {}
        for kp in range(11):
            dpair_sem_val[kp] = sFd[kp % 2].v + 16 * (kp // 2 + 1)

        def _t(e):
            wge(e, sS, s_mark)
            wge(e, sV, v_mark)
            for kp in range(11):
                e.wait_ge(sFd[kp % 2].h, dpair_sem_val[kp])
                for half in range(2):
                    kk = 2 * kp + half
                    for ti in range(2):
                        st_ = hqT[:, FI * ti + 128 * kk:FI * ti + 128 * (kk + 1)]
                        for j in range(2):
                            i = e.matmul(
                                PS_B[:, 1024 * ti + 512 * j:1024 * ti + 512 * (j + 1)],
                                st_,
                                dsl[:, E2 * (kp % 2) + D * half + 512 * j:
                                    E2 * (kp % 2) + D * half + 512 * (j + 1)],
                                start=(kk == 0), stop=(kk == 21))
                inc(i, sT)
                dpair_freed[kp] = sT.v
        blk.tensor(_t)

        def _sy(e):
            for kp in range(11):
                if kp >= 2:
                    wge(e, sT, dpair_freed[kp - 2])
                inc(e.dma_start(out=dsl[:, E2 * (kp % 2):E2 * (kp % 2) + E2],
                                in_=sc_d[:, E2 * kp:E2 * (kp + 1)]),
                    sFd[kp % 2], 16)
                assert sFd[kp % 2].v == dpair_sem_val[kp]
        blk.sync(_sy)
        return sT.v

    def em_v_fin(g, dn_t):
        xsl = E2 * (g % 2)

        def _v(e):
            wge(e, sT, dn_t)
            for ti in range(2):
                a_ap = xbuf[:, xsl + 1024 * ti:xsl + 1024 * (ti + 1)]
                e.tensor_tensor(a_ap, a_ap, PS_B[:, 1024 * ti:1024 * (ti + 1)],
                                ALU.add)
                inc(e.drain(), sD)
            inc(e.drain(), sV)
            fin_mark[0] = sV.v
        blk.vector(_v)
        return sV.v

    def em_sy_out(g):
        xsl = E2 * (g % 2)

        def _sy(e):
            wge(e, sV, fin_mark[0])
            for ti in range(2):
                r = 128 * (2 * g + ti)
                inc(e.dma_start(out=out_d[r:r + 128, :],
                                in_=xbuf[:, xsl + 1024 * ti:xsl + 1024 * (ti + 1)]),
                    sO, 16)
            out_mark[0] = sO.v
        blk.sync(_sy)

    def emit_attn0():
        em_sy_head(0)
        em_v_ln(0)
        m = em_s_ln(0)
        ln2 = em_v_ln2(0, m)
        lnc0 = em_s_ln2a(0, ln2)
        m = em_t_tr(0, 0, lnc0, xqT_mark[0])
        m = em_s_tr(0, 0, m)
        m = em_t_psi(0, 0, m)
        psi1 = None
        m = em_s_psi(0, 0, m)
        m = em_t_gat(0, 0, m)
        m = em_s_gat(0, 0, m)
        tr1m = em_t_tr(0, 1, lnc0, m)
        m = em_s_tr(0, 1, tr1m)
        psi1 = em_t_psi(0, 1, m)
        em_sy_phi(0, psi1)
        m = em_s_psi(0, 1, psi1)
        m = em_t_gat(0, 1, m)
        sg1 = em_s_gat(0, 1, m)
        m = em_t_phi(0, 0, sg1)
        m = em_s_phi(0, 0, m)
        m = em_t_phi(0, 1, m)
        sphi1 = em_s_phi(0, 1, m)
        a0 = em_v_a(0, 0, sphi1, tr1m, scr2k)
        c0 = em_t_c(0, 0, a0, sphi1, scr2k)
        a1 = em_v_a(0, 1, sphi1, tr1m, silu_bf)
        pv0 = em_s_pv(0, 0, c0, scr2k)
        p0 = em_v_p(0, 0, pv0, scr2k)
        c1 = em_t_c(0, 1, p0, pv0, silu_bf)
        pv1 = em_s_pv(0, 1, c1, silu_bf)
        p1 = em_v_p(0, 1, pv1, silu_bf)
        em_v_pr(0)
        m = em_s_pr(0, p1)
        m = em_v_pr2(0, m)
        m = em_s_pr2(0, m)
        em_v_pr3(0, m)
        sgn_s = em_s_xqp(0, a1)
        em_v_xq(0, 0, sgn_s, None, lnc0)
        xqv = em_v_xq(0, 1, sgn_s, None, lnc0)
        m = em_t_xqtr(0, xqv, pv1)
        em_s_xqe(0, m)

    def emit_period(g):
        nxt = g + 1
        has_next = nxt < ng
        tbase = sT.v
        vbase = sV.v
        if has_next:
            gu_plan = {s: tbase + [1, 2, 5, 7, 9, 12][s] for s in range(6)}
            vb_h1 = {s: vbase + [1, 5, 7, 9, 11, 13][s] for s in range(6)}
            vb_q = {s: vbase + [2, 6, 8, 10, 12, 14][s] for s in range(6)}
            vb_a1 = vbase + 16
            vb_p1 = vbase + 20
        else:
            gu_plan = {s: tbase + s + 1 for s in range(6)}
            vb_h1 = {s: vbase + 2 * s + 1 for s in range(6)}
            vb_q = {s: vbase + 2 * s + 2 for s in range(6)}
        if g == 0:
            usv = {j: sFg[j % 3].v + 16 * (j // 3 + 1) for j in range(12)}
            j0 = 0
        else:
            usv = dict(pre_usv)
            for j in range(3, 12):
                usv[j] = sFg[j % 3].v + 16 * ((j - 3) // 3 + 1)
            j0 = 3

        if has_next:
            em_sy_head(nxt)
        em_sy_gu(g, usv, gu_plan, j0)
        if has_next:
            em_sy_phi(nxt, tbase + 10)

        gu0 = em_t_gu(g, 0, usv, vb_h1, gu_plan[0])
        sig0 = em_s_sig(g, 0, gu0)
        h1v = em_v_h1(g, 0, sig0, vb_h1[0])
        sgn0 = em_s_sgn(g, 0, h1v)
        if has_next:
            mq = em_s_ln(nxt)
        em_v_h2(g, 0, sgn0, vb_q[0])

        gu1 = em_t_gu(g, 1, usv, vb_h1, gu_plan[1])
        sig1 = em_s_sig(g, 1, gu1)
        if has_next:
            em_v_ln(nxt)
            ln2 = em_v_ln2(nxt, mq)
            lnc = em_s_ln2a(nxt, ln2)
            trm = em_t_tr(nxt, 0, lnc, xqT_mark[0])
            str0 = em_s_tr(nxt, 0, trm)
        h1v = em_v_h1(g, 1, sig1, vb_h1[1])
        sgn1 = em_s_sgn(g, 1, h1v)
        em_v_h2(g, 1, sgn1, vb_q[1])
        if has_next:
            tpsi0 = em_t_psi(nxt, 0, str0)

        gu2 = em_t_gu(g, 2, usv, vb_h1, gu_plan[2])
        sig2 = em_s_sig(g, 2, gu2)
        h1v = em_v_h1(g, 2, sig2, vb_h1[2])
        if has_next:
            spsi0 = em_s_psi(nxt, 0, tpsi0)
        sgn2 = em_s_sgn(g, 2, h1v)
        em_v_h2(g, 2, sgn2, vb_q[2])
        if has_next:
            tgat0 = em_t_gat(nxt, 0, spsi0)

        gu3 = em_t_gu(g, 3, usv, vb_h1, gu_plan[3])
        sig3 = em_s_sig(g, 3, gu3)
        h1v = em_v_h1(g, 3, sig3, vb_h1[3])
        if has_next:
            sgat0 = em_s_gat(nxt, 0, tgat0)
            trm1 = em_t_tr(nxt, 1, lnc, sgat0)
            str1 = em_s_tr(nxt, 1, trm1)
        sgn3 = em_s_sgn(g, 3, h1v)
        em_v_h2(g, 3, sgn3, vb_q[3])

        gu4 = em_t_gu(g, 4, usv, vb_h1, gu_plan[4])
        sig4 = em_s_sig(g, 4, gu4)
        h1v = em_v_h1(g, 4, sig4, vb_h1[4])
        if has_next:
            tpsi1 = em_t_psi(nxt, 1, str1)
            assert tpsi1 == tbase + 10, (tpsi1, tbase)
            spsi1 = em_s_psi(nxt, 1, tpsi1)
        sgn4 = em_s_sgn(g, 4, h1v)
        em_v_h2(g, 4, sgn4, vb_q[4])
        if has_next:
            tgat1 = em_t_gat(nxt, 1, spsi1)
            sgat1 = em_s_gat(nxt, 1, tgat1)

        gu5 = em_t_gu(g, 5, usv, vb_h1, gu_plan[5])
        sig5 = em_s_sig(g, 5, gu5)
        h1v = em_v_h1(g, 5, sig5, vb_h1[5])
        sgn5 = em_s_sgn(g, 5, h1v)
        q5 = em_v_h2(g, 5, sgn5, vb_q[5])
        if has_next:
            tphi0 = em_t_phi(nxt, 0, sgat1)
            sphi0 = em_s_phi(nxt, 0, tphi0)
            tphi1 = em_t_phi(nxt, 1, sphi0)
            sphi1 = em_s_phi(nxt, 1, tphi1)

        if has_next:
            em_sy_gu_pre(nxt, gu_plan)
        ebase = sS.v
        htr_marks = em_t_htrw(g, q5, ebase)
        shtr = em_s_htr(g, htr_marks, ebase)
        if has_next:
            sgn_s = em_s_xqp(nxt, vb_a1)
            a0 = em_v_a(nxt, 0, sgat1, trm1, scr2k)
            assert a0 == vbase + 15
            c0 = em_t_c(nxt, 0, a0, sphi1, scr2k)
            a1 = em_v_a(nxt, 1, sgat1, trm1, silu_bf)
            assert a1 == vb_a1
            em_v_xq(nxt, 0, sgn_s, htr_marks[-1], lnc)
            pv0 = em_s_pv(nxt, 0, c0, scr2k)
            dn_t = em_t_dn(g, shtr, vb_h1[5])
            p0 = em_v_p(nxt, 0, pv0, scr2k)
            c1 = em_t_c(nxt, 1, p0, pv0, silu_bf)
            xqv = em_v_xq(nxt, 1, sgn_s, None, lnc)
            pv1 = em_s_pv(nxt, 1, c1, silu_bf)
            p1 = em_v_p(nxt, 1, pv1, silu_bf)
            assert p1 == vb_p1
            em_v_pr(nxt)
            m = em_s_pr(nxt, vb_p1)
            m2 = em_v_pr2(nxt, m)
            m = em_s_pr2(nxt, m2)
            fv = em_v_fin(g, dn_t)
            assert fv == vbase + 23
            em_sy_out(g)
            em_v_pr3(nxt, m)
            mt = em_t_xqtr(nxt, vb_p1, pv1)
            em_s_xqe(nxt, mt)
        else:
            dn_t = em_t_dn(g, shtr, vb_h1[5])
            fv = em_v_fin(g, dn_t)
            assert fv == vbase + 13
            em_sy_out(g)

    with nc.Block() as blk:
        emit_attn0()
        for g in range(ng):
            emit_period(g)

        def _sy_end(e):
            wge(e, sO)
        blk.sync(_sy_end)

    ctx.close()
    return nc


def _shard_inputs(inputs, tpc=4096):
    x = np.asarray(inputs["L0_attn_x"])
    B = x.shape[0]
    per = tpc // x.shape[1]
    wpsiT = np.ascontiguousarray(np.asarray(inputs["L0_attn_W_psi"]).T)
    wphiT = np.ascontiguousarray(np.asarray(inputs["L0_attn_W_phi"]).T)
    wgateT = np.ascontiguousarray(np.asarray(inputs["L0_attn_W_gate"]).T)
    shared = {
        "wpsiT": wpsiT, "wphiT": wphiT, "wgateT": wgateT,
        "bpsi": np.ascontiguousarray(np.asarray(inputs["L0_attn_b_psi"]).reshape(1, -1)),
        "bphi": np.ascontiguousarray(np.asarray(inputs["L0_attn_b_phi"]).reshape(1, -1)),
        "bgate": np.ascontiguousarray(np.asarray(inputs["L0_attn_b_gate"]).reshape(1, -1)),
        "omega": np.ascontiguousarray(np.asarray(inputs["L0_attn_omega"]).reshape(1, -1)),
        "lgam": np.ascontiguousarray(np.asarray(inputs["L0_attn_log_gamma"]).reshape(1, -1)),
        "dt": np.ascontiguousarray(np.asarray(inputs["L0_attn_dt"]).reshape(1, 1)),
        "wg": np.ascontiguousarray(np.asarray(inputs["L0_ffn_w_gate"])),
        "wu": np.ascontiguousarray(np.asarray(inputs["L0_ffn_w_up"])),
        "wd": np.ascontiguousarray(np.asarray(inputs["L0_ffn_w_down"])),
    }
    in_maps = []
    ncores = B // per
    for c in range(ncores):
        m = dict(shared)
        m["x"] = np.ascontiguousarray(
            x[c * per:(c + 1) * per].reshape(tpc, D))
        in_maps.append(m)
    return in_maps, ncores


def kernel(**inputs):
    x = np.asarray(inputs["L0_attn_x"])
    B, L, _ = x.shape
    tpc = (B * L) // 8
    in_maps, ncores = _shard_inputs(inputs, tpc)
    nc = build_program(tpc=tpc, seq_tiles=L // 128)
    res = run_bass_kernel_spmd(nc, in_maps, core_ids=list(range(ncores)))
    outs = [np.asarray(r["out"]).reshape(-1, L, D) for r in res.results]
    return np.concatenate(outs, axis=0).astype(np.float32)


# version 30
# speedup vs baseline: 1.0112x; 1.0112x over previous
import contextlib
import numpy as np
import ml_dtypes
import concourse.bass as bass
import concourse.mybir as mybir
from concourse.bass_utils import run_bass_kernel_spmd

F32 = mybir.dt.float32
BF16 = mybir.dt.bfloat16
AX = mybir.AxisListType.X
ALU = mybir.AluOpType
ACT = mybir.ActivationFunctionType

D = 1024
E2 = 2048
FI = 2816
def F(v):
    return float(np.float32(v))


AMAX = F(0.65625)
C1_2PI = F(6.2831855)
C2_2PI = F(2 * np.pi - 6.283185482025146484375)
PI = F(np.pi)
PI3 = F(3 * np.pi)
HALF_PI = F(np.pi / 2)

_QOPS = {}


def _ensure_qops():
    if _QOPS:
        return _QOPS
    from concourse.dve_spec import (Spec, Src0, Src1, C0, C1, C2, Zero, One,
                                    lower, AluOp, select, Bin)
    from concourse.dve_op import DveOp
    import concourse.dve_ops as dmod

    replaceable = ["BITWISE_XOR_TT_BV", "SHIFT_RIGHT_TT_BV"]

    def mk(name, spec, ref):
        for o in dmod.OPS:
            if o.name == name:
                return o
        shas = {}
        for ver in ("v3", "v4"):
            try:
                shas[ver] = lower(spec, ver=ver, perf_en=False, name=name).sha(ver)
            except Exception:
                pass
        op = DveOp(name, spec, subdim=False, uops_sha=shas, reference=ref)
        if dmod._CUSTOM_DVE_ROW_BASE + len(dmod.OPS) <= 31:
            dmod.OPS.append(op)
            row = dmod._CUSTOM_DVE_ROW_BASE + len(dmod.OPS) - 1
        else:
            victim = replaceable.pop()
            idx = next(i for i, o in enumerate(dmod.OPS) if o.name == victim)
            dmod.OPS[idx] = op
            row = dmod._CUSTOM_DVE_ROW_BASE + idx
            dmod._SUB_OPCODE_FOR_NAME.pop(victim, None)
        dmod._SUB_OPCODE_FOR_NAME[name] = row
        dmod.CUSTOM_DVE_SPECS[name] = op.spec
        dmod.CUSTOM_DVE_REFS[name] = op.ref
        return op

    prod = Src0 * Src1
    _QOPS["u"] = mk("OPQ_U_ANT", Spec(body=Bin(AluOp.ABSOLUTE_VALUE, prod, prod)),
                    lambda in0, in1, s0, s1, imm2: np.abs(in0 * in1))
    u = Src0
    um1 = u - One
    low = ((u > C0) + (u > C1)) + ((um1 > C0) + (um1 > C1))

    def _ref_low(in0, in1, s0, s1, imm2):
        dt_ = in0.dtype
        um = in0 - dt_.type(1.0)
        return ((in0 > s0).astype(dt_) + (in0 > s1).astype(dt_)
                + (um > s0).astype(dt_) + (um > s1).astype(dt_))
    _QOPS["low"] = mk("OPQ_LOW_ANT", Spec(body=low), _ref_low)
    ch1 = u > C0
    ch2 = u > C1
    ch3 = u > C2
    chc = (ch1 + ch2) + (ch3 + ch3)
    high = Src1 + (chc + chc)

    def _ref_high(in0, in1, s0, s1, imm2):
        dt_ = in0.dtype
        c = ((in0 > s0).astype(dt_) + (in0 > s1).astype(dt_)
             + dt_.type(2.0) * (in0 > imm2).astype(dt_))
        return in1 + dt_.type(2.0) * c
    _QOPS["high"] = mk("OPQ_HIGH_ANT", Spec(body=high), _ref_high)
    sgn = select(Src0 < Zero, Zero - Src1, Src1)
    _QOPS["sign"] = mk("OPQ_SIGN_ANT", Spec(body=sgn),
                       lambda in0, in1, s0, s1, imm2: np.where(in0 < 0, -in1, in1))
    return _QOPS


def build_program(tpc=4096, seq_tiles=16):
    nt = tpc // 128
    ng = nt // 2
    assert nt % 2 == 0 and seq_tiles % 2 == 0

    nc = bass.Bass("TRN2", target_bir_lowering=False)

    x_d = nc.dram_tensor("x", [tpc, D], F32, kind="ExternalInput").ap()
    wpsiT_d = nc.dram_tensor("wpsiT", [D, E2], F32, kind="ExternalInput").ap()
    wphiT_d = nc.dram_tensor("wphiT", [D, E2], F32, kind="ExternalInput").ap()
    wgateT_d = nc.dram_tensor("wgateT", [D, D], F32, kind="ExternalInput").ap()
    bpsi_d = nc.dram_tensor("bpsi", [1, E2], F32, kind="ExternalInput").ap()
    bphi_d = nc.dram_tensor("bphi", [1, E2], F32, kind="ExternalInput").ap()
    bgate_d = nc.dram_tensor("bgate", [1, D], F32, kind="ExternalInput").ap()
    omega_d = nc.dram_tensor("omega", [1, D], F32, kind="ExternalInput").ap()
    lgam_d = nc.dram_tensor("lgam", [1, D], F32, kind="ExternalInput").ap()
    dt_d = nc.dram_tensor("dt", [1, 1], F32, kind="ExternalInput").ap()
    wg_d = nc.dram_tensor("wg", [FI, D], F32, kind="ExternalInput").ap()
    wu_d = nc.dram_tensor("wu", [FI, D], F32, kind="ExternalInput").ap()
    wd_d = nc.dram_tensor("wd", [D, FI], F32, kind="ExternalInput").ap()
    out_d = nc.dram_tensor("out", [tpc, D], F32, kind="ExternalOutput").ap()
    sc_g = nc.dram_tensor("sc_g", [128, 8 * FI], BF16, kind="Internal").ap()
    sc_u = nc.dram_tensor("sc_u", [128, 8 * FI], BF16, kind="Internal").ap()
    sc_d = nc.dram_tensor("sc_d", [128, 22 * D], BF16, kind="Internal").ap()
    sc_w = nc.dram_tensor("sc_w", [128, 16 * E2], BF16, kind="Internal").ap()

    identN = np.eye(128, dtype=ml_dtypes.bfloat16)
    ltN = np.triu(np.ones((128, 128), dtype=np.float32)).astype(ml_dtypes.bfloat16)
    e127N = np.zeros((128, 128), dtype=np.float32)
    e127N[127, :] = 1.0
    e127N = e127N.astype(ml_dtypes.bfloat16)
    ident_d = nc.inline_tensor(identN, "identc").ap()
    lt_d = nc.inline_tensor(ltN, "ltc").ap()
    e127_d = nc.inline_tensor(e127N, "e127c").ap()
    ones1b_d = nc.inline_tensor(np.ones((1, 128), dtype=ml_dtypes.bfloat16), "o1b").ap()
    ones1f_d = nc.inline_tensor(np.ones((1, 128), dtype=np.float32), "o1f").ap()
    tp1_d = nc.inline_tensor(np.arange(1, 129, dtype=np.float32).reshape(128, 1),
                             "tp1").ap()
    tn1_d = nc.inline_tensor(-np.arange(1, 129, dtype=np.float32).reshape(128, 1),
                             "tn1").ap()

    ctx = contextlib.ExitStack()

    def sb(name, shape, dt):
        return ctx.enter_context(nc.sbuf_tensor(name, shape, dt))

    class Sem:
        def __init__(self, name):
            self.h = ctx.enter_context(nc.semaphore(name))
            self.v = 0

    sT, sV, sS = Sem("sT"), Sem("sV"), Sem("sS")
    sX, sF, sO, sG = Sem("sX"), Sem("sF"), Sem("sO"), Sem("sG")
    sW = Sem("sW")
    sC = Sem("sC")
    sFg = [Sem("sFg0"), Sem("sFg1"), Sem("sFg2")]
    sFd = [Sem("sFd0"), Sem("sFd1")]
    sD = Sem("sD")

    def inc(instr, s, n=1):
        instr.then_inc(s.h, n)
        s.v += n

    def wge(eng, s, v=None):
        eng.wait_ge(s.h, s.v if v is None else v)

    wgat_b = sb("wgat_b", [128, 8 * D], BF16)
    bpsi_b = sb("bpsi_b", [1, E2], BF16)
    bphi_b = sb("bphi_b", [1, E2], BF16)
    bgat_b = sb("bgat_b", [1, D], BF16)
    identb = sb("identb", [128, 128], BF16)
    ltb = sb("ltb", [128, 128], BF16)
    e127b = sb("e127b", [128, 128], BF16)
    ones1b = sb("ones1b", [1, 128], BF16)
    ones1f = sb("ones1f", [1, 128], F32)
    tp1c = sb("tp1c", [128, 1], F32)
    tn1c = sb("tn1c", [128, 1], F32)
    wm_re = sb("wm_re", [128, D], BF16)
    wm_im = sb("wm_im", [128, D], BF16)
    wp_re = sb("wp_re", [128, D], BF16)
    wp_im = sb("wp_im", [128, D], BF16)
    stat = sb("stat", [128, 16], F32)
    wbuf = sb("wbuf", [128, 8 * E2], BF16)

    PS_A = ctx.enter_context(nc.psum_tensor("PS_A", [128, 2048], F32))
    PS_B = ctx.enter_context(nc.psum_tensor("PS_B", [128, 2048], F32))
    PS_Tv = PS_B[:].bitcast(BF16)

    with nc.Block() as blk:
        def _sy(e):
            for dst, src in [(identb, ident_d), (ltb, lt_d), (e127b, e127_d),
                             (ones1b, ones1b_d), (ones1f, ones1f_d),
                             (tp1c, tp1_d), (tn1c, tn1_d)]:
                inc(e.dma_start(out=dst[:], in_=src), sX, 16)
        blk.sync(_sy)



    with contextlib.ExitStack() as tctx:
        tsb = lambda n, s, d: tctx.enter_context(nc.sbuf_tensor(n, s, d))
        tbl_sc = tsb("tbl_sc", [128, 2 * D], F32)
        tbl_e = tsb("tbl_e", [128, D], F32)
        tbl_p = tsb("tbl_p", [128, D], F32)
        tbl_m = tsb("tbl_m", [128, D], F32)
        tbl_s = tsb("tbl_s", [128, D], F32)
        tbl_c = tsb("tbl_c", [128, D], F32)
        tbl_k = tsb("tbl_k", [128, D], F32)
        rowa = tsb("rowa", [1, D], F32)
        roww = tsb("roww", [1, D], F32)
        dtab = tsb("dtab", [1, 1], F32)
        lgam_r = tsb("lgam_r", [1, D], F32)
        omega_r = tsb("omega_r", [1, D], F32)
        dt_r = tsb("dt_r", [1, 1], F32)

        with nc.Block() as blk:
            def _sy0(e):
                for dst, src_ in [(lgam_r, lgam_d), (omega_r, omega_d),
                                  (dt_r, dt_d)]:
                    inc(e.dma_start(out=dst[:], in_=src_), sX, 16)
            blk.sync(_sy0)

            def _sc(e):
                wge(e, sX)
                e.activation(dtab[:], dt_r[:], ACT.Abs)
                i = e.activation(rowa[:], lgam_r[:], ACT.Exp)
                inc(i, sS)
            blk.scalar(_sc)

            def _v(e):
                wge(e, sS)
                e.tensor_scalar(rowa[:], rowa[:], dtab[0:1, 0:1], AMAX,
                                ALU.mult, ALU.min)
                e.tensor_scalar(roww[:], omega_r[:], dtab[0:1, 0:1], None, ALU.mult)
                inc(e.drain(), sV)
            blk.vector(_v)

            def _t(e):
                wge(e, sV)
                for j in range(2):
                    e.matmul(PS_A[:, 512 * j:512 * (j + 1)], ones1f[:],
                             rowa[0:1, 512 * j:512 * (j + 1)], start=True, stop=True)
                for j in range(2):
                    i = e.matmul(PS_A[:, 1024 + 512 * j:1536 + 512 * j], ones1f[:],
                                 roww[0:1, 512 * j:512 * (j + 1)], start=True,
                                 stop=True)
                inc(i, sT)
            blk.tensor(_t)

            def _v2(e):
                wge(e, sT)
                e.tensor_copy(tbl_sc[:], PS_A[:])
                inc(e.drain(), sV)
            blk.vector(_v2)

        a_bc = tbl_sc[:, 0:D]
        w_bc = tbl_sc[:, D:2 * D]

        def emit_table(sgn, ecol, o_re, o_im):
            with nc.Block() as blk:
                def _v(e):
                    wge(e, sV)
                    e.tensor_scalar(tbl_e[:], a_bc, ecol[:, 0:1], None, ALU.mult)
                    e.tensor_scalar(tbl_p[:], w_bc, tp1c[:, 0:1], None, ALU.mult)
                    e.drain()
                    e.tensor_scalar(tbl_k[:], tbl_p[:], PI, None, ALU.is_gt)
                    e.drain()
                    e.scalar_tensor_tensor(tbl_k[:], tbl_p[:], PI3, tbl_k[:],
                                           ALU.is_gt, ALU.add)
                    e.drain()
                    e.scalar_tensor_tensor(tbl_p[:], tbl_k[:], F(-C1_2PI), tbl_p[:],
                                           ALU.mult, ALU.add)
                    e.drain()
                    e.scalar_tensor_tensor(tbl_p[:], tbl_k[:], F(-C2_2PI), tbl_p[:],
                                           ALU.mult, ALU.add)
                    inc(e.drain(), sV)
                blk.vector(_v)

                def _s(e):
                    wge(e, sV)
                    e.activation(tbl_m[:], tbl_e[:], ACT.Exp)
                    i = e.activation(tbl_s[:], tbl_p[:], ACT.Sin)
                    inc(i, sS)
                blk.scalar(_s)

                def _v2(e):
                    wge(e, sS)
                    e.tensor_scalar(tbl_c[:], tbl_p[:], F(-1.0), HALF_PI,
                                    ALU.mult, ALU.add)
                    e.drain()
                    e.tensor_scalar(tbl_k[:], tbl_c[:], PI, None, ALU.is_gt)
                    e.drain()
                    e.scalar_tensor_tensor(tbl_c[:], tbl_k[:], F(-C1_2PI), tbl_c[:],
                                           ALU.mult, ALU.add)
                    e.drain()
                    e.scalar_tensor_tensor(tbl_c[:], tbl_k[:], F(-C2_2PI), tbl_c[:],
                                           ALU.mult, ALU.add)
                    inc(e.drain(), sV)
                blk.vector(_v2)

                def _s2(e):
                    wge(e, sV)
                    i = e.activation(tbl_c[:], tbl_c[:], ACT.Sin)
                    inc(i, sS)
                blk.scalar(_s2)

                def _v3(e):
                    wge(e, sS)
                    e.tensor_tensor(o_re[:], tbl_m[:], tbl_c[:], ALU.mult)
                    e.scalar_tensor_tensor(o_im[:], tbl_m[:], F(sgn), tbl_s[:],
                                           ALU.mult, ALU.mult)
                    inc(e.drain(), sV)
                blk.vector(_v3)

        emit_table(-1.0, tp1c, wm_re, wm_im)
        emit_table(+1.0, tn1c, wp_re, wp_im)

    def bc16(ap2, nblk):
        return ap2.unsqueeze(2).broadcast_to([128, nblk, 16])

    def emit_quant(ve, z_ap, n, out_bf, QA, QB, AZ, bm, r6, sdq,
                   sgn_ap=None, az_mark=None, fuse=False, az_sem=None,
                   az_sep=False, early_free=False):
        nb = n // 16
        zv = z_ap.rearrange("p (b k) -> p b k", k=16)
        az3 = AZ.rearrange("p (b k) -> p b k", k=16)

        def E(i):
            inc(i, sD)
            return sD.v

        def W(m):
            ve.wait_ge(sD.h, m)

        if az_sem is None:
            az_sem = sS
        if sgn_ap is not None and not fuse:
            ve.wait_ge(az_sem.h, az_mark)
        m_red = E(ve.reduce_max(bm, zv, axis=AX, apply_absolute_value=True))
        if fuse:
            W(m_red)
            if not az_sep:
                ve.wait_ge(az_sem.h, az_mark)
            i_az = ve.scalar_tensor_tensor(AZ, z_ap, F(-1.0), z_ap,
                                           ALU.mult, ALU.max)
            if early_free:
                inc(i_az, sV)
                m_az = ("V", sV.v)
            else:
                inc(i_az, sD)
                m_az = ("D", sD.v)
        if sgn_ap is None:
            m_az = E(ve.scalar_tensor_tensor(AZ, z_ap, F(-1.0), z_ap, ALU.mult,
                                             ALU.max))
        W(m_red)
        E(ve.reciprocal(r6, bm))
        W(sD.v)
        m_r6 = E(ve.tensor_scalar(r6, r6, F(6.0), F(1e12), ALU.mult, ALU.min))
        W(m_r6)
        m_sd = E(ve.reciprocal(sdq, r6))
        W(m_sd)
        m_sdh = E(ve.tensor_scalar(sdq, sdq, F(0.5), None, ALU.mult))
        if sgn_ap is None:
            W(m_az)
        if fuse:
            if m_az[0] == "V":
                ve.wait_ge(sV.h, m_az[1])
            else:
                W(m_az[1])
        m_u = E(ve.tensor_tensor(az3, az3, bc16(r6, nb), ALU.mult))
        W(m_u)
        a = E(ve.tensor_scalar(QA, AZ, F(0.25), None, ALU.is_gt))
        c = E(ve.tensor_scalar(QB, AZ, F(2.5), None, ALU.is_gt))
        for thr_a, thr_c in ((0.75, 3.5), (1.25, 5.0), (1.75, 5.0)):
            W(a)
            a = E(ve.scalar_tensor_tensor(QA, AZ, F(thr_a), QA, ALU.is_gt,
                                          ALU.add))
            W(c)
            c = E(ve.scalar_tensor_tensor(QB, AZ, F(thr_c), QB, ALU.is_gt,
                                          ALU.add))
        W(a)
        W(c)
        q2 = E(ve.scalar_tensor_tensor(QB, QB, F(2.0), QA, ALU.mult, ALU.add))
        W(q2)
        if sgn_ap is None:
            s1 = E(ve.scalar_tensor_tensor(QA, z_ap, F(0.0), QB, ALU.is_gt,
                                           ALU.mult))
            s2 = E(ve.scalar_tensor_tensor(AZ, z_ap, F(0.0), QB, ALU.is_lt,
                                           ALU.mult))
            W(s1)
            W(s2)
            qs = E(ve.tensor_tensor(QA, QA, AZ, ALU.subtract))
            W(qs)
            W(m_sdh)
            E(ve.tensor_tensor(out_bf, QA.rearrange("p (b k) -> p b k", k=16),
                               bc16(sdq, nb), ALU.mult))
        else:
            W(m_sdh)
            t = E(ve.tensor_tensor(out_bf,
                                   QB.rearrange("p (b k) -> p b k", k=16),
                                   bc16(sdq, nb), ALU.mult))
            W(t)
            if fuse and az_sep:
                ve.wait_ge(az_sem.h, az_mark)
            E(ve.tensor_tensor(out_bf, out_bf, sgn_ap, ALU.mult))

    psiw_mark = [0]
    with contextlib.ExitStack() as qctx:
        qsb = lambda n, s, d: qctx.enter_context(nc.sbuf_tensor(n, s, d))
        slab_in = qsb("slab_in", [128, 3 * FI], F32)
        sAZ = qsb("sAZ", [128, FI], F32)
        sQA = qsb("sQA", [128, FI], F32)
        sQB = qsb("sQB", [128, FI], F32)
        sbm = qsb("sbm", [128, FI // 16], F32)
        sr6 = qsb("sr6", [128, FI // 16], F32)
        ssd = qsb("ssd", [128, FI // 16], F32)
        qout = qsb("qout", [128, 3 * FI], BF16)
        bigT = qsb("bigT", [128, 8 * FI], BF16)

        sSg = qsb("sSg", [128, 2 * FI], BF16)
        a_items = ([("psi", c, E2) for c in range(8)]
                   + [("phi", c, E2) for c in range(8)]
                   + [("gate", c, D) for c in range(8)])
        q_items = ([("wg", c, D) for c in range(22)]
                   + [("wu", c, D) for c in range(22)]
                   + [("wd", c, FI) for c in range(8)])
        items = []
        for k in range(len(q_items)):
            if k < len(a_items):
                items.append(a_items[k])
            items.append(q_items[k])
        items.append(("bias", 0, 0))
        SRCD = {"psi": wpsiT_d, "phi": wphiT_d, "gate": wgateT_d,
                "wg": wg_d, "wu": wu_d, "wd": wd_d}
        NCH = {"wg": 22, "wu": 22, "wd": 8}
        SCRD = {"wg": (sc_g, 8 * FI), "wu": (sc_u, 8 * FI), "wd": (sc_d, 22 * D)}
        HBASE = {"psi": 0, "phi": 8 * E2}
        QK = ("wg", "wu", "wd")
        vm = {}
        slab_free = {}
        slab_mark = {}
        sign_mark = {}
        evac_mark = {}
        qslot = {0: None, 1: None, 2: None}
        ev_last = [0]

        with nc.Block() as blk:
            def emit_dma(i):
                kind, c, w = items[i]

                def _sy(e, i=i, kind=kind, c=c, w=w):
                    if i >= 3:
                        wge(e, sV, slab_free[i - 3])
                        if (i - 3) in sign_mark:
                            wge(e, sG, sign_mark[i - 3])
                    for jb in (i - 1, i - 2):
                        if jb >= 0 and items[jb][0] == "bias":
                            wge(e, sV, vm[jb])
                    if kind == "bias":
                        wge(e, sV, vm[i - 1])
                        wge(e, sV, vm[i - 2])
                        inc(e.dma_start(out=slab_in[0:1, 0:E2], in_=bpsi_d),
                            sX, 16)
                        inc(e.dma_start(out=slab_in[0:1, E2:2 * E2], in_=bphi_d),
                            sX, 16)
                        inc(e.dma_start(out=slab_in[0:1, 2 * E2:2 * E2 + D],
                                        in_=bgate_d), sX, 16)
                        slab_mark[i] = sX.v
                    else:
                        inc(e.dma_start(
                            out=slab_in[:, (i % 3) * FI:(i % 3) * FI + w],
                            in_=SRCD[kind][128 * c:128 * (c + 1), :]),
                            sFg[i % 3], 16)
                        slab_mark[i] = sFg[i % 3].v
                blk.sync(_sy)

            def emit_sign(i):
                kind, c, w = items[i]

                def _s(e, i=i, w=w):
                    wge(e, sG)
                    if i >= 2:
                        wge(e, sV, vm[i - 2])
                    e.wait_ge(sFg[i % 3].h, slab_mark[i])
                    i_ = e.activation(sSg[:, (i % 2) * FI:(i % 2) * FI + w],
                                      slab_in[:, (i % 3) * FI:(i % 3) * FI + w],
                                      ACT.Sign)
                    inc(i_, sG)
                    sign_mark[i] = sG.v
                blk.scalar(_s)

            def wait_qslot(e, spec):
                if spec is None:
                    return
                kind_, val = spec
                if kind_ == "d":
                    e.wait_ge(val[0].h, val[1])
                else:
                    wge(e, sT, val)

            def emit_consume(i):
                kind, c, w = items[i]
                if kind == "bias":
                    def _vb(e, i=i):
                        e.wait_ge(sX.h, slab_mark[i])
                        e.tensor_copy(bpsi_b[:], slab_in[0:1, 0:E2])
                        e.tensor_copy(bphi_b[:], slab_in[0:1, E2:2 * E2])
                        e.tensor_copy(bgat_b[:], slab_in[0:1, 2 * E2:2 * E2 + D])
                        inc(e.drain(), sV)
                        vm[i] = sV.v
                        slab_free[i] = sV.v
                    blk.vector(_vb)
                    return
                if kind in ("psi", "phi", "gate"):
                    spec = qslot[i % 3] if kind != "gate" else None

                    def _v(e, i=i, kind=kind, c=c, w=w, spec=spec):
                        e.wait_ge(sFg[i % 3].h, slab_mark[i])
                        wait_qslot(e, spec)
                        dst = (wgat_b[:, D * c:D * (c + 1)] if kind == "gate"
                               else qout[:, (i % 3) * FI:(i % 3) * FI + w])
                        e.tensor_copy(dst,
                                      slab_in[:, (i % 3) * FI:(i % 3) * FI + w])
                        inc(e.drain(), sV)
                        vm[i] = sV.v
                        slab_free[i] = sV.v
                    blk.vector(_v)
                    if kind != "gate":
                        osem = (sFd[0], sFd[1], sC)[i % 3]

                        def _sy2(e, i=i, kind=kind, c=c, w=w, osem=osem):
                            wge(e, sV, vm[i])
                            inc(e.dma_start(
                                out=sc_w[:, HBASE[kind] + w * c:
                                         HBASE[kind] + w * (c + 1)],
                                in_=qout[:, (i % 3) * FI:(i % 3) * FI + w]),
                                osem, 16)
                        blk.sync(_sy2)
                        qslot[i % 3] = ("d", (osem, osem.v))
                    return
                n_tb = w // 128
                spec = qslot[i % 3]

                def _v(e, i=i, w=w, spec=spec):
                    e.wait_ge(sFg[i % 3].h, slab_mark[i])
                    wait_qslot(e, spec)
                    sl = slab_in[:, (i % 3) * FI:(i % 3) * FI + w]
                    emit_quant(e, sl, w,
                               qout[:, (i % 3) * FI:(i % 3) * FI + w],
                               sQA[:, 0:w], sQB[:, 0:w], sAZ[:, 0:w],
                               sbm[:, 0:w // 16], sr6[:, 0:w // 16],
                               ssd[:, 0:w // 16],
                               sgn_ap=sSg[:, (i % 2) * FI:(i % 2) * FI + w],
                               az_mark=sign_mark[i], fuse=True, az_sem=sG,
                               az_sep=True, early_free=True)
                    slab_free[i] = sV.v
                    inc(e.drain(), sV)
                    vm[i] = sV.v
                blk.vector(_v)
                n_waves = (n_tb + 7) // 8
                wv_marks = []
                ev_base = ev_last[0]
                for wv in range(n_waves):
                    lo, hi = 8 * wv, min(8 * (wv + 1), n_tb)

                    def _t(e, i=i, lo=lo, hi=hi, wv=wv, ev_base=ev_base):
                        wge(e, sS, ev_base + wv)
                        if wv == 0:
                            wge(e, sV, vm[i])
                        for b in range(lo, hi):
                            i_ = e.transpose(
                                PS_Tv[:, 128 * (b - lo):128 * (b - lo + 1)],
                                qout[:, (i % 3) * FI + 128 * b:
                                     (i % 3) * FI + 128 * (b + 1)], identb[:])
                        inc(i_, sT)
                        wv_marks.append(sT.v)
                    blk.tensor(_t)

                def _s(e, i=i, kind=kind, c=c, n_tb=n_tb, n_waves=n_waves,
                       wv_marks=wv_marks):
                    nch = NCH[kind]
                    for wv in range(n_waves):
                        lo, hi = 8 * wv, min(8 * (wv + 1), n_tb)
                        wge(e, sT, wv_marks[wv])
                        if c == 0 and wv == 0:
                            wge(e, sF)
                        bt = bigT[:].rearrange("p (b cc) -> p b cc",
                                               cc=nch * 128)
                        i_ = e.activation(
                            bt[:, lo:hi, 128 * c:128 * (c + 1)],
                            PS_Tv.rearrange("p (b cc) -> p b cc", cc=128)
                            [:, 0:hi - lo, :], ACT.Copy)
                        inc(i_, sS)
                        ev_last[0] = sS.v
                    evac_mark[i] = sS.v
                blk.scalar(_s)
                qslot[i % 3] = ("t", wv_marks[-1])
                if c == NCH[kind] - 1:
                    scr, scr_cols = SCRD[kind]

                    def _syw(e, i=i, scr=scr, scr_cols=scr_cols):
                        wge(e, sS, evac_mark[i])
                        inc(e.dma_start(out=scr[:, 0:scr_cols],
                                        in_=bigT[:, 0:scr_cols]), sF, 16)
                    blk.sync(_syw)

            nit = len(items)
            for i in range(nit - 1):
                emit_dma(i)
                if items[i][0] in QK:
                    emit_sign(i)
                if i >= 1:
                    emit_consume(i - 1)
            emit_consume(nit - 2)
            emit_dma(nit - 1)
            emit_consume(nit - 1)

            def _sy_wb(e):
                wge(e, sFd[0])
                wge(e, sFd[1])
                wge(e, sC)
                inc(e.dma_start(out=wbuf[:], in_=sc_w[:, 0:8 * E2]), sW, 16)
                psiw_mark[0] = sW.v
            blk.sync(_sy_wb)

    with nc.Block() as blk:
        def _fence(e):
            wge(e, sF)
            wge(e, sX)
            wge(e, sFd[0])
            wge(e, sFd[1])
            wge(e, sC)
            wge(e, sG)
        blk.sync(_fence)

    xbuf = sb("xbuf", [128, 2 * E2], F32)
    xn = sb("xn", [128, E2], F32)
    xnT = sb("xnT", [128, 2 * D], BF16)
    psi = sb("psi", [128, 2 * E2], BF16)
    phi = sb("phi", [128, 2 * E2], BF16)
    gat = sb("gat", [128, 2 * D], BF16)
    scr2k = sb("scr2k", [128, E2], BF16)
    xnbf = scr2k
    qsA = sb("qsA", [128, D], F32)
    qsB = sb("qsB", [128, D], F32)
    hfin = sb("hfin", [128, 2 * E2], BF16)
    bmF = sb("bmF", [128, 64], F32)
    r6F = sb("r6F", [128, 64], F32)
    sdF = sb("sdF", [128, 64], F32)
    xqT = sb("xqT", [128, E2], BF16)
    hsl = sb("hsl", [128, 1024], F32)
    silu_t = sb("silu_t", [128, 1024], F32)
    hq = sb("hq", [128, 2 * FI], BF16)
    hqT = sb("hqT", [128, 2 * FI], BF16)
    gusl = sb("gusl", [128, 3 * 4096], BF16)
    dsl = sb("dsl", [128, 2 * E2], BF16)
    sgnb = sb("sgnb", [128, 1024], BF16)

    ISLW = [512, 512, 512, 512, 512, 256]

    fin_mark = [0]
    out_mark = [0]
    phi_mark = [0]
    phiw_mark = [0]
    xqT_mark = [0]
    PS_Abf = PS_A[:].bitcast(BF16)
    PS_Bbf = PS_B[:].bitcast(BF16)
    silu_bf = silu_t[:].bitcast(BF16)
    hqTf = hqT[:].bitcast(F32)

    def em_sy_head(g):
        xsl = E2 * (g % 2)

        def _sy(e):
            wge(e, sO, out_mark[0])
            for i in range(2):
                r = 128 * (2 * g + i)
                inc(e.dma_start(out=xbuf[:, xsl + 1024 * i:xsl + 1024 * (i + 1)],
                                in_=x_d[r:r + 128, :]), sX, 16)
            if g > 0:
                wge(e, sT, phi_mark[0])
                inc(e.dma_start(out=wbuf[:], in_=sc_w[:, 0:8 * E2]), sW, 16)
                psiw_mark[0] = sW.v
        blk.sync(_sy)

    def em_v_ln(g):
        xsl = E2 * (g % 2)

        def _v(e):
            wge(e, sX)
            x3 = xbuf[:, xsl:xsl + E2].rearrange("p (t c) -> p t c", c=1024)
            e.reduce_sum(stat[:, 0:2], x3, axis=AX)
            inc(e.drain(), sD)
            i = e.tensor_scalar(stat[:, 0:2], stat[:, 0:2], F(1.0 / 1024.0),
                                None, ALU.mult)
            inc(i, sV)
        blk.vector(_v)
        return sV.v

    def em_s_ln(g):
        xsl = E2 * (g % 2)

        def _s(e):
            wge(e, sS)
            wge(e, sX)
            wge(e, sV)
            for ti in range(2):
                e.activation(scr2k[:, 1024 * ti:1024 * (ti + 1)],
                             xbuf[:, xsl + 1024 * ti:xsl + 1024 * (ti + 1)],
                             ACT.Square, accum_out=stat[:, 4 + ti:5 + ti])
            inc(e.drain(), sS)
        blk.scalar(_s)
        return sS.v

    def em_v_ln2(g, sq_mark):
        xsl = E2 * (g % 2)

        def _v(e):
            wge(e, sS, sq_mark)
            wge(e, sV)
            var, y, t, sq = stat[:, 6:8], stat[:, 8:10], stat[:, 10:12], stat[:, 2:4]
            e.tensor_tensor(sq, stat[:, 0:2], stat[:, 0:2], ALU.mult)
            inc(e.drain(), sD)
            e.scalar_tensor_tensor(var, stat[:, 4:6], F(1.0 / 1024.0), sq,
                                   ALU.mult, ALU.subtract)
            inc(e.drain(), sD)
            e.tensor_scalar(var, var, F(1e-5), None, ALU.add)
            inc(e.drain(), sD)
            e.tensor_scalar(y, var, F(-0.5), F(1.5), ALU.mult, ALU.add)
            inc(e.drain(), sD)
            for _ in range(4):
                e.tensor_tensor(t, y, y, ALU.mult)
                inc(e.drain(), sD)
                e.tensor_tensor(t, t, var, ALU.mult)
                inc(e.drain(), sD)
                e.tensor_scalar(t, t, F(-0.5), F(1.5), ALU.mult, ALU.add)
                inc(e.drain(), sD)
                e.tensor_tensor(y, y, t, ALU.mult)
                inc(e.drain(), sD)
            for ti in range(2):
                e.tensor_scalar(xn[:, 1024 * ti:1024 * (ti + 1)],
                                xbuf[:, xsl + 1024 * ti:xsl + 1024 * (ti + 1)],
                                stat[:, ti:ti + 1], stat[:, 8 + ti:9 + ti],
                                ALU.subtract, ALU.mult)
            inc(e.drain(), sD)
            e.wait_ge(sD.h, sD.v)
            e.tensor_copy(xnbf[:, 0:D], xn[:, 0:D])
            e.tensor_copy(xnbf[:, D:2 * D], xn[:, D:2 * D])
            inc(e.drain(), sV)
        blk.vector(_v)
        return sV.v

    def em_t_tr(g, ti, ln2_v, bank_free_s):
        def _t(e):
            e.wait_ge(sV.h, ln2_v)
            e.wait_ge(sS.h, bank_free_s)
            for k in range(8):
                i = e.transpose(PS_Abf[:, 128 * k:128 * (k + 1)],
                                xnbf[:, D * ti + 128 * k:D * ti + 128 * (k + 1)],
                                identb[:])
            inc(i, sT)
        blk.tensor(_t)
        return sT.v

    def em_s_tr(g, ti, t_mark):
        def _s(e):
            wge(e, sT, t_mark)
            i = e.activation(xnT[:, D * ti:D * (ti + 1)], PS_Abf[:, 0:1024],
                             ACT.Copy)
            inc(i, sS)
        blk.scalar(_s)
        return sS.v

    def em_t_psi(g, ti, s_mark):
        def _t(e):
            wge(e, sS, s_mark)
            e.wait_ge(sW.h, psiw_mark[0])
            for k in range(8):
                for j in range(4):
                    e.matmul(PS_A[:, 512 * j:512 * (j + 1)],
                             xnT[:, D * ti + 128 * k:D * ti + 128 * (k + 1)],
                             wbuf[:, E2 * k + 512 * j:E2 * k + 512 * (j + 1)],
                             start=(k == 0), stop=False)
            for j in range(4):
                i = e.matmul(PS_A[:, 512 * j:512 * (j + 1)], ones1b[:],
                             bpsi_b[0:1, 512 * j:512 * (j + 1)],
                             start=False, stop=True)
            inc(i, sT)
        blk.tensor(_t)
        return sT.v

    def em_s_psi(g, ti, t_mark):
        def _s(e):
            wge(e, sT, t_mark)
            i = e.activation(psi[:, E2 * ti:E2 * (ti + 1)], PS_A[:], ACT.Copy)
            inc(i, sS)
        blk.scalar(_s)
        return sS.v

    def em_t_gat(g, ti, s_mark):
        def _t(e):
            wge(e, sS, s_mark)
            for k in range(8):
                for j in range(2):
                    e.matmul(PS_A[:, 512 * j:512 * (j + 1)],
                             xnT[:, D * ti + 128 * k:D * ti + 128 * (k + 1)],
                             wgat_b[:, D * k + 512 * j:D * k + 512 * (j + 1)],
                             start=(k == 0), stop=False)
            for j in range(2):
                i = e.matmul(PS_A[:, 512 * j:512 * (j + 1)], ones1b[:],
                             bgat_b[0:1, 512 * j:512 * (j + 1)],
                             start=False, stop=True)
            inc(i, sT)
        blk.tensor(_t)
        return sT.v

    def em_s_gat(g, ti, t_mark):
        def _s(e):
            wge(e, sT, t_mark)
            i = e.activation(gat[:, D * ti:D * (ti + 1)], PS_A[:, 0:1024],
                             ACT.Sigmoid)
            inc(i, sS)
        blk.scalar(_s)
        return sS.v

    def em_sy_phi(g, psi1_t):
        def _sy(e):
            wge(e, sT, psi1_t)
            inc(e.dma_start(out=wbuf[:], in_=sc_w[:, 8 * E2:16 * E2]), sW, 16)
            phiw_mark[0] = sW.v
        blk.sync(_sy)

    def em_t_phi(g, ti, s_mark):
        def _t(e):
            wge(e, sS, s_mark)
            e.wait_ge(sW.h, phiw_mark[0])
            for k in range(8):
                for j in range(4):
                    e.matmul(PS_A[:, 512 * j:512 * (j + 1)],
                             xnT[:, D * ti + 128 * k:D * ti + 128 * (k + 1)],
                             wbuf[:, E2 * k + 512 * j:E2 * k + 512 * (j + 1)],
                             start=(k == 0), stop=False)
            for j in range(4):
                i = e.matmul(PS_A[:, 512 * j:512 * (j + 1)], ones1b[:],
                             bphi_b[0:1, 512 * j:512 * (j + 1)],
                             start=False, stop=True)
            inc(i, sT)
            if ti == 1:
                phi_mark[0] = sT.v
        blk.tensor(_t)
        return sT.v

    def em_s_phi(g, ti, t_mark):
        def _s(e):
            wge(e, sT, t_mark)
            i = e.activation(phi[:, E2 * ti:E2 * (ti + 1)], PS_A[:], ACT.Copy)
            inc(i, sS)
        blk.scalar(_s)
        return sS.v

    def em_v_a(g, ti, s_mark, t_mark, dst):
        def _v(e):
            wge(e, sS, s_mark)
            wge(e, sT, t_mark)
            pr = psi[:, E2 * ti:E2 * ti + D]
            pim = psi[:, E2 * ti + D:E2 * (ti + 1)]
            g_ = gat[:, D * ti:D * (ti + 1)]
            t1, t2 = xqT[:, 0:D], xqT[:, D:2 * D]
            e.tensor_tensor(t1, pr, wm_re[:], ALU.mult)
            i = e.tensor_tensor(t2, pim, wm_im[:], ALU.mult)
            inc(i, sD)
            e.wait_ge(sD.h, sD.v)
            i = e.tensor_tensor(dst[:, 0:D], t1, t2, ALU.subtract)
            inc(i, sD)
            e.wait_ge(sD.h, sD.v)
            e.tensor_tensor(t1, pr, wm_im[:], ALU.mult)
            i = e.tensor_tensor(t2, pim, wm_re[:], ALU.mult)
            inc(i, sD)
            e.wait_ge(sD.h, sD.v)
            i = e.tensor_tensor(dst[:, D:E2], t1, t2, ALU.add)
            inc(i, sD)
            e.wait_ge(sD.h, sD.v)
            e.tensor_tensor(dst[:, 0:D], dst[:, 0:D], g_, ALU.mult)
            e.tensor_tensor(dst[:, D:E2], dst[:, D:E2], g_, ALU.mult)
            inc(e.drain(), sV)
        blk.vector(_v)
        return sV.v

    def em_t_c(g, ti, v_mark, s_mark, src):
        tt = 2 * g + ti

        def _t(e):
            wge(e, sV, v_mark)
            wge(e, sS, s_mark)
            first = (tt % seq_tiles == 0)
            prev = E2 * ((tt + 1) % 2)
            for j in range(4):
                i = e.matmul(PS_A[:, 512 * j:512 * (j + 1)], ltb[:],
                             src[:, 512 * j:512 * (j + 1)],
                             start=True, stop=first)
            if not first:
                for j in range(4):
                    i = e.matmul(PS_A[:, 512 * j:512 * (j + 1)], e127b[:],
                                 hfin[:, prev + 512 * j:prev + 512 * (j + 1)],
                                 start=False, stop=True)
            inc(i, sT)
        blk.tensor(_t)
        return sT.v

    def em_s_pv(g, ti, t_mark, dst):
        def _s(e):
            wge(e, sT, t_mark)
            i = e.activation(dst[:], PS_A[:], ACT.Copy)
            inc(i, sS)
        blk.scalar(_s)
        return sS.v

    def em_v_p(g, ti, s_mark, src):
        tt = 2 * g + ti

        def _v(e):
            wge(e, sS, s_mark)
            cur = E2 * (tt % 2)
            t1, t2 = xqT[:, 0:D], xqT[:, D:2 * D]
            e.tensor_tensor(t1, src[:, 0:D], wp_re[:], ALU.mult)
            i = e.tensor_tensor(t2, src[:, D:E2], wp_im[:], ALU.mult)
            inc(i, sD)
            e.wait_ge(sD.h, sD.v)
            i = e.tensor_tensor(hfin[:, cur:cur + D], t1, t2, ALU.subtract)
            inc(i, sD)
            e.wait_ge(sD.h, sD.v)
            e.tensor_tensor(t1, src[:, 0:D], wp_im[:], ALU.mult)
            i = e.tensor_tensor(t2, src[:, D:E2], wp_re[:], ALU.mult)
            inc(i, sD)
            e.wait_ge(sD.h, sD.v)
            i = e.tensor_tensor(hfin[:, cur + D:cur + E2], t1, t2, ALU.add)
            inc(e.drain(), sV)
        blk.vector(_v)
        return sV.v

    def em_v_pr(g):
        def _v(e):
            h3 = hfin[:].rearrange("p (q c) -> p q c", c=1024)
            e.reduce_sum(stat[:, 0:4], h3, axis=AX)
            inc(e.drain(), sD)
            i = e.tensor_scalar(stat[:, 0:4], stat[:, 0:4], F(1.0 / 1024.0),
                                None, ALU.mult)
            inc(i, sV)
        blk.vector(_v)
        return sV.v

    def em_s_pr(g, p1_v):
        def _s(e):
            wge(e, sS)
            wge(e, sV, p1_v)
            dumps = [scr2k[:, 0:1024], scr2k[:, 1024:2048],
                     gat[:, 0:1024], gat[:, 1024:2048]]
            for q in range(4):
                i = e.activation(dumps[q], hfin[:, 1024 * q:1024 * (q + 1)],
                                 ACT.Square, accum_out=stat[:, 8 + q:9 + q])
            inc(i, sS)
        blk.scalar(_s)
        return sS.v

    def em_v_pr2(g, s_mark):
        def _v(e):
            wge(e, sS, s_mark)
            wge(e, sV)
            sq = stat[:, 4:8]
            e.tensor_tensor(sq, stat[:, 0:4], stat[:, 0:4], ALU.mult)
            inc(e.drain(), sD)
            i = e.scalar_tensor_tensor(stat[:, 8:12], stat[:, 8:12],
                                       F(1.0 / 1024.0), sq, ALU.mult,
                                       ALU.subtract)
            inc(i, sV)
        blk.vector(_v)
        return sV.v

    def em_s_pr2(g, v_mark):
        def _s(e):
            wge(e, sV, v_mark)
            i = e.sqrt(stat[:, 12:16], stat[:, 8:12])
            inc(i, sS)
        blk.scalar(_s)
        return sS.v

    def em_v_pr3(g, s_mark):
        xsl = E2 * (g % 2)

        def _v(e):
            wge(e, sS, s_mark)
            y, rr, var = stat[:, 12:16], stat[:, 4:8], stat[:, 8:12]
            for _ in range(2):
                e.reciprocal(rr, y)
                inc(e.drain(), sD)
                e.tensor_tensor(rr, var, rr, ALU.mult)
                inc(e.drain(), sD)
                e.tensor_tensor(y, y, rr, ALU.add)
                inc(e.drain(), sD)
                e.tensor_scalar(y, y, F(0.5), None, ALU.mult)
                inc(e.drain(), sD)
            e.tensor_scalar(y, y, F(1e-6), None, ALU.add)
            inc(e.drain(), sD)
            e.reciprocal(rr, y)
            inc(e.drain(), sD)
            for ti in range(2):
                for p_ in range(2):
                    q = 2 * ti + p_
                    e.tensor_scalar(psi[:, E2 * ti + 1024 * p_:
                                        E2 * ti + 1024 * (p_ + 1)],
                                    hfin[:, 1024 * q:1024 * (q + 1)],
                                    stat[:, q:q + 1], stat[:, 4 + q:5 + q],
                                    ALU.subtract, ALU.mult)
            inc(e.drain(), sD)
            e.wait_ge(sD.h, sD.v)
            for ti in range(2):
                e.tensor_tensor(psi[:, E2 * ti:E2 * ti + D],
                                psi[:, E2 * ti:E2 * ti + D],
                                phi[:, E2 * ti:E2 * ti + D], ALU.mult)
                e.tensor_tensor(psi[:, E2 * ti + D:E2 * (ti + 1)],
                                psi[:, E2 * ti + D:E2 * (ti + 1)],
                                phi[:, E2 * ti + D:E2 * (ti + 1)], ALU.mult)
            inc(e.drain(), sD)
            e.wait_ge(sD.h, sD.v)
            for ti in range(2):
                a_ap = xbuf[:, xsl + 1024 * ti:xsl + 1024 * (ti + 1)]
                e.tensor_tensor(a_ap, a_ap, psi[:, E2 * ti:E2 * ti + D], ALU.add)
                inc(e.drain(), sD)
                e.wait_ge(sD.h, sD.v)
                i = e.tensor_tensor(a_ap, a_ap,
                                    psi[:, E2 * ti + D:E2 * (ti + 1)], ALU.add)
                inc(i, sD)
            inc(e.drain(), sV)
        blk.vector(_v)
        return sV.v

    def em_s_xqp(g, a1_v):
        def _s(e):
            wge(e, sS)
            wge(e, sV, a1_v)
            i = e.activation(gat[:, 0:2 * D], xn[:, 0:2 * D], ACT.Sign)
            inc(i, sS)
        blk.scalar(_s)
        return sS.v

    def em_v_xq(g, ti, sgn_s, free_t):
        def _v(e):
            if free_t is not None:
                wge(e, sT, free_t)
            emit_quant(e, xn[:, 1024 * ti:1024 * (ti + 1)], D,
                       hq[:, 1024 * ti:1024 * (ti + 1)],
                       qsA[:], qsB[:], hsl[:],
                       bmF[:, 0:64], r6F[:, 0:64], sdF[:, 0:64],
                       sgn_ap=gat[:, D * ti:D * (ti + 1)], az_mark=sgn_s,
                       fuse=True, az_sep=True)
            inc(e.drain(), sV)
        blk.vector(_v)
        return sV.v

    def em_t_xqtr(g, v_mark, ps_free_s):
        def _t(e):
            wge(e, sV, v_mark)
            wge(e, sS, ps_free_s)
            for ti in range(2):
                dstp = PS_Abf[:, 1024 * ti:1024 * (ti + 1)]
                for k in range(8):
                    i = e.transpose(dstp[:, 128 * k:128 * (k + 1)],
                                    hq[:, 1024 * ti + 128 * k:
                                       1024 * ti + 128 * (k + 1)], identb[:])
            inc(i, sT)
        blk.tensor(_t)
        return sT.v

    def em_s_xqe(g, t_mark):
        def _s(e):
            wge(e, sT, t_mark)
            i = e.activation(xqT[:, 0:2 * D], PS_Abf[:, 0:2048], ACT.Copy)
            inc(i, sS)
            xqT_mark[0] = sS.v
        blk.scalar(_s)
        return sS.v

    def em_t_gu(g, s, usv, vb_h1, expect):
        w = ISLW[s]
        gj, uj = 2 * s, 2 * s + 1

        def _t(e):
            if s == 0:
                wge(e, sS, xqT_mark[0])
                wge(e, sV, fin_mark[0])
            else:
                e.wait_ge(sV.h, vb_h1[s - 1])
            e.wait_ge(sFg[gj % 3].h, usv[gj])
            e.wait_ge(sFg[uj % 3].h, usv[uj])
            gsl_ap = gusl[:, 4096 * (gj % 3):4096 * (gj % 3) + 8 * w]
            usl_ap = gusl[:, 4096 * (uj % 3):4096 * (uj % 3) + 8 * w]
            for k in range(8):
                for (ti, po) in ((0, 0), (1, 1024)):
                    xt = xqT[:, D * ti + 128 * k:D * ti + 128 * (k + 1)]
                    e.matmul(PS_B[:, po:po + w], xt,
                             gsl_ap[:, w * k:w * (k + 1)],
                             start=(k == 0), stop=(k == 7))
                    i = e.matmul(PS_B[:, po + 512:po + 512 + w], xt,
                                 usl_ap[:, w * k:w * (k + 1)],
                                 start=(k == 0), stop=(k == 7))
            inc(i, sT)
            assert sT.v == expect, (sT.v, expect)
        blk.tensor(_t)
        return sT.v

    def em_sy_gu(g, usv, gu_plan, j0):
        def _sy(e):
            for j in range(j0, 12):
                mat, s = j % 2, j // 2
                w = ISLW[s]
                scr = sc_u if mat else sc_g
                slot = gusl[:, 4096 * (j % 3):4096 * (j % 3) + 8 * w]
                if j >= 3:
                    wge(e, sT, gu_plan[(j - 3) // 2])
                src_ = scr[:].rearrange("p (k c) -> p k c", c=FI)[:, :,
                                                                 512 * s:512 * s + w]
                inc(e.dma_start(out=slot.rearrange("p (k c) -> p k c", c=w),
                                in_=src_), sFg[j % 3], 16)
                assert sFg[j % 3].v == usv[j]
        blk.sync(_sy)

    pre_usv = {}

    def em_sy_gu_pre(nxt, gu_plan_cur):
        def _sy(e):
            for j in range(3):
                mat, s = j % 2, j // 2
                w = ISLW[s]
                scr = sc_u if mat else sc_g
                slot = gusl[:, 4096 * j:4096 * j + 8 * w]
                wge(e, sT, gu_plan_cur[(j + 9) // 2])
                src_ = scr[:].rearrange("p (k c) -> p k c", c=FI)[:, :,
                                                                 512 * s:512 * s + w]
                inc(e.dma_start(out=slot.rearrange("p (k c) -> p k c", c=w),
                                in_=src_), sFg[j], 16)
                pre_usv[j] = sFg[j].v
        blk.sync(_sy)

    def em_s_sig(g, s, gu_t):
        w = ISLW[s]

        def _s(e):
            wge(e, sS)
            wge(e, sT, gu_t)
            ps3 = PS_B[:].rearrange("p (t c) -> p t c", c=1024)[:, :, 0:w]
            sl3 = silu_t[:, 0:2 * w].rearrange("p (t c) -> p t c", c=w)
            i = e.activation(sl3, ps3, ACT.Silu)
            inc(i, sS)
        blk.scalar(_s)
        return sS.v

    def em_v_h1(g, s, sig_s, expect):
        w = ISLW[s]

        def _v(e):
            wge(e, sS, sig_s)
            sl3 = silu_t[:, 0:2 * w].rearrange("p (t c) -> p t c", c=w)
            u3 = PS_B[:].rearrange("p (t c) -> p t c", c=1024)[:, :, 512:512 + w]
            h3 = hsl[:, 0:2 * w].rearrange("p (t c) -> p t c", c=w)
            i = e.tensor_tensor(h3, sl3, u3, ALU.mult)
            inc(i, sV)
            assert sV.v == expect, (sV.v, expect)
        blk.vector(_v)
        return sV.v

    def em_s_sgn(g, s, h1_v):
        w = ISLW[s]

        def _s(e):
            wge(e, sS)
            wge(e, sV, h1_v)
            i = e.activation(sgnb[:, 0:2 * w], hsl[:, 0:2 * w], ACT.Sign)
            inc(i, sS)
        blk.scalar(_s)
        return sS.v

    def em_v_h2(g, s, sgn_s, expect):
        w = ISLW[s]
        off = 512 * s

        def _v(e):
            wge(e, sV)
            nb2 = 2 * w // 16
            hq3 = hq[:].rearrange("p (t c) -> p t c", c=FI)[:, :, off:off + w]
            emit_quant(e, hsl[:, 0:2 * w], 2 * w, hq3,
                       qsA[:, 0:2 * w], qsB[:, 0:2 * w], hqTf[:, 0:2 * w],
                       bmF[:, 0:nb2], r6F[:, 0:nb2], sdF[:, 0:nb2],
                       sgn_ap=sgnb[:, 0:2 * w], az_mark=sgn_s, fuse=True,
                       az_sep=True)
            inc(e.drain(), sV)
            assert sV.v == expect, (sV.v, expect)
        blk.vector(_v)
        return sV.v

    HTR_WAVES = [(ti, 128 * 8 * wv, min(128 * 8 * (wv + 1), FI))
                 for ti in range(2) for wv in range(3)]

    def em_t_htrw(g, q5_v, ebase):
        htr_marks = []
        for widx, (ti, lo, hi) in enumerate(HTR_WAVES):
            def _t(e, ti=ti, lo=lo, hi=hi, widx=widx):
                e.wait_ge(sS.h, ebase + max(0, widx - 1))
                if widx == 0:
                    wge(e, sV, q5_v)
                dstp = PS_Bbf[:, 0:1024] if widx % 2 == 0 else PS_Bbf[:, 1024:2048]
                nb = (hi - lo) // 128
                for b in range(nb):
                    i = e.transpose(dstp[:, 128 * b:128 * (b + 1)],
                                    hq[:, FI * ti + lo + 128 * b:
                                       FI * ti + lo + 128 * (b + 1)], identb[:])
                inc(i, sT)
                htr_marks.append(sT.v)
            blk.tensor(_t)
        return htr_marks

    def em_s_htr(g, htr_marks, ebase):
        def _s(e):
            for widx, (ti, lo, hi) in enumerate(HTR_WAVES):
                wge(e, sT, htr_marks[widx])
                srcp = PS_Bbf[:, 0:1024] if widx % 2 == 0 else PS_Bbf[:, 1024:2048]
                i = e.activation(hqT[:, FI * ti + lo:FI * ti + hi],
                                 srcp[:, 0:hi - lo], ACT.Copy)
                inc(i, sS)
                assert sS.v == ebase + widx + 1, (sS.v, ebase, widx)
        blk.scalar(_s)
        return sS.v

    def em_t_dn(g, s_mark, v_mark):
        dpair_freed = {}
        dpair_sem_val = {}
        for kp in range(11):
            dpair_sem_val[kp] = sFd[kp % 2].v + 16 * (kp // 2 + 1)

        def _t(e):
            wge(e, sS, s_mark)
            wge(e, sV, v_mark)
            for kp in range(11):
                e.wait_ge(sFd[kp % 2].h, dpair_sem_val[kp])
                for half in range(2):
                    kk = 2 * kp + half
                    for ti in range(2):
                        st_ = hqT[:, FI * ti + 128 * kk:FI * ti + 128 * (kk + 1)]
                        for j in range(2):
                            i = e.matmul(
                                PS_B[:, 1024 * ti + 512 * j:1024 * ti + 512 * (j + 1)],
                                st_,
                                dsl[:, E2 * (kp % 2) + D * half + 512 * j:
                                    E2 * (kp % 2) + D * half + 512 * (j + 1)],
                                start=(kk == 0), stop=(kk == 21))
                inc(i, sT)
                dpair_freed[kp] = sT.v
        blk.tensor(_t)

        def _sy(e):
            for kp in range(11):
                if kp >= 2:
                    wge(e, sT, dpair_freed[kp - 2])
                inc(e.dma_start(out=dsl[:, E2 * (kp % 2):E2 * (kp % 2) + E2],
                                in_=sc_d[:, E2 * kp:E2 * (kp + 1)]),
                    sFd[kp % 2], 16)
                assert sFd[kp % 2].v == dpair_sem_val[kp]
        blk.sync(_sy)
        return sT.v

    def em_v_fin(g, dn_t):
        xsl = E2 * (g % 2)

        def _v(e):
            wge(e, sT, dn_t)
            for ti in range(2):
                a_ap = xbuf[:, xsl + 1024 * ti:xsl + 1024 * (ti + 1)]
                e.tensor_tensor(a_ap, a_ap, PS_B[:, 1024 * ti:1024 * (ti + 1)],
                                ALU.add)
                inc(e.drain(), sD)
            inc(e.drain(), sV)
            fin_mark[0] = sV.v
        blk.vector(_v)
        return sV.v

    def em_sy_out(g):
        xsl = E2 * (g % 2)

        def _sy(e):
            wge(e, sV, fin_mark[0])
            for ti in range(2):
                r = 128 * (2 * g + ti)
                inc(e.dma_start(out=out_d[r:r + 128, :],
                                in_=xbuf[:, xsl + 1024 * ti:xsl + 1024 * (ti + 1)]),
                    sO, 16)
            out_mark[0] = sO.v
        blk.sync(_sy)

    def emit_attn0():
        em_sy_head(0)
        em_v_ln(0)
        m = em_s_ln(0)
        ln2 = em_v_ln2(0, m)
        m = em_t_tr(0, 0, ln2, xqT_mark[0])
        m = em_s_tr(0, 0, m)
        m = em_t_psi(0, 0, m)
        psi1 = None
        m = em_s_psi(0, 0, m)
        m = em_t_gat(0, 0, m)
        m = em_s_gat(0, 0, m)
        tr1m = em_t_tr(0, 1, ln2, m)
        m = em_s_tr(0, 1, tr1m)
        psi1 = em_t_psi(0, 1, m)
        em_sy_phi(0, psi1)
        m = em_s_psi(0, 1, psi1)
        m = em_t_gat(0, 1, m)
        sg1 = em_s_gat(0, 1, m)
        m = em_t_phi(0, 0, sg1)
        m = em_s_phi(0, 0, m)
        m = em_t_phi(0, 1, m)
        sphi1 = em_s_phi(0, 1, m)
        a0 = em_v_a(0, 0, sphi1, tr1m, scr2k)
        c0 = em_t_c(0, 0, a0, sphi1, scr2k)
        a1 = em_v_a(0, 1, sphi1, tr1m, silu_bf)
        pv0 = em_s_pv(0, 0, c0, scr2k)
        p0 = em_v_p(0, 0, pv0, scr2k)
        c1 = em_t_c(0, 1, p0, pv0, silu_bf)
        pv1 = em_s_pv(0, 1, c1, silu_bf)
        p1 = em_v_p(0, 1, pv1, silu_bf)
        em_v_pr(0)
        m = em_s_pr(0, p1)
        m = em_v_pr2(0, m)
        m = em_s_pr2(0, m)
        em_v_pr3(0, m)
        sgn_s = em_s_xqp(0, a1)
        em_v_xq(0, 0, sgn_s, None)
        xqv = em_v_xq(0, 1, sgn_s, None)
        m = em_t_xqtr(0, xqv, pv1)
        em_s_xqe(0, m)

    def emit_period(g):
        nxt = g + 1
        has_next = nxt < ng
        tbase = sT.v
        vbase = sV.v
        if has_next:
            gu_plan = {s: tbase + [1, 2, 5, 7, 9, 12][s] for s in range(6)}
            vb_h1 = {s: vbase + [1, 5, 7, 9, 11, 13][s] for s in range(6)}
            vb_q = {s: vbase + [2, 6, 8, 10, 12, 14][s] for s in range(6)}
            vb_a1 = vbase + 16
            vb_p1 = vbase + 20
        else:
            gu_plan = {s: tbase + s + 1 for s in range(6)}
            vb_h1 = {s: vbase + 2 * s + 1 for s in range(6)}
            vb_q = {s: vbase + 2 * s + 2 for s in range(6)}
        if g == 0:
            usv = {j: sFg[j % 3].v + 16 * (j // 3 + 1) for j in range(12)}
            j0 = 0
        else:
            usv = dict(pre_usv)
            for j in range(3, 12):
                usv[j] = sFg[j % 3].v + 16 * ((j - 3) // 3 + 1)
            j0 = 3

        if has_next:
            em_sy_head(nxt)
        em_sy_gu(g, usv, gu_plan, j0)
        if has_next:
            em_sy_phi(nxt, tbase + 10)

        gu0 = em_t_gu(g, 0, usv, vb_h1, gu_plan[0])
        sig0 = em_s_sig(g, 0, gu0)
        h1v = em_v_h1(g, 0, sig0, vb_h1[0])
        sgn0 = em_s_sgn(g, 0, h1v)
        if has_next:
            mq = em_s_ln(nxt)
        em_v_h2(g, 0, sgn0, vb_q[0])

        gu1 = em_t_gu(g, 1, usv, vb_h1, gu_plan[1])
        sig1 = em_s_sig(g, 1, gu1)
        if has_next:
            em_v_ln(nxt)
            ln2 = em_v_ln2(nxt, mq)
            trm = em_t_tr(nxt, 0, ln2, xqT_mark[0])
            str0 = em_s_tr(nxt, 0, trm)
        h1v = em_v_h1(g, 1, sig1, vb_h1[1])
        sgn1 = em_s_sgn(g, 1, h1v)
        em_v_h2(g, 1, sgn1, vb_q[1])
        if has_next:
            tpsi0 = em_t_psi(nxt, 0, str0)

        gu2 = em_t_gu(g, 2, usv, vb_h1, gu_plan[2])
        sig2 = em_s_sig(g, 2, gu2)
        h1v = em_v_h1(g, 2, sig2, vb_h1[2])
        if has_next:
            spsi0 = em_s_psi(nxt, 0, tpsi0)
        sgn2 = em_s_sgn(g, 2, h1v)
        em_v_h2(g, 2, sgn2, vb_q[2])
        if has_next:
            tgat0 = em_t_gat(nxt, 0, spsi0)

        gu3 = em_t_gu(g, 3, usv, vb_h1, gu_plan[3])
        sig3 = em_s_sig(g, 3, gu3)
        h1v = em_v_h1(g, 3, sig3, vb_h1[3])
        if has_next:
            sgat0 = em_s_gat(nxt, 0, tgat0)
            trm1 = em_t_tr(nxt, 1, ln2, sgat0)
            str1 = em_s_tr(nxt, 1, trm1)
        sgn3 = em_s_sgn(g, 3, h1v)
        em_v_h2(g, 3, sgn3, vb_q[3])

        gu4 = em_t_gu(g, 4, usv, vb_h1, gu_plan[4])
        sig4 = em_s_sig(g, 4, gu4)
        h1v = em_v_h1(g, 4, sig4, vb_h1[4])
        if has_next:
            tpsi1 = em_t_psi(nxt, 1, str1)
            assert tpsi1 == tbase + 10, (tpsi1, tbase)
            spsi1 = em_s_psi(nxt, 1, tpsi1)
        sgn4 = em_s_sgn(g, 4, h1v)
        em_v_h2(g, 4, sgn4, vb_q[4])
        if has_next:
            tgat1 = em_t_gat(nxt, 1, spsi1)
            sgat1 = em_s_gat(nxt, 1, tgat1)

        gu5 = em_t_gu(g, 5, usv, vb_h1, gu_plan[5])
        sig5 = em_s_sig(g, 5, gu5)
        h1v = em_v_h1(g, 5, sig5, vb_h1[5])
        sgn5 = em_s_sgn(g, 5, h1v)
        q5 = em_v_h2(g, 5, sgn5, vb_q[5])
        if has_next:
            tphi0 = em_t_phi(nxt, 0, sgat1)
            sphi0 = em_s_phi(nxt, 0, tphi0)
            tphi1 = em_t_phi(nxt, 1, sphi0)
            sphi1 = em_s_phi(nxt, 1, tphi1)

        if has_next:
            em_sy_gu_pre(nxt, gu_plan)
        ebase = sS.v
        htr_marks = em_t_htrw(g, q5, ebase)
        shtr = em_s_htr(g, htr_marks, ebase)
        if has_next:
            sgn_s = em_s_xqp(nxt, vb_a1)
            a0 = em_v_a(nxt, 0, sgat1, trm1, scr2k)
            assert a0 == vbase + 15
            c0 = em_t_c(nxt, 0, a0, sphi1, scr2k)
            a1 = em_v_a(nxt, 1, sgat1, trm1, silu_bf)
            assert a1 == vb_a1
            em_v_xq(nxt, 0, sgn_s, htr_marks[-1])
            pv0 = em_s_pv(nxt, 0, c0, scr2k)
            dn_t = em_t_dn(g, shtr, vb_h1[5])
            p0 = em_v_p(nxt, 0, pv0, scr2k)
            c1 = em_t_c(nxt, 1, p0, pv0, silu_bf)
            xqv = em_v_xq(nxt, 1, sgn_s, None)
            pv1 = em_s_pv(nxt, 1, c1, silu_bf)
            p1 = em_v_p(nxt, 1, pv1, silu_bf)
            assert p1 == vb_p1
            em_v_pr(nxt)
            m = em_s_pr(nxt, vb_p1)
            m2 = em_v_pr2(nxt, m)
            m = em_s_pr2(nxt, m2)
            fv = em_v_fin(g, dn_t)
            assert fv == vbase + 23
            em_sy_out(g)
            em_v_pr3(nxt, m)
            mt = em_t_xqtr(nxt, vb_p1, pv1)
            em_s_xqe(nxt, mt)
        else:
            dn_t = em_t_dn(g, shtr, vb_h1[5])
            fv = em_v_fin(g, dn_t)
            assert fv == vbase + 13
            em_sy_out(g)

    with nc.Block() as blk:
        emit_attn0()
        for g in range(ng):
            emit_period(g)

        def _sy_end(e):
            wge(e, sO)
        blk.sync(_sy_end)

    ctx.close()
    return nc


def _shard_inputs(inputs, tpc=4096):
    x = np.asarray(inputs["L0_attn_x"])
    B = x.shape[0]
    per = tpc // x.shape[1]
    wpsiT = np.ascontiguousarray(np.asarray(inputs["L0_attn_W_psi"]).T)
    wphiT = np.ascontiguousarray(np.asarray(inputs["L0_attn_W_phi"]).T)
    wgateT = np.ascontiguousarray(np.asarray(inputs["L0_attn_W_gate"]).T)
    shared = {
        "wpsiT": wpsiT, "wphiT": wphiT, "wgateT": wgateT,
        "bpsi": np.ascontiguousarray(np.asarray(inputs["L0_attn_b_psi"]).reshape(1, -1)),
        "bphi": np.ascontiguousarray(np.asarray(inputs["L0_attn_b_phi"]).reshape(1, -1)),
        "bgate": np.ascontiguousarray(np.asarray(inputs["L0_attn_b_gate"]).reshape(1, -1)),
        "omega": np.ascontiguousarray(np.asarray(inputs["L0_attn_omega"]).reshape(1, -1)),
        "lgam": np.ascontiguousarray(np.asarray(inputs["L0_attn_log_gamma"]).reshape(1, -1)),
        "dt": np.ascontiguousarray(np.asarray(inputs["L0_attn_dt"]).reshape(1, 1)),
        "wg": np.ascontiguousarray(np.asarray(inputs["L0_ffn_w_gate"])),
        "wu": np.ascontiguousarray(np.asarray(inputs["L0_ffn_w_up"])),
        "wd": np.ascontiguousarray(np.asarray(inputs["L0_ffn_w_down"])),
    }
    in_maps = []
    ncores = B // per
    for c in range(ncores):
        m = dict(shared)
        m["x"] = np.ascontiguousarray(
            x[c * per:(c + 1) * per].reshape(tpc, D))
        in_maps.append(m)
    return in_maps, ncores


def kernel(**inputs):
    x = np.asarray(inputs["L0_attn_x"])
    B, L, _ = x.shape
    tpc = (B * L) // 8
    in_maps, ncores = _shard_inputs(inputs, tpc)
    nc = build_program(tpc=tpc, seq_tiles=L // 128)
    res = run_bass_kernel_spmd(nc, in_maps, core_ids=list(range(ncores)))
    outs = [np.asarray(r["out"]).reshape(-1, L, D) for r in res.results]
    return np.concatenate(outs, axis=0).astype(np.float32)


# version 31
# speedup vs baseline: 1.0218x; 1.0105x over previous
import contextlib
import numpy as np
import ml_dtypes
import concourse.bass as bass
import concourse.mybir as mybir
from concourse.bass_utils import run_bass_kernel_spmd

F32 = mybir.dt.float32
BF16 = mybir.dt.bfloat16
AX = mybir.AxisListType.X
ALU = mybir.AluOpType
ACT = mybir.ActivationFunctionType

D = 1024
E2 = 2048
FI = 2816
def F(v):
    return float(np.float32(v))


AMAX = F(0.65625)
C1_2PI = F(6.2831855)
C2_2PI = F(2 * np.pi - 6.283185482025146484375)
PI = F(np.pi)
PI3 = F(3 * np.pi)
HALF_PI = F(np.pi / 2)

_QOPS = {}


def _ensure_qops():
    if _QOPS:
        return _QOPS
    from concourse.dve_spec import (Spec, Src0, Src1, C0, C1, C2, Zero, One,
                                    lower, AluOp, select, Bin)
    from concourse.dve_op import DveOp
    import concourse.dve_ops as dmod

    replaceable = ["BITWISE_XOR_TT_BV", "SHIFT_RIGHT_TT_BV"]

    def mk(name, spec, ref):
        for o in dmod.OPS:
            if o.name == name:
                return o
        shas = {}
        for ver in ("v3", "v4"):
            try:
                shas[ver] = lower(spec, ver=ver, perf_en=False, name=name).sha(ver)
            except Exception:
                pass
        op = DveOp(name, spec, subdim=False, uops_sha=shas, reference=ref)
        if dmod._CUSTOM_DVE_ROW_BASE + len(dmod.OPS) <= 31:
            dmod.OPS.append(op)
            row = dmod._CUSTOM_DVE_ROW_BASE + len(dmod.OPS) - 1
        else:
            victim = replaceable.pop()
            idx = next(i for i, o in enumerate(dmod.OPS) if o.name == victim)
            dmod.OPS[idx] = op
            row = dmod._CUSTOM_DVE_ROW_BASE + idx
            dmod._SUB_OPCODE_FOR_NAME.pop(victim, None)
        dmod._SUB_OPCODE_FOR_NAME[name] = row
        dmod.CUSTOM_DVE_SPECS[name] = op.spec
        dmod.CUSTOM_DVE_REFS[name] = op.ref
        return op

    prod = Src0 * Src1
    _QOPS["u"] = mk("OPQ_U_ANT", Spec(body=Bin(AluOp.ABSOLUTE_VALUE, prod, prod)),
                    lambda in0, in1, s0, s1, imm2: np.abs(in0 * in1))
    u = Src0
    um1 = u - One
    low = ((u > C0) + (u > C1)) + ((um1 > C0) + (um1 > C1))

    def _ref_low(in0, in1, s0, s1, imm2):
        dt_ = in0.dtype
        um = in0 - dt_.type(1.0)
        return ((in0 > s0).astype(dt_) + (in0 > s1).astype(dt_)
                + (um > s0).astype(dt_) + (um > s1).astype(dt_))
    _QOPS["low"] = mk("OPQ_LOW_ANT", Spec(body=low), _ref_low)
    ch1 = u > C0
    ch2 = u > C1
    ch3 = u > C2
    chc = (ch1 + ch2) + (ch3 + ch3)
    high = Src1 + (chc + chc)

    def _ref_high(in0, in1, s0, s1, imm2):
        dt_ = in0.dtype
        c = ((in0 > s0).astype(dt_) + (in0 > s1).astype(dt_)
             + dt_.type(2.0) * (in0 > imm2).astype(dt_))
        return in1 + dt_.type(2.0) * c
    _QOPS["high"] = mk("OPQ_HIGH_ANT", Spec(body=high), _ref_high)
    sgn = select(Src0 < Zero, Zero - Src1, Src1)
    _QOPS["sign"] = mk("OPQ_SIGN_ANT", Spec(body=sgn),
                       lambda in0, in1, s0, s1, imm2: np.where(in0 < 0, -in1, in1))
    return _QOPS


def build_program(tpc=4096, seq_tiles=16):
    nt = tpc // 128
    ng = nt // 2
    assert nt % 2 == 0 and seq_tiles % 2 == 0

    nc = bass.Bass("TRN2", target_bir_lowering=False)

    x_d = nc.dram_tensor("x", [tpc, D], F32, kind="ExternalInput").ap()
    wpsiT_d = nc.dram_tensor("wpsiT", [D, E2], F32, kind="ExternalInput").ap()
    wphiT_d = nc.dram_tensor("wphiT", [D, E2], F32, kind="ExternalInput").ap()
    wgateT_d = nc.dram_tensor("wgateT", [D, D], F32, kind="ExternalInput").ap()
    bpsi_d = nc.dram_tensor("bpsi", [1, E2], F32, kind="ExternalInput").ap()
    bphi_d = nc.dram_tensor("bphi", [1, E2], F32, kind="ExternalInput").ap()
    bgate_d = nc.dram_tensor("bgate", [1, D], F32, kind="ExternalInput").ap()
    omega_d = nc.dram_tensor("omega", [1, D], F32, kind="ExternalInput").ap()
    lgam_d = nc.dram_tensor("lgam", [1, D], F32, kind="ExternalInput").ap()
    dt_d = nc.dram_tensor("dt", [1, 1], F32, kind="ExternalInput").ap()
    wg_d = nc.dram_tensor("wg", [FI, D], F32, kind="ExternalInput").ap()
    wu_d = nc.dram_tensor("wu", [FI, D], F32, kind="ExternalInput").ap()
    wd_d = nc.dram_tensor("wd", [D, FI], F32, kind="ExternalInput").ap()
    out_d = nc.dram_tensor("out", [tpc, D], F32, kind="ExternalOutput").ap()
    sc_g = nc.dram_tensor("sc_g", [128, 8 * FI], BF16, kind="Internal").ap()
    sc_u = nc.dram_tensor("sc_u", [128, 8 * FI], BF16, kind="Internal").ap()
    sc_d = nc.dram_tensor("sc_d", [128, 22 * D], BF16, kind="Internal").ap()
    sc_w = nc.dram_tensor("sc_w", [128, 16 * E2], BF16, kind="Internal").ap()

    identN = np.eye(128, dtype=ml_dtypes.bfloat16)
    ltN = np.triu(np.ones((128, 128), dtype=np.float32)).astype(ml_dtypes.bfloat16)
    e127N = np.zeros((128, 128), dtype=np.float32)
    e127N[127, :] = 1.0
    e127N = e127N.astype(ml_dtypes.bfloat16)
    ident_d = nc.inline_tensor(identN, "identc").ap()
    lt_d = nc.inline_tensor(ltN, "ltc").ap()
    e127_d = nc.inline_tensor(e127N, "e127c").ap()
    ones1b_d = nc.inline_tensor(np.ones((1, 128), dtype=ml_dtypes.bfloat16), "o1b").ap()
    ones1f_d = nc.inline_tensor(np.ones((1, 128), dtype=np.float32), "o1f").ap()
    tp1_d = nc.inline_tensor(np.arange(1, 129, dtype=np.float32).reshape(128, 1),
                             "tp1").ap()
    tn1_d = nc.inline_tensor(-np.arange(1, 129, dtype=np.float32).reshape(128, 1),
                             "tn1").ap()

    ctx = contextlib.ExitStack()

    def sb(name, shape, dt):
        return ctx.enter_context(nc.sbuf_tensor(name, shape, dt))

    class Sem:
        def __init__(self, name):
            self.h = ctx.enter_context(nc.semaphore(name))
            self.v = 0

    sT, sV, sS = Sem("sT"), Sem("sV"), Sem("sS")
    sX, sF, sO, sG = Sem("sX"), Sem("sF"), Sem("sO"), Sem("sG")
    sW = Sem("sW")
    sC = Sem("sC")
    sFg = [Sem("sFg0"), Sem("sFg1"), Sem("sFg2")]
    sFd = [Sem("sFd0"), Sem("sFd1")]
    sD = Sem("sD")

    def inc(instr, s, n=1):
        instr.then_inc(s.h, n)
        s.v += n

    def wge(eng, s, v=None):
        eng.wait_ge(s.h, s.v if v is None else v)

    wgat_b = sb("wgat_b", [128, 8 * D], BF16)
    bpsi_b = sb("bpsi_b", [1, E2], BF16)
    bphi_b = sb("bphi_b", [1, E2], BF16)
    bgat_b = sb("bgat_b", [1, D], BF16)
    identb = sb("identb", [128, 128], BF16)
    ltb = sb("ltb", [128, 128], BF16)
    e127b = sb("e127b", [128, 128], BF16)
    ones1b = sb("ones1b", [1, 128], BF16)
    ones1f = sb("ones1f", [1, 128], F32)
    tp1c = sb("tp1c", [128, 1], F32)
    tn1c = sb("tn1c", [128, 1], F32)
    wm_re = sb("wm_re", [128, D], BF16)
    wm_im = sb("wm_im", [128, D], BF16)
    wp_re = sb("wp_re", [128, D], BF16)
    wp_im = sb("wp_im", [128, D], BF16)
    stat = sb("stat", [128, 16], F32)
    wbuf = sb("wbuf", [128, 8 * E2], BF16)

    PS_A = ctx.enter_context(nc.psum_tensor("PS_A", [128, 2048], F32))
    PS_B = ctx.enter_context(nc.psum_tensor("PS_B", [128, 2048], F32))
    PS_Tv = PS_B[:].bitcast(BF16)

    with nc.Block() as blk:
        def _sy(e):
            for dst, src in [(identb, ident_d), (ltb, lt_d), (e127b, e127_d),
                             (ones1b, ones1b_d), (ones1f, ones1f_d),
                             (tp1c, tp1_d), (tn1c, tn1_d)]:
                inc(e.dma_start(out=dst[:], in_=src), sX, 16)
        blk.sync(_sy)



    with contextlib.ExitStack() as tctx:
        tsb = lambda n, s, d: tctx.enter_context(nc.sbuf_tensor(n, s, d))
        tbl_sc = tsb("tbl_sc", [128, 2 * D], F32)
        tbl_e = tsb("tbl_e", [128, D], F32)
        tbl_p = tsb("tbl_p", [128, D], F32)
        tbl_m = tsb("tbl_m", [128, D], F32)
        tbl_s = tsb("tbl_s", [128, D], F32)
        tbl_c = tsb("tbl_c", [128, D], F32)
        tbl_k = tsb("tbl_k", [128, D], F32)
        rowa = tsb("rowa", [1, D], F32)
        roww = tsb("roww", [1, D], F32)
        dtab = tsb("dtab", [1, 1], F32)
        lgam_r = tsb("lgam_r", [1, D], F32)
        omega_r = tsb("omega_r", [1, D], F32)
        dt_r = tsb("dt_r", [1, 1], F32)

        with nc.Block() as blk:
            def _sy0(e):
                for dst, src_ in [(lgam_r, lgam_d), (omega_r, omega_d),
                                  (dt_r, dt_d)]:
                    inc(e.dma_start(out=dst[:], in_=src_), sX, 16)
            blk.sync(_sy0)

            def _sc(e):
                wge(e, sX)
                e.activation(dtab[:], dt_r[:], ACT.Abs)
                i = e.activation(rowa[:], lgam_r[:], ACT.Exp)
                inc(i, sS)
            blk.scalar(_sc)

            def _v(e):
                wge(e, sS)
                e.tensor_scalar(rowa[:], rowa[:], dtab[0:1, 0:1], AMAX,
                                ALU.mult, ALU.min)
                e.tensor_scalar(roww[:], omega_r[:], dtab[0:1, 0:1], None, ALU.mult)
                inc(e.drain(), sV)
            blk.vector(_v)

            def _t(e):
                wge(e, sV)
                for j in range(2):
                    e.matmul(PS_A[:, 512 * j:512 * (j + 1)], ones1f[:],
                             rowa[0:1, 512 * j:512 * (j + 1)], start=True, stop=True)
                for j in range(2):
                    i = e.matmul(PS_A[:, 1024 + 512 * j:1536 + 512 * j], ones1f[:],
                                 roww[0:1, 512 * j:512 * (j + 1)], start=True,
                                 stop=True)
                inc(i, sT)
            blk.tensor(_t)

            def _v2(e):
                wge(e, sT)
                e.tensor_copy(tbl_sc[:], PS_A[:])
                inc(e.drain(), sV)
            blk.vector(_v2)

        a_bc = tbl_sc[:, 0:D]
        w_bc = tbl_sc[:, D:2 * D]

        def emit_table(sgn, ecol, o_re, o_im):
            with nc.Block() as blk:
                def _v(e):
                    wge(e, sV)
                    e.tensor_scalar(tbl_e[:], a_bc, ecol[:, 0:1], None, ALU.mult)
                    e.tensor_scalar(tbl_p[:], w_bc, tp1c[:, 0:1], None, ALU.mult)
                    e.drain()
                    e.tensor_scalar(tbl_k[:], tbl_p[:], PI, None, ALU.is_gt)
                    e.drain()
                    e.scalar_tensor_tensor(tbl_k[:], tbl_p[:], PI3, tbl_k[:],
                                           ALU.is_gt, ALU.add)
                    e.drain()
                    e.scalar_tensor_tensor(tbl_p[:], tbl_k[:], F(-C1_2PI), tbl_p[:],
                                           ALU.mult, ALU.add)
                    e.drain()
                    e.scalar_tensor_tensor(tbl_p[:], tbl_k[:], F(-C2_2PI), tbl_p[:],
                                           ALU.mult, ALU.add)
                    inc(e.drain(), sV)
                blk.vector(_v)

                def _s(e):
                    wge(e, sV)
                    e.activation(tbl_m[:], tbl_e[:], ACT.Exp)
                    i = e.activation(tbl_s[:], tbl_p[:], ACT.Sin)
                    inc(i, sS)
                blk.scalar(_s)

                def _v2(e):
                    wge(e, sS)
                    e.tensor_scalar(tbl_c[:], tbl_p[:], F(-1.0), HALF_PI,
                                    ALU.mult, ALU.add)
                    e.drain()
                    e.tensor_scalar(tbl_k[:], tbl_c[:], PI, None, ALU.is_gt)
                    e.drain()
                    e.scalar_tensor_tensor(tbl_c[:], tbl_k[:], F(-C1_2PI), tbl_c[:],
                                           ALU.mult, ALU.add)
                    e.drain()
                    e.scalar_tensor_tensor(tbl_c[:], tbl_k[:], F(-C2_2PI), tbl_c[:],
                                           ALU.mult, ALU.add)
                    inc(e.drain(), sV)
                blk.vector(_v2)

                def _s2(e):
                    wge(e, sV)
                    i = e.activation(tbl_c[:], tbl_c[:], ACT.Sin)
                    inc(i, sS)
                blk.scalar(_s2)

                def _v3(e):
                    wge(e, sS)
                    e.tensor_tensor(o_re[:], tbl_m[:], tbl_c[:], ALU.mult)
                    e.scalar_tensor_tensor(o_im[:], tbl_m[:], F(sgn), tbl_s[:],
                                           ALU.mult, ALU.mult)
                    inc(e.drain(), sV)
                blk.vector(_v3)

        emit_table(-1.0, tp1c, wm_re, wm_im)
        emit_table(+1.0, tn1c, wp_re, wp_im)

    def bc16(ap2, nblk):
        return ap2.unsqueeze(2).broadcast_to([128, nblk, 16])

    def emit_quant(ve, z_ap, n, out_bf, QA, QB, AZ, bm, r6, sdq,
                   sgn_ap=None, az_mark=None, fuse=False, az_sem=None,
                   az_sep=False, early_free=False):
        nb = n // 16
        zv = z_ap.rearrange("p (b k) -> p b k", k=16)
        az3 = AZ.rearrange("p (b k) -> p b k", k=16)

        def E(i):
            inc(i, sD)
            return sD.v

        def W(m):
            ve.wait_ge(sD.h, m)

        if az_sem is None:
            az_sem = sS
        if sgn_ap is not None and not fuse:
            ve.wait_ge(az_sem.h, az_mark)
        m_red = E(ve.reduce_max(bm, zv, axis=AX, apply_absolute_value=True))
        if fuse:
            W(m_red)
            if not az_sep:
                ve.wait_ge(az_sem.h, az_mark)
            i_az = ve.scalar_tensor_tensor(AZ, z_ap, F(-1.0), z_ap,
                                           ALU.mult, ALU.max)
            if early_free:
                inc(i_az, sV)
                m_az = ("V", sV.v)
            else:
                inc(i_az, sD)
                m_az = ("D", sD.v)
        if sgn_ap is None:
            m_az = E(ve.scalar_tensor_tensor(AZ, z_ap, F(-1.0), z_ap, ALU.mult,
                                             ALU.max))
        W(m_red)
        E(ve.reciprocal(r6, bm))
        W(sD.v)
        m_r6 = E(ve.tensor_scalar(r6, r6, F(6.0), F(1e12), ALU.mult, ALU.min))
        W(m_r6)
        m_sd = E(ve.reciprocal(sdq, r6))
        W(m_sd)
        m_sdh = E(ve.tensor_scalar(sdq, sdq, F(0.5), None, ALU.mult))
        if sgn_ap is None:
            W(m_az)
        if fuse:
            if m_az[0] == "V":
                ve.wait_ge(sV.h, m_az[1])
            else:
                W(m_az[1])
        m_u = E(ve.tensor_tensor(az3, az3, bc16(r6, nb), ALU.mult))
        W(m_u)
        a = E(ve.tensor_scalar(QA, AZ, F(0.25), None, ALU.is_gt))
        c = E(ve.tensor_scalar(QB, AZ, F(2.5), None, ALU.is_gt))
        for thr_a, thr_c in ((0.75, 3.5), (1.25, 5.0), (1.75, 5.0)):
            W(a)
            a = E(ve.scalar_tensor_tensor(QA, AZ, F(thr_a), QA, ALU.is_gt,
                                          ALU.add))
            W(c)
            c = E(ve.scalar_tensor_tensor(QB, AZ, F(thr_c), QB, ALU.is_gt,
                                          ALU.add))
        W(a)
        W(c)
        q2 = E(ve.scalar_tensor_tensor(QB, QB, F(2.0), QA, ALU.mult, ALU.add))
        W(q2)
        if sgn_ap is None:
            s1 = E(ve.scalar_tensor_tensor(QA, z_ap, F(0.0), QB, ALU.is_gt,
                                           ALU.mult))
            s2 = E(ve.scalar_tensor_tensor(AZ, z_ap, F(0.0), QB, ALU.is_lt,
                                           ALU.mult))
            W(s1)
            W(s2)
            qs = E(ve.tensor_tensor(QA, QA, AZ, ALU.subtract))
            W(qs)
            W(m_sdh)
            E(ve.tensor_tensor(out_bf, QA.rearrange("p (b k) -> p b k", k=16),
                               bc16(sdq, nb), ALU.mult))
        else:
            W(m_sdh)
            t = E(ve.tensor_tensor(out_bf,
                                   QB.rearrange("p (b k) -> p b k", k=16),
                                   bc16(sdq, nb), ALU.mult))
            W(t)
            if fuse and az_sep:
                ve.wait_ge(az_sem.h, az_mark)
            E(ve.tensor_tensor(out_bf, out_bf, sgn_ap, ALU.mult))

    psiw_mark = [0]
    with contextlib.ExitStack() as qctx:
        qsb = lambda n, s, d: qctx.enter_context(nc.sbuf_tensor(n, s, d))
        slab_in = qsb("slab_in", [128, 3 * FI], F32)
        sAZ = qsb("sAZ", [128, FI], F32)
        sQA = qsb("sQA", [128, FI], F32)
        sQB = qsb("sQB", [128, FI], F32)
        sbm = qsb("sbm", [128, FI // 16], F32)
        sr6 = qsb("sr6", [128, FI // 16], F32)
        ssd = qsb("ssd", [128, FI // 16], F32)
        qout = qsb("qout", [128, 3 * FI], BF16)
        bigT = qsb("bigT", [128, 8 * FI], BF16)

        sSg = qsb("sSg", [128, 2 * FI], BF16)
        a_items = ([("psi", c, E2) for c in range(8)]
                   + [("phi", c, E2) for c in range(8)]
                   + [("gate", c, D) for c in range(8)])
        q_items = ([("wg", c, D) for c in range(22)]
                   + [("wu", c, D) for c in range(22)]
                   + [("wd", c, FI) for c in range(8)])
        items = []
        for k in range(len(q_items)):
            if k < len(a_items):
                items.append(a_items[k])
            items.append(q_items[k])
        items.append(("bias", 0, 0))
        SRCD = {"psi": wpsiT_d, "phi": wphiT_d, "gate": wgateT_d,
                "wg": wg_d, "wu": wu_d, "wd": wd_d}
        NCH = {"wg": 22, "wu": 22, "wd": 8}
        SCRD = {"wg": (sc_g, 8 * FI), "wu": (sc_u, 8 * FI), "wd": (sc_d, 22 * D)}
        HBASE = {"psi": 0, "phi": 8 * E2}
        QK = ("wg", "wu", "wd")
        vm = {}
        slab_free = {}
        slab_mark = {}
        sign_mark = {}
        evac_mark = {}
        qslot = {0: None, 1: None, 2: None}
        ev_last = [0]

        with nc.Block() as blk:
            def emit_dma(i):
                kind, c, w = items[i]

                def _sy(e, i=i, kind=kind, c=c, w=w):
                    if i >= 3:
                        wge(e, sV, slab_free[i - 3])
                        if (i - 3) in sign_mark:
                            wge(e, sG, sign_mark[i - 3])
                    for jb in (i - 1, i - 2):
                        if jb >= 0 and items[jb][0] == "bias":
                            wge(e, sV, vm[jb])
                    if kind == "bias":
                        wge(e, sV, vm[i - 1])
                        wge(e, sV, vm[i - 2])
                        inc(e.dma_start(out=slab_in[0:1, 0:E2], in_=bpsi_d),
                            sX, 16)
                        inc(e.dma_start(out=slab_in[0:1, E2:2 * E2], in_=bphi_d),
                            sX, 16)
                        inc(e.dma_start(out=slab_in[0:1, 2 * E2:2 * E2 + D],
                                        in_=bgate_d), sX, 16)
                        slab_mark[i] = sX.v
                    else:
                        inc(e.dma_start(
                            out=slab_in[:, (i % 3) * FI:(i % 3) * FI + w],
                            in_=SRCD[kind][128 * c:128 * (c + 1), :]),
                            sFg[i % 3], 16)
                        slab_mark[i] = sFg[i % 3].v
                blk.sync(_sy)

            def emit_sign(i):
                kind, c, w = items[i]

                def _s(e, i=i, w=w):
                    wge(e, sG)
                    if i >= 2:
                        wge(e, sV, vm[i - 2])
                    e.wait_ge(sFg[i % 3].h, slab_mark[i])
                    i_ = e.activation(sSg[:, (i % 2) * FI:(i % 2) * FI + w],
                                      slab_in[:, (i % 3) * FI:(i % 3) * FI + w],
                                      ACT.Sign)
                    inc(i_, sG)
                    sign_mark[i] = sG.v
                blk.scalar(_s)

            def wait_qslot(e, spec):
                if spec is None:
                    return
                kind_, val = spec
                if kind_ == "d":
                    e.wait_ge(val[0].h, val[1])
                else:
                    wge(e, sT, val)

            def emit_consume(i):
                kind, c, w = items[i]
                if kind == "bias":
                    def _vb(e, i=i):
                        e.wait_ge(sX.h, slab_mark[i])
                        e.tensor_copy(bpsi_b[:], slab_in[0:1, 0:E2])
                        e.tensor_copy(bphi_b[:], slab_in[0:1, E2:2 * E2])
                        e.tensor_copy(bgat_b[:], slab_in[0:1, 2 * E2:2 * E2 + D])
                        inc(e.drain(), sV)
                        vm[i] = sV.v
                        slab_free[i] = sV.v
                    blk.vector(_vb)
                    return
                if kind in ("psi", "phi", "gate"):
                    spec = qslot[i % 3] if kind != "gate" else None

                    def _v(e, i=i, kind=kind, c=c, w=w, spec=spec):
                        e.wait_ge(sFg[i % 3].h, slab_mark[i])
                        wait_qslot(e, spec)
                        dst = (wgat_b[:, D * c:D * (c + 1)] if kind == "gate"
                               else qout[:, (i % 3) * FI:(i % 3) * FI + w])
                        e.tensor_copy(dst,
                                      slab_in[:, (i % 3) * FI:(i % 3) * FI + w])
                        inc(e.drain(), sV)
                        vm[i] = sV.v
                        slab_free[i] = sV.v
                    blk.vector(_v)
                    if kind != "gate":
                        osem = (sFd[0], sFd[1], sC)[i % 3]

                        def _sy2(e, i=i, kind=kind, c=c, w=w, osem=osem):
                            wge(e, sV, vm[i])
                            inc(e.dma_start(
                                out=sc_w[:, HBASE[kind] + w * c:
                                         HBASE[kind] + w * (c + 1)],
                                in_=qout[:, (i % 3) * FI:(i % 3) * FI + w]),
                                osem, 16)
                        blk.sync(_sy2)
                        qslot[i % 3] = ("d", (osem, osem.v))
                    return
                n_tb = w // 128
                spec = qslot[i % 3]

                def _v(e, i=i, w=w, spec=spec):
                    e.wait_ge(sFg[i % 3].h, slab_mark[i])
                    wait_qslot(e, spec)
                    sl = slab_in[:, (i % 3) * FI:(i % 3) * FI + w]
                    emit_quant(e, sl, w,
                               qout[:, (i % 3) * FI:(i % 3) * FI + w],
                               sQA[:, 0:w], sQB[:, 0:w], sAZ[:, 0:w],
                               sbm[:, 0:w // 16], sr6[:, 0:w // 16],
                               ssd[:, 0:w // 16],
                               sgn_ap=sSg[:, (i % 2) * FI:(i % 2) * FI + w],
                               az_mark=sign_mark[i], fuse=True, az_sem=sG,
                               az_sep=True, early_free=True)
                    slab_free[i] = sV.v
                    inc(e.drain(), sV)
                    vm[i] = sV.v
                blk.vector(_v)
                n_waves = (n_tb + 7) // 8
                wv_marks = []
                ev_base = ev_last[0]
                for wv in range(n_waves):
                    lo, hi = 8 * wv, min(8 * (wv + 1), n_tb)

                    def _t(e, i=i, lo=lo, hi=hi, wv=wv, ev_base=ev_base):
                        wge(e, sS, ev_base + wv)
                        if wv == 0:
                            wge(e, sV, vm[i])
                        for b in range(lo, hi):
                            i_ = e.transpose(
                                PS_Tv[:, 128 * (b - lo):128 * (b - lo + 1)],
                                qout[:, (i % 3) * FI + 128 * b:
                                     (i % 3) * FI + 128 * (b + 1)], identb[:])
                        inc(i_, sT)
                        wv_marks.append(sT.v)
                    blk.tensor(_t)

                def _s(e, i=i, kind=kind, c=c, n_tb=n_tb, n_waves=n_waves,
                       wv_marks=wv_marks):
                    nch = NCH[kind]
                    for wv in range(n_waves):
                        lo, hi = 8 * wv, min(8 * (wv + 1), n_tb)
                        wge(e, sT, wv_marks[wv])
                        if c == 0 and wv == 0:
                            wge(e, sF)
                        bt = bigT[:].rearrange("p (b cc) -> p b cc",
                                               cc=nch * 128)
                        i_ = e.activation(
                            bt[:, lo:hi, 128 * c:128 * (c + 1)],
                            PS_Tv.rearrange("p (b cc) -> p b cc", cc=128)
                            [:, 0:hi - lo, :], ACT.Copy)
                        inc(i_, sS)
                        ev_last[0] = sS.v
                    evac_mark[i] = sS.v
                blk.scalar(_s)
                qslot[i % 3] = ("t", wv_marks[-1])
                if c == NCH[kind] - 1:
                    scr, scr_cols = SCRD[kind]

                    def _syw(e, i=i, scr=scr, scr_cols=scr_cols):
                        wge(e, sS, evac_mark[i])
                        inc(e.dma_start(out=scr[:, 0:scr_cols],
                                        in_=bigT[:, 0:scr_cols]), sF, 16)
                    blk.sync(_syw)

            nit = len(items)
            for i in range(nit - 1):
                emit_dma(i)
                if items[i][0] in QK:
                    emit_sign(i)
                if i >= 1:
                    emit_consume(i - 1)
            emit_consume(nit - 2)
            emit_dma(nit - 1)
            emit_consume(nit - 1)

            def _sy_wb(e):
                wge(e, sFd[0])
                wge(e, sFd[1])
                wge(e, sC)
                inc(e.dma_start(out=wbuf[:], in_=sc_w[:, 0:8 * E2]), sW, 16)
                psiw_mark[0] = sW.v
            blk.sync(_sy_wb)

    with nc.Block() as blk:
        def _fence(e):
            wge(e, sF)
            wge(e, sX)
            wge(e, sFd[0])
            wge(e, sFd[1])
            wge(e, sC)
            wge(e, sG)
        blk.sync(_fence)

    xbuf = sb("xbuf", [128, 2 * E2], F32)
    xn = sb("xn", [128, E2], F32)
    xnT = sb("xnT", [128, 2 * D], BF16)
    psi = sb("psi", [128, 2 * E2], BF16)
    phi = sb("phi", [128, 2 * E2], BF16)
    gat = sb("gat", [128, 2 * D], BF16)
    scr2k = sb("scr2k", [128, E2], BF16)
    xnbf = scr2k
    qsA = sb("qsA", [128, D], F32)
    qsB = sb("qsB", [128, D], F32)
    hfin = sb("hfin", [128, 2 * E2], BF16)
    bmF = sb("bmF", [128, 64], F32)
    r6F = sb("r6F", [128, 64], F32)
    sdF = sb("sdF", [128, 64], F32)
    xqT = sb("xqT", [128, E2], BF16)
    hsl = sb("hsl", [128, 1024], F32)
    silu_t = sb("silu_t", [128, 1024], F32)
    hq = sb("hq", [128, 2 * FI], BF16)
    hqT = sb("hqT", [128, 2 * FI], BF16)
    gusl = sb("gusl", [128, 3 * 4096], BF16)
    dsl = sb("dsl", [128, 2 * E2], BF16)
    sgnb = sb("sgnb", [128, 1024], BF16)

    ISLW = [512, 512, 512, 512, 512, 256]

    fin_mark = [0]
    out_mark = [0]
    phi_mark = [0]
    phiw_mark = [0]
    xqT_mark = [0]
    PS_Abf = PS_A[:].bitcast(BF16)
    PS_Bbf = PS_B[:].bitcast(BF16)
    silu_bf = silu_t[:].bitcast(BF16)
    hqTf = hqT[:].bitcast(F32)

    def em_sy_head(g):
        xsl = E2 * (g % 2)

        def _sy(e):
            wge(e, sO, out_mark[0])
            for i in range(2):
                r = 128 * (2 * g + i)
                inc(e.dma_start(out=xbuf[:, xsl + 1024 * i:xsl + 1024 * (i + 1)],
                                in_=x_d[r:r + 128, :]), sX, 16)
            if g > 0:
                wge(e, sT, phi_mark[0])
                inc(e.dma_start(out=wbuf[:], in_=sc_w[:, 0:8 * E2]), sW, 16)
                psiw_mark[0] = sW.v
        blk.sync(_sy)

    def em_v_ln(g):
        xsl = E2 * (g % 2)

        def _v(e):
            wge(e, sX)
            x3 = xbuf[:, xsl:xsl + E2].rearrange("p (t c) -> p t c", c=1024)
            e.reduce_sum(stat[:, 0:2], x3, axis=AX)
            inc(e.drain(), sD)
            i = e.tensor_scalar(stat[:, 0:2], stat[:, 0:2], F(1.0 / 1024.0),
                                None, ALU.mult)
            inc(i, sV)
        blk.vector(_v)
        return sV.v

    def em_s_ln(g):
        xsl = E2 * (g % 2)

        def _s(e):
            wge(e, sS)
            wge(e, sX)
            wge(e, sV)
            for ti in range(2):
                e.activation(scr2k[:, 1024 * ti:1024 * (ti + 1)],
                             xbuf[:, xsl + 1024 * ti:xsl + 1024 * (ti + 1)],
                             ACT.Square, accum_out=stat[:, 4 + ti:5 + ti])
            inc(e.drain(), sS)
        blk.scalar(_s)
        return sS.v

    def em_v_ln2(g, sq_mark):
        xsl = E2 * (g % 2)

        def _va(e):
            wge(e, sS, sq_mark)
            wge(e, sV)
            var, sq = stat[:, 6:8], stat[:, 2:4]
            e.tensor_tensor(sq, stat[:, 0:2], stat[:, 0:2], ALU.mult)
            inc(e.drain(), sD)
            e.scalar_tensor_tensor(var, stat[:, 4:6], F(1.0 / 1024.0), sq,
                                   ALU.mult, ALU.subtract)
            inc(e.drain(), sD)
            i = e.tensor_scalar(var, var, F(1e-5), None, ALU.add)
            inc(i, sV)
        blk.vector(_va)

        def _sr(e):
            wge(e, sS)
            wge(e, sV)
            i = e.activation(stat[:, 10:12], stat[:, 6:8], ACT.Sqrt)
            inc(i, sS)
        blk.scalar(_sr)
        rsq_s = sS.v

        def _v(e):
            wge(e, sS, rsq_s)
            e.reciprocal(stat[:, 8:10], stat[:, 10:12])
            inc(e.drain(), sD)
            e.wait_ge(sD.h, sD.v)
            for ti in range(2):
                e.tensor_scalar(xn[:, 1024 * ti:1024 * (ti + 1)],
                                xbuf[:, xsl + 1024 * ti:xsl + 1024 * (ti + 1)],
                                stat[:, ti:ti + 1], stat[:, 8 + ti:9 + ti],
                                ALU.subtract, ALU.mult)
            inc(e.drain(), sD)
            e.wait_ge(sD.h, sD.v)
            e.tensor_copy(xnbf[:, 0:D], xn[:, 0:D])
            e.tensor_copy(xnbf[:, D:2 * D], xn[:, D:2 * D])
            inc(e.drain(), sV)
        blk.vector(_v)
        return sV.v

    def em_t_tr(g, ti, ln2_v, bank_free_s):
        def _t(e):
            e.wait_ge(sV.h, ln2_v)
            e.wait_ge(sS.h, bank_free_s)
            for k in range(8):
                i = e.transpose(PS_Abf[:, 128 * k:128 * (k + 1)],
                                xnbf[:, D * ti + 128 * k:D * ti + 128 * (k + 1)],
                                identb[:])
            inc(i, sT)
        blk.tensor(_t)
        return sT.v

    def em_s_tr(g, ti, t_mark):
        def _s(e):
            wge(e, sT, t_mark)
            i = e.activation(xnT[:, D * ti:D * (ti + 1)], PS_Abf[:, 0:1024],
                             ACT.Copy)
            inc(i, sS)
        blk.scalar(_s)
        return sS.v

    def em_t_psi(g, ti, s_mark):
        def _t(e):
            wge(e, sS, s_mark)
            e.wait_ge(sW.h, psiw_mark[0])
            for k in range(8):
                for j in range(4):
                    e.matmul(PS_A[:, 512 * j:512 * (j + 1)],
                             xnT[:, D * ti + 128 * k:D * ti + 128 * (k + 1)],
                             wbuf[:, E2 * k + 512 * j:E2 * k + 512 * (j + 1)],
                             start=(k == 0), stop=False)
            for j in range(4):
                i = e.matmul(PS_A[:, 512 * j:512 * (j + 1)], ones1b[:],
                             bpsi_b[0:1, 512 * j:512 * (j + 1)],
                             start=False, stop=True)
            inc(i, sT)
        blk.tensor(_t)
        return sT.v

    def em_s_psi(g, ti, t_mark):
        def _s(e):
            wge(e, sT, t_mark)
            i = e.activation(psi[:, E2 * ti:E2 * (ti + 1)], PS_A[:], ACT.Copy)
            inc(i, sS)
        blk.scalar(_s)
        return sS.v

    def em_t_gat(g, ti, s_mark):
        def _t(e):
            wge(e, sS, s_mark)
            for k in range(8):
                for j in range(2):
                    e.matmul(PS_A[:, 512 * j:512 * (j + 1)],
                             xnT[:, D * ti + 128 * k:D * ti + 128 * (k + 1)],
                             wgat_b[:, D * k + 512 * j:D * k + 512 * (j + 1)],
                             start=(k == 0), stop=False)
            for j in range(2):
                i = e.matmul(PS_A[:, 512 * j:512 * (j + 1)], ones1b[:],
                             bgat_b[0:1, 512 * j:512 * (j + 1)],
                             start=False, stop=True)
            inc(i, sT)
        blk.tensor(_t)
        return sT.v

    def em_s_gat(g, ti, t_mark):
        def _s(e):
            wge(e, sT, t_mark)
            i = e.activation(gat[:, D * ti:D * (ti + 1)], PS_A[:, 0:1024],
                             ACT.Sigmoid)
            inc(i, sS)
        blk.scalar(_s)
        return sS.v

    def em_sy_phi(g, psi1_t):
        def _sy(e):
            wge(e, sT, psi1_t)
            inc(e.dma_start(out=wbuf[:], in_=sc_w[:, 8 * E2:16 * E2]), sW, 16)
            phiw_mark[0] = sW.v
        blk.sync(_sy)

    def em_t_phi(g, ti, s_mark):
        def _t(e):
            wge(e, sS, s_mark)
            e.wait_ge(sW.h, phiw_mark[0])
            for k in range(8):
                for j in range(4):
                    e.matmul(PS_A[:, 512 * j:512 * (j + 1)],
                             xnT[:, D * ti + 128 * k:D * ti + 128 * (k + 1)],
                             wbuf[:, E2 * k + 512 * j:E2 * k + 512 * (j + 1)],
                             start=(k == 0), stop=False)
            for j in range(4):
                i = e.matmul(PS_A[:, 512 * j:512 * (j + 1)], ones1b[:],
                             bphi_b[0:1, 512 * j:512 * (j + 1)],
                             start=False, stop=True)
            inc(i, sT)
            if ti == 1:
                phi_mark[0] = sT.v
        blk.tensor(_t)
        return sT.v

    def em_s_phi(g, ti, t_mark):
        def _s(e):
            wge(e, sT, t_mark)
            i = e.activation(phi[:, E2 * ti:E2 * (ti + 1)], PS_A[:], ACT.Copy)
            inc(i, sS)
        blk.scalar(_s)
        return sS.v

    def em_v_a(g, ti, s_mark, t_mark, dst):
        def _v(e):
            wge(e, sS, s_mark)
            wge(e, sT, t_mark)
            pr = psi[:, E2 * ti:E2 * ti + D]
            pim = psi[:, E2 * ti + D:E2 * (ti + 1)]
            g_ = gat[:, D * ti:D * (ti + 1)]
            t1, t2 = xqT[:, 0:D], xqT[:, D:2 * D]
            e.tensor_tensor(t1, pr, wm_re[:], ALU.mult)
            i = e.tensor_tensor(t2, pim, wm_im[:], ALU.mult)
            inc(i, sD)
            e.wait_ge(sD.h, sD.v)
            i = e.tensor_tensor(dst[:, 0:D], t1, t2, ALU.subtract)
            inc(i, sD)
            e.wait_ge(sD.h, sD.v)
            e.tensor_tensor(t1, pr, wm_im[:], ALU.mult)
            i = e.tensor_tensor(t2, pim, wm_re[:], ALU.mult)
            inc(i, sD)
            e.wait_ge(sD.h, sD.v)
            i = e.tensor_tensor(dst[:, D:E2], t1, t2, ALU.add)
            inc(i, sD)
            e.wait_ge(sD.h, sD.v)
            e.tensor_tensor(dst[:, 0:D], dst[:, 0:D], g_, ALU.mult)
            e.tensor_tensor(dst[:, D:E2], dst[:, D:E2], g_, ALU.mult)
            inc(e.drain(), sV)
        blk.vector(_v)
        return sV.v

    def em_t_c(g, ti, v_mark, s_mark, src):
        tt = 2 * g + ti

        def _t(e):
            wge(e, sV, v_mark)
            wge(e, sS, s_mark)
            first = (tt % seq_tiles == 0)
            prev = E2 * ((tt + 1) % 2)
            for j in range(4):
                i = e.matmul(PS_A[:, 512 * j:512 * (j + 1)], ltb[:],
                             src[:, 512 * j:512 * (j + 1)],
                             start=True, stop=first)
            if not first:
                for j in range(4):
                    i = e.matmul(PS_A[:, 512 * j:512 * (j + 1)], e127b[:],
                                 hfin[:, prev + 512 * j:prev + 512 * (j + 1)],
                                 start=False, stop=True)
            inc(i, sT)
        blk.tensor(_t)
        return sT.v

    def em_s_pv(g, ti, t_mark, dst):
        def _s(e):
            wge(e, sT, t_mark)
            i = e.activation(dst[:], PS_A[:], ACT.Copy)
            inc(i, sS)
        blk.scalar(_s)
        return sS.v

    def em_v_p(g, ti, s_mark, src):
        tt = 2 * g + ti

        def _v(e):
            wge(e, sS, s_mark)
            cur = E2 * (tt % 2)
            t1, t2 = xqT[:, 0:D], xqT[:, D:2 * D]
            e.tensor_tensor(t1, src[:, 0:D], wp_re[:], ALU.mult)
            i = e.tensor_tensor(t2, src[:, D:E2], wp_im[:], ALU.mult)
            inc(i, sD)
            e.wait_ge(sD.h, sD.v)
            i = e.tensor_tensor(hfin[:, cur:cur + D], t1, t2, ALU.subtract)
            inc(i, sD)
            e.wait_ge(sD.h, sD.v)
            e.tensor_tensor(t1, src[:, 0:D], wp_im[:], ALU.mult)
            i = e.tensor_tensor(t2, src[:, D:E2], wp_re[:], ALU.mult)
            inc(i, sD)
            e.wait_ge(sD.h, sD.v)
            i = e.tensor_tensor(hfin[:, cur + D:cur + E2], t1, t2, ALU.add)
            inc(e.drain(), sV)
        blk.vector(_v)
        return sV.v

    def em_v_pr(g):
        def _v(e):
            h3 = hfin[:].rearrange("p (q c) -> p q c", c=1024)
            e.reduce_sum(stat[:, 0:4], h3, axis=AX)
            inc(e.drain(), sD)
            i = e.tensor_scalar(stat[:, 0:4], stat[:, 0:4], F(1.0 / 1024.0),
                                None, ALU.mult)
            inc(i, sV)
        blk.vector(_v)
        return sV.v

    def em_s_pr(g, p1_v):
        def _s(e):
            wge(e, sS)
            wge(e, sV, p1_v)
            dumps = [scr2k[:, 0:1024], scr2k[:, 1024:2048],
                     gat[:, 0:1024], gat[:, 1024:2048]]
            for q in range(4):
                i = e.activation(dumps[q], hfin[:, 1024 * q:1024 * (q + 1)],
                                 ACT.Square, accum_out=stat[:, 8 + q:9 + q])
            inc(i, sS)
        blk.scalar(_s)
        return sS.v

    def em_v_pr2(g, s_mark):
        def _v(e):
            wge(e, sS, s_mark)
            wge(e, sV)
            sq = stat[:, 4:8]
            e.tensor_tensor(sq, stat[:, 0:4], stat[:, 0:4], ALU.mult)
            inc(e.drain(), sD)
            i = e.scalar_tensor_tensor(stat[:, 8:12], stat[:, 8:12],
                                       F(1.0 / 1024.0), sq, ALU.mult,
                                       ALU.subtract)
            inc(i, sV)
        blk.vector(_v)
        return sV.v

    def em_s_pr2(g, v_mark):
        def _s(e):
            wge(e, sV, v_mark)
            i = e.sqrt(stat[:, 12:16], stat[:, 8:12])
            inc(i, sS)
        blk.scalar(_s)
        return sS.v

    def em_v_pr3(g, s_mark):
        xsl = E2 * (g % 2)

        def _v(e):
            wge(e, sS, s_mark)
            y, rr = stat[:, 12:16], stat[:, 4:8]
            e.tensor_scalar(y, y, F(1e-6), None, ALU.add)
            inc(e.drain(), sD)
            e.reciprocal(rr, y)
            inc(e.drain(), sD)
            for ti in range(2):
                for p_ in range(2):
                    q = 2 * ti + p_
                    e.tensor_scalar(psi[:, E2 * ti + 1024 * p_:
                                        E2 * ti + 1024 * (p_ + 1)],
                                    hfin[:, 1024 * q:1024 * (q + 1)],
                                    stat[:, q:q + 1], stat[:, 4 + q:5 + q],
                                    ALU.subtract, ALU.mult)
            inc(e.drain(), sD)
            e.wait_ge(sD.h, sD.v)
            for ti in range(2):
                e.tensor_tensor(psi[:, E2 * ti:E2 * ti + D],
                                psi[:, E2 * ti:E2 * ti + D],
                                phi[:, E2 * ti:E2 * ti + D], ALU.mult)
                e.tensor_tensor(psi[:, E2 * ti + D:E2 * (ti + 1)],
                                psi[:, E2 * ti + D:E2 * (ti + 1)],
                                phi[:, E2 * ti + D:E2 * (ti + 1)], ALU.mult)
            inc(e.drain(), sD)
            e.wait_ge(sD.h, sD.v)
            for ti in range(2):
                a_ap = xbuf[:, xsl + 1024 * ti:xsl + 1024 * (ti + 1)]
                e.tensor_tensor(a_ap, a_ap, psi[:, E2 * ti:E2 * ti + D], ALU.add)
                inc(e.drain(), sD)
                e.wait_ge(sD.h, sD.v)
                i = e.tensor_tensor(a_ap, a_ap,
                                    psi[:, E2 * ti + D:E2 * (ti + 1)], ALU.add)
                inc(i, sD)
            inc(e.drain(), sV)
        blk.vector(_v)
        return sV.v

    def em_s_xqp(g, a1_v):
        def _s(e):
            wge(e, sS)
            wge(e, sV, a1_v)
            i = e.activation(gat[:, 0:2 * D], xn[:, 0:2 * D], ACT.Sign)
            inc(i, sS)
        blk.scalar(_s)
        return sS.v

    def em_v_xq(g, ti, sgn_s, free_t):
        def _v(e):
            if free_t is not None:
                wge(e, sT, free_t)
            emit_quant(e, xn[:, 1024 * ti:1024 * (ti + 1)], D,
                       hq[:, 1024 * ti:1024 * (ti + 1)],
                       qsA[:], qsB[:], hsl[:],
                       bmF[:, 0:64], r6F[:, 0:64], sdF[:, 0:64],
                       sgn_ap=gat[:, D * ti:D * (ti + 1)], az_mark=sgn_s,
                       fuse=True, az_sep=True)
            inc(e.drain(), sV)
        blk.vector(_v)
        return sV.v

    def em_t_xqtr(g, v_mark, ps_free_s):
        def _t(e):
            wge(e, sV, v_mark)
            wge(e, sS, ps_free_s)
            for ti in range(2):
                dstp = PS_Abf[:, 1024 * ti:1024 * (ti + 1)]
                for k in range(8):
                    i = e.transpose(dstp[:, 128 * k:128 * (k + 1)],
                                    hq[:, 1024 * ti + 128 * k:
                                       1024 * ti + 128 * (k + 1)], identb[:])
            inc(i, sT)
        blk.tensor(_t)
        return sT.v

    def em_s_xqe(g, t_mark):
        def _s(e):
            wge(e, sT, t_mark)
            i = e.activation(xqT[:, 0:2 * D], PS_Abf[:, 0:2048], ACT.Copy)
            inc(i, sS)
            xqT_mark[0] = sS.v
        blk.scalar(_s)
        return sS.v

    def em_t_gu(g, s, usv, vb_h1, expect):
        w = ISLW[s]
        gj, uj = 2 * s, 2 * s + 1

        def _t(e):
            if s == 0:
                wge(e, sS, xqT_mark[0])
                wge(e, sV, fin_mark[0])
            else:
                e.wait_ge(sV.h, vb_h1[s - 1])
            e.wait_ge(sFg[gj % 3].h, usv[gj])
            e.wait_ge(sFg[uj % 3].h, usv[uj])
            gsl_ap = gusl[:, 4096 * (gj % 3):4096 * (gj % 3) + 8 * w]
            usl_ap = gusl[:, 4096 * (uj % 3):4096 * (uj % 3) + 8 * w]
            for k in range(8):
                for (ti, po) in ((0, 0), (1, 1024)):
                    xt = xqT[:, D * ti + 128 * k:D * ti + 128 * (k + 1)]
                    e.matmul(PS_B[:, po:po + w], xt,
                             gsl_ap[:, w * k:w * (k + 1)],
                             start=(k == 0), stop=(k == 7))
                    i = e.matmul(PS_B[:, po + 512:po + 512 + w], xt,
                                 usl_ap[:, w * k:w * (k + 1)],
                                 start=(k == 0), stop=(k == 7))
            inc(i, sT)
            assert sT.v == expect, (sT.v, expect)
        blk.tensor(_t)
        return sT.v

    def em_sy_gu(g, usv, gu_plan, j0):
        def _sy(e):
            for j in range(j0, 12):
                mat, s = j % 2, j // 2
                w = ISLW[s]
                scr = sc_u if mat else sc_g
                slot = gusl[:, 4096 * (j % 3):4096 * (j % 3) + 8 * w]
                if j >= 3:
                    wge(e, sT, gu_plan[(j - 3) // 2])
                src_ = scr[:].rearrange("p (k c) -> p k c", c=FI)[:, :,
                                                                 512 * s:512 * s + w]
                inc(e.dma_start(out=slot.rearrange("p (k c) -> p k c", c=w),
                                in_=src_), sFg[j % 3], 16)
                assert sFg[j % 3].v == usv[j]
        blk.sync(_sy)

    pre_usv = {}

    def em_sy_gu_pre(nxt, gu_plan_cur):
        def _sy(e):
            for j in range(3):
                mat, s = j % 2, j // 2
                w = ISLW[s]
                scr = sc_u if mat else sc_g
                slot = gusl[:, 4096 * j:4096 * j + 8 * w]
                wge(e, sT, gu_plan_cur[(j + 9) // 2])
                src_ = scr[:].rearrange("p (k c) -> p k c", c=FI)[:, :,
                                                                 512 * s:512 * s + w]
                inc(e.dma_start(out=slot.rearrange("p (k c) -> p k c", c=w),
                                in_=src_), sFg[j], 16)
                pre_usv[j] = sFg[j].v
        blk.sync(_sy)

    def em_s_sig(g, s, gu_t):
        w = ISLW[s]

        def _s(e):
            wge(e, sS)
            wge(e, sT, gu_t)
            ps3 = PS_B[:].rearrange("p (t c) -> p t c", c=1024)[:, :, 0:w]
            sl3 = silu_t[:, 0:2 * w].rearrange("p (t c) -> p t c", c=w)
            i = e.activation(sl3, ps3, ACT.Silu)
            inc(i, sS)
        blk.scalar(_s)
        return sS.v

    def em_v_h1(g, s, sig_s, expect):
        w = ISLW[s]

        def _v(e):
            wge(e, sS, sig_s)
            sl3 = silu_t[:, 0:2 * w].rearrange("p (t c) -> p t c", c=w)
            u3 = PS_B[:].rearrange("p (t c) -> p t c", c=1024)[:, :, 512:512 + w]
            h3 = hsl[:, 0:2 * w].rearrange("p (t c) -> p t c", c=w)
            i = e.tensor_tensor(h3, sl3, u3, ALU.mult)
            inc(i, sV)
            assert sV.v == expect, (sV.v, expect)
        blk.vector(_v)
        return sV.v

    def em_s_sgn(g, s, h1_v):
        w = ISLW[s]

        def _s(e):
            wge(e, sS)
            wge(e, sV, h1_v)
            i = e.activation(sgnb[:, 0:2 * w], hsl[:, 0:2 * w], ACT.Sign)
            inc(i, sS)
        blk.scalar(_s)
        return sS.v

    def em_v_h2(g, s, sgn_s, expect):
        w = ISLW[s]
        off = 512 * s

        def _v(e):
            wge(e, sV)
            nb2 = 2 * w // 16
            hq3 = hq[:].rearrange("p (t c) -> p t c", c=FI)[:, :, off:off + w]
            emit_quant(e, hsl[:, 0:2 * w], 2 * w, hq3,
                       qsA[:, 0:2 * w], qsB[:, 0:2 * w], hqTf[:, 0:2 * w],
                       bmF[:, 0:nb2], r6F[:, 0:nb2], sdF[:, 0:nb2],
                       sgn_ap=sgnb[:, 0:2 * w], az_mark=sgn_s, fuse=True,
                       az_sep=True)
            inc(e.drain(), sV)
            assert sV.v == expect, (sV.v, expect)
        blk.vector(_v)
        return sV.v

    HTR_WAVES = [(ti, 128 * 8 * wv, min(128 * 8 * (wv + 1), FI))
                 for ti in range(2) for wv in range(3)]

    def em_t_htrw(g, q5_v, ebase):
        htr_marks = []
        for widx, (ti, lo, hi) in enumerate(HTR_WAVES):
            def _t(e, ti=ti, lo=lo, hi=hi, widx=widx):
                e.wait_ge(sS.h, ebase + max(0, widx - 1))
                if widx == 0:
                    wge(e, sV, q5_v)
                dstp = PS_Bbf[:, 0:1024] if widx % 2 == 0 else PS_Bbf[:, 1024:2048]
                nb = (hi - lo) // 128
                for b in range(nb):
                    i = e.transpose(dstp[:, 128 * b:128 * (b + 1)],
                                    hq[:, FI * ti + lo + 128 * b:
                                       FI * ti + lo + 128 * (b + 1)], identb[:])
                inc(i, sT)
                htr_marks.append(sT.v)
            blk.tensor(_t)
        return htr_marks

    def em_s_htr(g, htr_marks, ebase):
        def _s(e):
            for widx, (ti, lo, hi) in enumerate(HTR_WAVES):
                wge(e, sT, htr_marks[widx])
                srcp = PS_Bbf[:, 0:1024] if widx % 2 == 0 else PS_Bbf[:, 1024:2048]
                i = e.activation(hqT[:, FI * ti + lo:FI * ti + hi],
                                 srcp[:, 0:hi - lo], ACT.Copy)
                inc(i, sS)
                assert sS.v == ebase + widx + 1, (sS.v, ebase, widx)
        blk.scalar(_s)
        return sS.v

    def em_t_dn(g, s_mark, v_mark):
        dpair_freed = {}
        dpair_sem_val = {}
        for kp in range(11):
            dpair_sem_val[kp] = sFd[kp % 2].v + 16 * (kp // 2 + 1)

        def _t(e):
            wge(e, sS, s_mark)
            wge(e, sV, v_mark)
            for kp in range(11):
                e.wait_ge(sFd[kp % 2].h, dpair_sem_val[kp])
                for half in range(2):
                    kk = 2 * kp + half
                    for ti in range(2):
                        st_ = hqT[:, FI * ti + 128 * kk:FI * ti + 128 * (kk + 1)]
                        for j in range(2):
                            i = e.matmul(
                                PS_B[:, 1024 * ti + 512 * j:1024 * ti + 512 * (j + 1)],
                                st_,
                                dsl[:, E2 * (kp % 2) + D * half + 512 * j:
                                    E2 * (kp % 2) + D * half + 512 * (j + 1)],
                                start=(kk == 0), stop=(kk == 21))
                inc(i, sT)
                dpair_freed[kp] = sT.v
        blk.tensor(_t)

        def _sy(e):
            for kp in range(11):
                if kp >= 2:
                    wge(e, sT, dpair_freed[kp - 2])
                inc(e.dma_start(out=dsl[:, E2 * (kp % 2):E2 * (kp % 2) + E2],
                                in_=sc_d[:, E2 * kp:E2 * (kp + 1)]),
                    sFd[kp % 2], 16)
                assert sFd[kp % 2].v == dpair_sem_val[kp]
        blk.sync(_sy)
        return sT.v

    def em_v_fin(g, dn_t):
        xsl = E2 * (g % 2)

        def _v(e):
            wge(e, sT, dn_t)
            for ti in range(2):
                a_ap = xbuf[:, xsl + 1024 * ti:xsl + 1024 * (ti + 1)]
                e.tensor_tensor(a_ap, a_ap, PS_B[:, 1024 * ti:1024 * (ti + 1)],
                                ALU.add)
                inc(e.drain(), sD)
            inc(e.drain(), sV)
            fin_mark[0] = sV.v
        blk.vector(_v)
        return sV.v

    def em_sy_out(g):
        xsl = E2 * (g % 2)

        def _sy(e):
            wge(e, sV, fin_mark[0])
            for ti in range(2):
                r = 128 * (2 * g + ti)
                inc(e.dma_start(out=out_d[r:r + 128, :],
                                in_=xbuf[:, xsl + 1024 * ti:xsl + 1024 * (ti + 1)]),
                    sO, 16)
            out_mark[0] = sO.v
        blk.sync(_sy)

    def emit_attn0():
        em_sy_head(0)
        em_v_ln(0)
        m = em_s_ln(0)
        ln2 = em_v_ln2(0, m)
        m = em_t_tr(0, 0, ln2, xqT_mark[0])
        m = em_s_tr(0, 0, m)
        m = em_t_psi(0, 0, m)
        psi1 = None
        m = em_s_psi(0, 0, m)
        m = em_t_gat(0, 0, m)
        m = em_s_gat(0, 0, m)
        tr1m = em_t_tr(0, 1, ln2, m)
        m = em_s_tr(0, 1, tr1m)
        psi1 = em_t_psi(0, 1, m)
        em_sy_phi(0, psi1)
        m = em_s_psi(0, 1, psi1)
        m = em_t_gat(0, 1, m)
        sg1 = em_s_gat(0, 1, m)
        m = em_t_phi(0, 0, sg1)
        m = em_s_phi(0, 0, m)
        m = em_t_phi(0, 1, m)
        sphi1 = em_s_phi(0, 1, m)
        a0 = em_v_a(0, 0, sphi1, tr1m, scr2k)
        c0 = em_t_c(0, 0, a0, sphi1, scr2k)
        a1 = em_v_a(0, 1, sphi1, tr1m, silu_bf)
        pv0 = em_s_pv(0, 0, c0, scr2k)
        p0 = em_v_p(0, 0, pv0, scr2k)
        c1 = em_t_c(0, 1, p0, pv0, silu_bf)
        pv1 = em_s_pv(0, 1, c1, silu_bf)
        p1 = em_v_p(0, 1, pv1, silu_bf)
        em_v_pr(0)
        m = em_s_pr(0, p1)
        m = em_v_pr2(0, m)
        m = em_s_pr2(0, m)
        em_v_pr3(0, m)
        sgn_s = em_s_xqp(0, a1)
        em_v_xq(0, 0, sgn_s, None)
        xqv = em_v_xq(0, 1, sgn_s, None)
        m = em_t_xqtr(0, xqv, pv1)
        em_s_xqe(0, m)

    def emit_period(g):
        nxt = g + 1
        has_next = nxt < ng
        tbase = sT.v
        vbase = sV.v
        if has_next:
            gu_plan = {s: tbase + [1, 2, 5, 7, 9, 12][s] for s in range(6)}
            vb_h1 = {s: vbase + [1, 6, 8, 10, 12, 14][s] for s in range(6)}
            vb_q = {s: vbase + [2, 7, 9, 11, 13, 15][s] for s in range(6)}
            vb_a1 = vbase + 17
            vb_p1 = vbase + 21
        else:
            gu_plan = {s: tbase + s + 1 for s in range(6)}
            vb_h1 = {s: vbase + 2 * s + 1 for s in range(6)}
            vb_q = {s: vbase + 2 * s + 2 for s in range(6)}
        if g == 0:
            usv = {j: sFg[j % 3].v + 16 * (j // 3 + 1) for j in range(12)}
            j0 = 0
        else:
            usv = dict(pre_usv)
            for j in range(3, 12):
                usv[j] = sFg[j % 3].v + 16 * ((j - 3) // 3 + 1)
            j0 = 3

        if has_next:
            em_sy_head(nxt)
        em_sy_gu(g, usv, gu_plan, j0)
        if has_next:
            em_sy_phi(nxt, tbase + 10)

        gu0 = em_t_gu(g, 0, usv, vb_h1, gu_plan[0])
        sig0 = em_s_sig(g, 0, gu0)
        h1v = em_v_h1(g, 0, sig0, vb_h1[0])
        sgn0 = em_s_sgn(g, 0, h1v)
        if has_next:
            mq = em_s_ln(nxt)
        em_v_h2(g, 0, sgn0, vb_q[0])

        gu1 = em_t_gu(g, 1, usv, vb_h1, gu_plan[1])
        sig1 = em_s_sig(g, 1, gu1)
        if has_next:
            em_v_ln(nxt)
            ln2 = em_v_ln2(nxt, mq)
            trm = em_t_tr(nxt, 0, ln2, xqT_mark[0])
            str0 = em_s_tr(nxt, 0, trm)
        h1v = em_v_h1(g, 1, sig1, vb_h1[1])
        sgn1 = em_s_sgn(g, 1, h1v)
        em_v_h2(g, 1, sgn1, vb_q[1])
        if has_next:
            tpsi0 = em_t_psi(nxt, 0, str0)

        gu2 = em_t_gu(g, 2, usv, vb_h1, gu_plan[2])
        sig2 = em_s_sig(g, 2, gu2)
        h1v = em_v_h1(g, 2, sig2, vb_h1[2])
        if has_next:
            spsi0 = em_s_psi(nxt, 0, tpsi0)
        sgn2 = em_s_sgn(g, 2, h1v)
        em_v_h2(g, 2, sgn2, vb_q[2])
        if has_next:
            tgat0 = em_t_gat(nxt, 0, spsi0)

        gu3 = em_t_gu(g, 3, usv, vb_h1, gu_plan[3])
        sig3 = em_s_sig(g, 3, gu3)
        h1v = em_v_h1(g, 3, sig3, vb_h1[3])
        if has_next:
            sgat0 = em_s_gat(nxt, 0, tgat0)
            trm1 = em_t_tr(nxt, 1, ln2, sgat0)
            str1 = em_s_tr(nxt, 1, trm1)
        sgn3 = em_s_sgn(g, 3, h1v)
        em_v_h2(g, 3, sgn3, vb_q[3])

        gu4 = em_t_gu(g, 4, usv, vb_h1, gu_plan[4])
        sig4 = em_s_sig(g, 4, gu4)
        h1v = em_v_h1(g, 4, sig4, vb_h1[4])
        if has_next:
            tpsi1 = em_t_psi(nxt, 1, str1)
            assert tpsi1 == tbase + 10, (tpsi1, tbase)
            spsi1 = em_s_psi(nxt, 1, tpsi1)
        sgn4 = em_s_sgn(g, 4, h1v)
        em_v_h2(g, 4, sgn4, vb_q[4])
        if has_next:
            tgat1 = em_t_gat(nxt, 1, spsi1)
            sgat1 = em_s_gat(nxt, 1, tgat1)

        gu5 = em_t_gu(g, 5, usv, vb_h1, gu_plan[5])
        sig5 = em_s_sig(g, 5, gu5)
        h1v = em_v_h1(g, 5, sig5, vb_h1[5])
        sgn5 = em_s_sgn(g, 5, h1v)
        q5 = em_v_h2(g, 5, sgn5, vb_q[5])
        if has_next:
            tphi0 = em_t_phi(nxt, 0, sgat1)
            sphi0 = em_s_phi(nxt, 0, tphi0)
            tphi1 = em_t_phi(nxt, 1, sphi0)
            sphi1 = em_s_phi(nxt, 1, tphi1)

        if has_next:
            em_sy_gu_pre(nxt, gu_plan)
        ebase = sS.v
        htr_marks = em_t_htrw(g, q5, ebase)
        shtr = em_s_htr(g, htr_marks, ebase)
        if has_next:
            sgn_s = em_s_xqp(nxt, vb_a1)
            a0 = em_v_a(nxt, 0, sgat1, trm1, scr2k)
            assert a0 == vbase + 16
            c0 = em_t_c(nxt, 0, a0, sphi1, scr2k)
            a1 = em_v_a(nxt, 1, sgat1, trm1, silu_bf)
            assert a1 == vb_a1
            em_v_xq(nxt, 0, sgn_s, htr_marks[-1])
            pv0 = em_s_pv(nxt, 0, c0, scr2k)
            dn_t = em_t_dn(g, shtr, vb_h1[5])
            p0 = em_v_p(nxt, 0, pv0, scr2k)
            c1 = em_t_c(nxt, 1, p0, pv0, silu_bf)
            xqv = em_v_xq(nxt, 1, sgn_s, None)
            pv1 = em_s_pv(nxt, 1, c1, silu_bf)
            p1 = em_v_p(nxt, 1, pv1, silu_bf)
            assert p1 == vb_p1
            em_v_pr(nxt)
            m = em_s_pr(nxt, vb_p1)
            m2 = em_v_pr2(nxt, m)
            m = em_s_pr2(nxt, m2)
            fv = em_v_fin(g, dn_t)
            assert fv == vbase + 24
            em_sy_out(g)
            em_v_pr3(nxt, m)
            mt = em_t_xqtr(nxt, vb_p1, pv1)
            em_s_xqe(nxt, mt)
        else:
            dn_t = em_t_dn(g, shtr, vb_h1[5])
            fv = em_v_fin(g, dn_t)
            assert fv == vbase + 13
            em_sy_out(g)

    with nc.Block() as blk:
        emit_attn0()
        for g in range(ng):
            emit_period(g)

        def _sy_end(e):
            wge(e, sO)
        blk.sync(_sy_end)

    ctx.close()
    return nc


def _shard_inputs(inputs, tpc=4096):
    x = np.asarray(inputs["L0_attn_x"])
    B = x.shape[0]
    per = tpc // x.shape[1]
    wpsiT = np.ascontiguousarray(np.asarray(inputs["L0_attn_W_psi"]).T)
    wphiT = np.ascontiguousarray(np.asarray(inputs["L0_attn_W_phi"]).T)
    wgateT = np.ascontiguousarray(np.asarray(inputs["L0_attn_W_gate"]).T)
    shared = {
        "wpsiT": wpsiT, "wphiT": wphiT, "wgateT": wgateT,
        "bpsi": np.ascontiguousarray(np.asarray(inputs["L0_attn_b_psi"]).reshape(1, -1)),
        "bphi": np.ascontiguousarray(np.asarray(inputs["L0_attn_b_phi"]).reshape(1, -1)),
        "bgate": np.ascontiguousarray(np.asarray(inputs["L0_attn_b_gate"]).reshape(1, -1)),
        "omega": np.ascontiguousarray(np.asarray(inputs["L0_attn_omega"]).reshape(1, -1)),
        "lgam": np.ascontiguousarray(np.asarray(inputs["L0_attn_log_gamma"]).reshape(1, -1)),
        "dt": np.ascontiguousarray(np.asarray(inputs["L0_attn_dt"]).reshape(1, 1)),
        "wg": np.ascontiguousarray(np.asarray(inputs["L0_ffn_w_gate"])),
        "wu": np.ascontiguousarray(np.asarray(inputs["L0_ffn_w_up"])),
        "wd": np.ascontiguousarray(np.asarray(inputs["L0_ffn_w_down"])),
    }
    in_maps = []
    ncores = B // per
    for c in range(ncores):
        m = dict(shared)
        m["x"] = np.ascontiguousarray(
            x[c * per:(c + 1) * per].reshape(tpc, D))
        in_maps.append(m)
    return in_maps, ncores


def kernel(**inputs):
    x = np.asarray(inputs["L0_attn_x"])
    B, L, _ = x.shape
    tpc = (B * L) // 8
    in_maps, ncores = _shard_inputs(inputs, tpc)
    nc = build_program(tpc=tpc, seq_tiles=L // 128)
    res = run_bass_kernel_spmd(nc, in_maps, core_ids=list(range(ncores)))
    outs = [np.asarray(r["out"]).reshape(-1, L, D) for r in res.results]
    return np.concatenate(outs, axis=0).astype(np.float32)


# version 32
# speedup vs baseline: 1.0221x; 1.0003x over previous
import contextlib
import numpy as np
import ml_dtypes
import concourse.bass as bass
import concourse.mybir as mybir
from concourse.bass_utils import run_bass_kernel_spmd

F32 = mybir.dt.float32
BF16 = mybir.dt.bfloat16
AX = mybir.AxisListType.X
ALU = mybir.AluOpType
ACT = mybir.ActivationFunctionType

D = 1024
E2 = 2048
FI = 2816
def F(v):
    return float(np.float32(v))


AMAX = F(0.65625)
C1_2PI = F(6.2831855)
C2_2PI = F(2 * np.pi - 6.283185482025146484375)
PI = F(np.pi)
PI3 = F(3 * np.pi)
HALF_PI = F(np.pi / 2)

_QOPS = {}


def _ensure_qops():
    if _QOPS:
        return _QOPS
    from concourse.dve_spec import (Spec, Src0, Src1, C0, C1, C2, Zero, One,
                                    lower, AluOp, select, Bin)
    from concourse.dve_op import DveOp
    import concourse.dve_ops as dmod

    replaceable = ["BITWISE_XOR_TT_BV", "SHIFT_RIGHT_TT_BV"]

    def mk(name, spec, ref):
        for o in dmod.OPS:
            if o.name == name:
                return o
        shas = {}
        for ver in ("v3", "v4"):
            try:
                shas[ver] = lower(spec, ver=ver, perf_en=False, name=name).sha(ver)
            except Exception:
                pass
        op = DveOp(name, spec, subdim=False, uops_sha=shas, reference=ref)
        if dmod._CUSTOM_DVE_ROW_BASE + len(dmod.OPS) <= 31:
            dmod.OPS.append(op)
            row = dmod._CUSTOM_DVE_ROW_BASE + len(dmod.OPS) - 1
        else:
            victim = replaceable.pop()
            idx = next(i for i, o in enumerate(dmod.OPS) if o.name == victim)
            dmod.OPS[idx] = op
            row = dmod._CUSTOM_DVE_ROW_BASE + idx
            dmod._SUB_OPCODE_FOR_NAME.pop(victim, None)
        dmod._SUB_OPCODE_FOR_NAME[name] = row
        dmod.CUSTOM_DVE_SPECS[name] = op.spec
        dmod.CUSTOM_DVE_REFS[name] = op.ref
        return op

    prod = Src0 * Src1
    _QOPS["u"] = mk("OPQ_U_ANT", Spec(body=Bin(AluOp.ABSOLUTE_VALUE, prod, prod)),
                    lambda in0, in1, s0, s1, imm2: np.abs(in0 * in1))
    u = Src0
    um1 = u - One
    low = ((u > C0) + (u > C1)) + ((um1 > C0) + (um1 > C1))

    def _ref_low(in0, in1, s0, s1, imm2):
        dt_ = in0.dtype
        um = in0 - dt_.type(1.0)
        return ((in0 > s0).astype(dt_) + (in0 > s1).astype(dt_)
                + (um > s0).astype(dt_) + (um > s1).astype(dt_))
    _QOPS["low"] = mk("OPQ_LOW_ANT", Spec(body=low), _ref_low)
    ch1 = u > C0
    ch2 = u > C1
    ch3 = u > C2
    chc = (ch1 + ch2) + (ch3 + ch3)
    high = Src1 + (chc + chc)

    def _ref_high(in0, in1, s0, s1, imm2):
        dt_ = in0.dtype
        c = ((in0 > s0).astype(dt_) + (in0 > s1).astype(dt_)
             + dt_.type(2.0) * (in0 > imm2).astype(dt_))
        return in1 + dt_.type(2.0) * c
    _QOPS["high"] = mk("OPQ_HIGH_ANT", Spec(body=high), _ref_high)
    sgn = select(Src0 < Zero, Zero - Src1, Src1)
    _QOPS["sign"] = mk("OPQ_SIGN_ANT", Spec(body=sgn),
                       lambda in0, in1, s0, s1, imm2: np.where(in0 < 0, -in1, in1))
    return _QOPS


def build_program(tpc=4096, seq_tiles=16):
    nt = tpc // 128
    ng = nt // 2
    assert nt % 2 == 0 and seq_tiles % 2 == 0

    nc = bass.Bass("TRN2", target_bir_lowering=False)

    x_d = nc.dram_tensor("x", [tpc, D], F32, kind="ExternalInput").ap()
    wpsiT_d = nc.dram_tensor("wpsiT", [D, E2], F32, kind="ExternalInput").ap()
    wphiT_d = nc.dram_tensor("wphiT", [D, E2], F32, kind="ExternalInput").ap()
    wgateT_d = nc.dram_tensor("wgateT", [D, D], F32, kind="ExternalInput").ap()
    bpsi_d = nc.dram_tensor("bpsi", [1, E2], F32, kind="ExternalInput").ap()
    bphi_d = nc.dram_tensor("bphi", [1, E2], F32, kind="ExternalInput").ap()
    bgate_d = nc.dram_tensor("bgate", [1, D], F32, kind="ExternalInput").ap()
    omega_d = nc.dram_tensor("omega", [1, D], F32, kind="ExternalInput").ap()
    lgam_d = nc.dram_tensor("lgam", [1, D], F32, kind="ExternalInput").ap()
    dt_d = nc.dram_tensor("dt", [1, 1], F32, kind="ExternalInput").ap()
    wg_d = nc.dram_tensor("wg", [FI, D], F32, kind="ExternalInput").ap()
    wu_d = nc.dram_tensor("wu", [FI, D], F32, kind="ExternalInput").ap()
    wd_d = nc.dram_tensor("wd", [D, FI], F32, kind="ExternalInput").ap()
    out_d = nc.dram_tensor("out", [tpc, D], F32, kind="ExternalOutput").ap()
    sc_g = nc.dram_tensor("sc_g", [128, 8 * FI], BF16, kind="Internal").ap()
    sc_u = nc.dram_tensor("sc_u", [128, 8 * FI], BF16, kind="Internal").ap()
    sc_d = nc.dram_tensor("sc_d", [128, 22 * D], BF16, kind="Internal").ap()
    sc_w = nc.dram_tensor("sc_w", [128, 16 * E2], BF16, kind="Internal").ap()

    identN = np.eye(128, dtype=ml_dtypes.bfloat16)
    ltN = np.triu(np.ones((128, 128), dtype=np.float32)).astype(ml_dtypes.bfloat16)
    e127N = np.zeros((128, 128), dtype=np.float32)
    e127N[127, :] = 1.0
    e127N = e127N.astype(ml_dtypes.bfloat16)
    ident_d = nc.inline_tensor(identN, "identc").ap()
    lt_d = nc.inline_tensor(ltN, "ltc").ap()
    e127_d = nc.inline_tensor(e127N, "e127c").ap()
    ones1b_d = nc.inline_tensor(np.ones((1, 128), dtype=ml_dtypes.bfloat16), "o1b").ap()
    ones1f_d = nc.inline_tensor(np.ones((1, 128), dtype=np.float32), "o1f").ap()
    tp1_d = nc.inline_tensor(np.arange(1, 129, dtype=np.float32).reshape(128, 1),
                             "tp1").ap()
    tn1_d = nc.inline_tensor(-np.arange(1, 129, dtype=np.float32).reshape(128, 1),
                             "tn1").ap()

    ctx = contextlib.ExitStack()

    def sb(name, shape, dt):
        return ctx.enter_context(nc.sbuf_tensor(name, shape, dt))

    class Sem:
        def __init__(self, name):
            self.h = ctx.enter_context(nc.semaphore(name))
            self.v = 0

    sT, sV, sS = Sem("sT"), Sem("sV"), Sem("sS")
    sX, sF, sO, sG = Sem("sX"), Sem("sF"), Sem("sO"), Sem("sG")
    sW = Sem("sW")
    sC = Sem("sC")
    sFg = [Sem("sFg0"), Sem("sFg1"), Sem("sFg2")]
    sFd = [Sem("sFd0"), Sem("sFd1")]
    sD = Sem("sD")

    def inc(instr, s, n=1):
        instr.then_inc(s.h, n)
        s.v += n

    def wge(eng, s, v=None):
        eng.wait_ge(s.h, s.v if v is None else v)

    wgat_b = sb("wgat_b", [128, 8 * D], BF16)
    bpsi_b = sb("bpsi_b", [1, E2], BF16)
    bphi_b = sb("bphi_b", [1, E2], BF16)
    bgat_b = sb("bgat_b", [1, D], BF16)
    identb = sb("identb", [128, 128], BF16)
    ltb = sb("ltb", [128, 128], BF16)
    e127b = sb("e127b", [128, 128], BF16)
    ones1b = sb("ones1b", [1, 128], BF16)
    ones1f = sb("ones1f", [1, 128], F32)
    tp1c = sb("tp1c", [128, 1], F32)
    tn1c = sb("tn1c", [128, 1], F32)
    wm_re = sb("wm_re", [128, D], BF16)
    wm_im = sb("wm_im", [128, D], BF16)
    wp_re = sb("wp_re", [128, D], BF16)
    wp_im = sb("wp_im", [128, D], BF16)
    stat = sb("stat", [128, 16], F32)
    wbuf = sb("wbuf", [128, 8 * E2], BF16)

    PS_A = ctx.enter_context(nc.psum_tensor("PS_A", [128, 2048], F32))
    PS_B = ctx.enter_context(nc.psum_tensor("PS_B", [128, 2048], F32))
    PS_Tv = PS_B[:].bitcast(BF16)

    with nc.Block() as blk:
        def _sy(e):
            for dst, src in [(identb, ident_d), (ltb, lt_d), (e127b, e127_d),
                             (ones1b, ones1b_d), (ones1f, ones1f_d),
                             (tp1c, tp1_d), (tn1c, tn1_d)]:
                inc(e.dma_start(out=dst[:], in_=src), sX, 16)
        blk.sync(_sy)



    with contextlib.ExitStack() as tctx:
        tsb = lambda n, s, d: tctx.enter_context(nc.sbuf_tensor(n, s, d))
        tbl_sc = tsb("tbl_sc", [128, 2 * D], F32)
        tbl_e = tsb("tbl_e", [128, D], F32)
        tbl_p = tsb("tbl_p", [128, D], F32)
        tbl_m = tsb("tbl_m", [128, D], F32)
        tbl_s = tsb("tbl_s", [128, D], F32)
        tbl_c = tsb("tbl_c", [128, D], F32)
        tbl_k = tsb("tbl_k", [128, D], F32)
        rowa = tsb("rowa", [1, D], F32)
        roww = tsb("roww", [1, D], F32)
        dtab = tsb("dtab", [1, 1], F32)
        lgam_r = tsb("lgam_r", [1, D], F32)
        omega_r = tsb("omega_r", [1, D], F32)
        dt_r = tsb("dt_r", [1, 1], F32)

        with nc.Block() as blk:
            def _sy0(e):
                for dst, src_ in [(lgam_r, lgam_d), (omega_r, omega_d),
                                  (dt_r, dt_d)]:
                    inc(e.dma_start(out=dst[:], in_=src_), sX, 16)
            blk.sync(_sy0)

            def _sc(e):
                wge(e, sX)
                e.activation(dtab[:], dt_r[:], ACT.Abs)
                i = e.activation(rowa[:], lgam_r[:], ACT.Exp)
                inc(i, sS)
            blk.scalar(_sc)

            def _v(e):
                wge(e, sS)
                e.tensor_scalar(rowa[:], rowa[:], dtab[0:1, 0:1], AMAX,
                                ALU.mult, ALU.min)
                e.tensor_scalar(roww[:], omega_r[:], dtab[0:1, 0:1], None, ALU.mult)
                inc(e.drain(), sV)
            blk.vector(_v)

            def _t(e):
                wge(e, sV)
                for j in range(2):
                    e.matmul(PS_A[:, 512 * j:512 * (j + 1)], ones1f[:],
                             rowa[0:1, 512 * j:512 * (j + 1)], start=True, stop=True)
                for j in range(2):
                    i = e.matmul(PS_A[:, 1024 + 512 * j:1536 + 512 * j], ones1f[:],
                                 roww[0:1, 512 * j:512 * (j + 1)], start=True,
                                 stop=True)
                inc(i, sT)
            blk.tensor(_t)

            def _v2(e):
                wge(e, sT)
                e.tensor_copy(tbl_sc[:], PS_A[:])
                inc(e.drain(), sV)
            blk.vector(_v2)

        a_bc = tbl_sc[:, 0:D]
        w_bc = tbl_sc[:, D:2 * D]

        def emit_table(sgn, ecol, o_re, o_im):
            with nc.Block() as blk:
                def _v(e):
                    wge(e, sV)
                    e.tensor_scalar(tbl_e[:], a_bc, ecol[:, 0:1], None, ALU.mult)
                    e.tensor_scalar(tbl_p[:], w_bc, tp1c[:, 0:1], None, ALU.mult)
                    e.drain()
                    e.tensor_scalar(tbl_k[:], tbl_p[:], PI, None, ALU.is_gt)
                    e.drain()
                    e.scalar_tensor_tensor(tbl_k[:], tbl_p[:], PI3, tbl_k[:],
                                           ALU.is_gt, ALU.add)
                    e.drain()
                    e.scalar_tensor_tensor(tbl_p[:], tbl_k[:], F(-C1_2PI), tbl_p[:],
                                           ALU.mult, ALU.add)
                    e.drain()
                    e.scalar_tensor_tensor(tbl_p[:], tbl_k[:], F(-C2_2PI), tbl_p[:],
                                           ALU.mult, ALU.add)
                    inc(e.drain(), sV)
                blk.vector(_v)

                def _s(e):
                    wge(e, sV)
                    e.activation(tbl_m[:], tbl_e[:], ACT.Exp)
                    i = e.activation(tbl_s[:], tbl_p[:], ACT.Sin)
                    inc(i, sS)
                blk.scalar(_s)

                def _v2(e):
                    wge(e, sS)
                    e.tensor_scalar(tbl_c[:], tbl_p[:], F(-1.0), HALF_PI,
                                    ALU.mult, ALU.add)
                    e.drain()
                    e.tensor_scalar(tbl_k[:], tbl_c[:], PI, None, ALU.is_gt)
                    e.drain()
                    e.scalar_tensor_tensor(tbl_c[:], tbl_k[:], F(-C1_2PI), tbl_c[:],
                                           ALU.mult, ALU.add)
                    e.drain()
                    e.scalar_tensor_tensor(tbl_c[:], tbl_k[:], F(-C2_2PI), tbl_c[:],
                                           ALU.mult, ALU.add)
                    inc(e.drain(), sV)
                blk.vector(_v2)

                def _s2(e):
                    wge(e, sV)
                    i = e.activation(tbl_c[:], tbl_c[:], ACT.Sin)
                    inc(i, sS)
                blk.scalar(_s2)

                def _v3(e):
                    wge(e, sS)
                    e.tensor_tensor(o_re[:], tbl_m[:], tbl_c[:], ALU.mult)
                    e.scalar_tensor_tensor(o_im[:], tbl_m[:], F(sgn), tbl_s[:],
                                           ALU.mult, ALU.mult)
                    inc(e.drain(), sV)
                blk.vector(_v3)

        emit_table(-1.0, tp1c, wm_re, wm_im)
        emit_table(+1.0, tn1c, wp_re, wp_im)

    def bc16(ap2, nblk):
        return ap2.unsqueeze(2).broadcast_to([128, nblk, 16])

    def emit_quant(ve, z_ap, n, out_bf, QA, QB, AZ, bm, r6, sdq,
                   sgn_ap=None, az_mark=None, fuse=False, az_sem=None,
                   az_sep=False, early_free=False):
        nb = n // 16
        zv = z_ap.rearrange("p (b k) -> p b k", k=16)
        az3 = AZ.rearrange("p (b k) -> p b k", k=16)

        def E(i):
            inc(i, sD)
            return sD.v

        def W(m):
            ve.wait_ge(sD.h, m)

        if az_sem is None:
            az_sem = sS
        if sgn_ap is not None and not fuse:
            ve.wait_ge(az_sem.h, az_mark)
        m_red = E(ve.reduce_max(bm, zv, axis=AX, apply_absolute_value=True))
        if fuse:
            W(m_red)
            if not az_sep:
                ve.wait_ge(az_sem.h, az_mark)
            i_az = ve.scalar_tensor_tensor(AZ, z_ap, F(-1.0), z_ap,
                                           ALU.mult, ALU.max)
            if early_free:
                inc(i_az, sV)
                m_az = ("V", sV.v)
            else:
                inc(i_az, sD)
                m_az = ("D", sD.v)
        if sgn_ap is None:
            m_az = E(ve.scalar_tensor_tensor(AZ, z_ap, F(-1.0), z_ap, ALU.mult,
                                             ALU.max))
        W(m_red)
        E(ve.reciprocal(r6, bm))
        W(sD.v)
        m_r6 = E(ve.tensor_scalar(r6, r6, F(6.0), F(1e12), ALU.mult, ALU.min))
        W(m_r6)
        m_sd = E(ve.reciprocal(sdq, r6))
        W(m_sd)
        m_sdh = E(ve.tensor_scalar(sdq, sdq, F(0.5), None, ALU.mult))
        if sgn_ap is None:
            W(m_az)
        if fuse:
            if m_az[0] == "V":
                ve.wait_ge(sV.h, m_az[1])
            else:
                W(m_az[1])
        m_u = E(ve.tensor_tensor(az3, az3, bc16(r6, nb), ALU.mult))
        W(m_u)
        a = E(ve.tensor_scalar(QA, AZ, F(0.25), None, ALU.is_gt))
        c = E(ve.tensor_scalar(QB, AZ, F(2.5), None, ALU.is_gt))
        for thr_a, thr_c in ((0.75, 3.5), (1.25, 5.0), (1.75, 5.0)):
            W(a)
            a = E(ve.scalar_tensor_tensor(QA, AZ, F(thr_a), QA, ALU.is_gt,
                                          ALU.add))
            W(c)
            c = E(ve.scalar_tensor_tensor(QB, AZ, F(thr_c), QB, ALU.is_gt,
                                          ALU.add))
        W(a)
        W(c)
        q2 = E(ve.scalar_tensor_tensor(QB, QB, F(2.0), QA, ALU.mult, ALU.add))
        W(q2)
        if sgn_ap is None:
            s1 = E(ve.scalar_tensor_tensor(QA, z_ap, F(0.0), QB, ALU.is_gt,
                                           ALU.mult))
            s2 = E(ve.scalar_tensor_tensor(AZ, z_ap, F(0.0), QB, ALU.is_lt,
                                           ALU.mult))
            W(s1)
            W(s2)
            qs = E(ve.tensor_tensor(QA, QA, AZ, ALU.subtract))
            W(qs)
            W(m_sdh)
            E(ve.tensor_tensor(out_bf, QA.rearrange("p (b k) -> p b k", k=16),
                               bc16(sdq, nb), ALU.mult))
        else:
            W(m_sdh)
            t = E(ve.tensor_tensor(out_bf,
                                   QB.rearrange("p (b k) -> p b k", k=16),
                                   bc16(sdq, nb), ALU.mult))
            W(t)
            if fuse and az_sep:
                ve.wait_ge(az_sem.h, az_mark)
            E(ve.tensor_tensor(out_bf, out_bf, sgn_ap, ALU.mult))

    psiw_mark = [0]
    with contextlib.ExitStack() as qctx:
        qsb = lambda n, s, d: qctx.enter_context(nc.sbuf_tensor(n, s, d))
        slab_in = qsb("slab_in", [128, 3 * FI], F32)
        sAZ = qsb("sAZ", [128, FI], F32)
        sQA = qsb("sQA", [128, FI], F32)
        sQB = qsb("sQB", [128, FI], F32)
        sbm = qsb("sbm", [128, FI // 16], F32)
        sr6 = qsb("sr6", [128, FI // 16], F32)
        ssd = qsb("ssd", [128, FI // 16], F32)
        qout = qsb("qout", [128, 3 * FI], BF16)
        bigT = qsb("bigT", [128, 8 * FI], BF16)

        sSg = qsb("sSg", [128, 2 * FI], BF16)
        a_items = ([("psi", c, E2) for c in range(8)]
                   + [("phi", c, E2) for c in range(8)]
                   + [("gate", c, D) for c in range(8)])
        q_items = ([("wg", c, D) for c in range(22)]
                   + [("wu", c, D) for c in range(22)]
                   + [("wd", c, FI) for c in range(8)])
        items = []
        for k in range(len(q_items)):
            if k < len(a_items):
                items.append(a_items[k])
            items.append(q_items[k])
        items.append(("bias", 0, 0))
        SRCD = {"psi": wpsiT_d, "phi": wphiT_d, "gate": wgateT_d,
                "wg": wg_d, "wu": wu_d, "wd": wd_d}
        NCH = {"wg": 22, "wu": 22, "wd": 8}
        SCRD = {"wg": (sc_g, 8 * FI), "wu": (sc_u, 8 * FI), "wd": (sc_d, 22 * D)}
        HBASE = {"psi": 0, "phi": 8 * E2}
        QK = ("wg", "wu", "wd")
        vm = {}
        slab_free = {}
        slab_mark = {}
        sign_mark = {}
        evac_mark = {}
        qslot = {0: None, 1: None, 2: None}
        ev_last = [0]

        with nc.Block() as blk:
            def emit_dma(i):
                kind, c, w = items[i]

                def _sy(e, i=i, kind=kind, c=c, w=w):
                    if i >= 3:
                        wge(e, sV, slab_free[i - 3])
                        if (i - 3) in sign_mark:
                            wge(e, sG, sign_mark[i - 3])
                    for jb in (i - 1, i - 2):
                        if jb >= 0 and items[jb][0] == "bias":
                            wge(e, sV, vm[jb])
                    if kind == "bias":
                        wge(e, sV, vm[i - 1])
                        wge(e, sV, vm[i - 2])
                        inc(e.dma_start(out=slab_in[0:1, 0:E2], in_=bpsi_d),
                            sX, 16)
                        inc(e.dma_start(out=slab_in[0:1, E2:2 * E2], in_=bphi_d),
                            sX, 16)
                        inc(e.dma_start(out=slab_in[0:1, 2 * E2:2 * E2 + D],
                                        in_=bgate_d), sX, 16)
                        slab_mark[i] = sX.v
                    else:
                        inc(e.dma_start(
                            out=slab_in[:, (i % 3) * FI:(i % 3) * FI + w],
                            in_=SRCD[kind][128 * c:128 * (c + 1), :]),
                            sFg[i % 3], 16)
                        slab_mark[i] = sFg[i % 3].v
                blk.sync(_sy)

            def emit_sign(i):
                kind, c, w = items[i]

                def _s(e, i=i, w=w):
                    wge(e, sG)
                    if i >= 2:
                        wge(e, sV, vm[i - 2])
                    e.wait_ge(sFg[i % 3].h, slab_mark[i])
                    i_ = e.activation(sSg[:, (i % 2) * FI:(i % 2) * FI + w],
                                      slab_in[:, (i % 3) * FI:(i % 3) * FI + w],
                                      ACT.Sign)
                    inc(i_, sG)
                    sign_mark[i] = sG.v
                blk.scalar(_s)

            def wait_qslot(e, spec):
                if spec is None:
                    return
                kind_, val = spec
                if kind_ == "d":
                    e.wait_ge(val[0].h, val[1])
                else:
                    wge(e, sT, val)

            def emit_consume(i):
                kind, c, w = items[i]
                if kind == "bias":
                    def _vb(e, i=i):
                        e.wait_ge(sX.h, slab_mark[i])
                        e.tensor_copy(bpsi_b[:], slab_in[0:1, 0:E2])
                        e.tensor_copy(bphi_b[:], slab_in[0:1, E2:2 * E2])
                        e.tensor_copy(bgat_b[:], slab_in[0:1, 2 * E2:2 * E2 + D])
                        inc(e.drain(), sV)
                        vm[i] = sV.v
                        slab_free[i] = sV.v
                    blk.vector(_vb)
                    return
                if kind in ("psi", "phi", "gate"):
                    spec = qslot[i % 3] if kind != "gate" else None

                    def _v(e, i=i, kind=kind, c=c, w=w, spec=spec):
                        e.wait_ge(sFg[i % 3].h, slab_mark[i])
                        wait_qslot(e, spec)
                        dst = (wgat_b[:, D * c:D * (c + 1)] if kind == "gate"
                               else qout[:, (i % 3) * FI:(i % 3) * FI + w])
                        e.tensor_copy(dst,
                                      slab_in[:, (i % 3) * FI:(i % 3) * FI + w])
                        inc(e.drain(), sV)
                        vm[i] = sV.v
                        slab_free[i] = sV.v
                    blk.vector(_v)
                    if kind != "gate":
                        osem = (sFd[0], sFd[1], sC)[i % 3]

                        def _sy2(e, i=i, kind=kind, c=c, w=w, osem=osem):
                            wge(e, sV, vm[i])
                            inc(e.dma_start(
                                out=sc_w[:, HBASE[kind] + w * c:
                                         HBASE[kind] + w * (c + 1)],
                                in_=qout[:, (i % 3) * FI:(i % 3) * FI + w]),
                                osem, 16)
                        blk.sync(_sy2)
                        qslot[i % 3] = ("d", (osem, osem.v))
                    return
                n_tb = w // 128
                spec = qslot[i % 3]

                def _v(e, i=i, w=w, spec=spec):
                    e.wait_ge(sFg[i % 3].h, slab_mark[i])
                    wait_qslot(e, spec)
                    sl = slab_in[:, (i % 3) * FI:(i % 3) * FI + w]
                    emit_quant(e, sl, w,
                               qout[:, (i % 3) * FI:(i % 3) * FI + w],
                               sQA[:, 0:w], sQB[:, 0:w], sAZ[:, 0:w],
                               sbm[:, 0:w // 16], sr6[:, 0:w // 16],
                               ssd[:, 0:w // 16],
                               sgn_ap=sSg[:, (i % 2) * FI:(i % 2) * FI + w],
                               az_mark=sign_mark[i], fuse=True, az_sem=sG,
                               az_sep=True, early_free=True)
                    slab_free[i] = sV.v
                    inc(e.drain(), sV)
                    vm[i] = sV.v
                blk.vector(_v)
                n_waves = (n_tb + 7) // 8
                wv_marks = []
                ev_base = ev_last[0]
                for wv in range(n_waves):
                    lo, hi = 8 * wv, min(8 * (wv + 1), n_tb)

                    def _t(e, i=i, lo=lo, hi=hi, wv=wv, ev_base=ev_base):
                        wge(e, sS, ev_base + wv)
                        if wv == 0:
                            wge(e, sV, vm[i])
                        for b in range(lo, hi):
                            i_ = e.transpose(
                                PS_Tv[:, 128 * (b - lo):128 * (b - lo + 1)],
                                qout[:, (i % 3) * FI + 128 * b:
                                     (i % 3) * FI + 128 * (b + 1)], identb[:])
                        inc(i_, sT)
                        wv_marks.append(sT.v)
                    blk.tensor(_t)

                def _s(e, i=i, kind=kind, c=c, n_tb=n_tb, n_waves=n_waves,
                       wv_marks=wv_marks):
                    nch = NCH[kind]
                    for wv in range(n_waves):
                        lo, hi = 8 * wv, min(8 * (wv + 1), n_tb)
                        wge(e, sT, wv_marks[wv])
                        if c == 0 and wv == 0:
                            wge(e, sF)
                        bt = bigT[:].rearrange("p (b cc) -> p b cc",
                                               cc=nch * 128)
                        i_ = e.activation(
                            bt[:, lo:hi, 128 * c:128 * (c + 1)],
                            PS_Tv.rearrange("p (b cc) -> p b cc", cc=128)
                            [:, 0:hi - lo, :], ACT.Copy)
                        inc(i_, sS)
                        ev_last[0] = sS.v
                    evac_mark[i] = sS.v
                blk.scalar(_s)
                qslot[i % 3] = ("t", wv_marks[-1])
                if c == NCH[kind] - 1:
                    scr, scr_cols = SCRD[kind]

                    def _syw(e, i=i, scr=scr, scr_cols=scr_cols):
                        wge(e, sS, evac_mark[i])
                        inc(e.dma_start(out=scr[:, 0:scr_cols],
                                        in_=bigT[:, 0:scr_cols]), sF, 16)
                    blk.sync(_syw)

            nit = len(items)
            for i in range(nit - 1):
                emit_dma(i)
                if items[i][0] in QK:
                    emit_sign(i)
                if i >= 1:
                    emit_consume(i - 1)
            emit_consume(nit - 2)
            emit_dma(nit - 1)
            emit_consume(nit - 1)

            def _sy_wb(e):
                wge(e, sFd[0])
                wge(e, sFd[1])
                wge(e, sC)
                inc(e.dma_start(out=wbuf[:], in_=sc_w[:, 0:8 * E2]), sW, 16)
                psiw_mark[0] = sW.v
            blk.sync(_sy_wb)

    with nc.Block() as blk:
        def _fence(e):
            wge(e, sF)
            wge(e, sX)
            wge(e, sFd[0])
            wge(e, sFd[1])
            wge(e, sC)
            wge(e, sG)
        blk.sync(_fence)

    xbuf = sb("xbuf", [128, 2 * E2], F32)
    xn = sb("xn", [128, E2], F32)
    xnT = sb("xnT", [128, 2 * D], BF16)
    psi = sb("psi", [128, 2 * E2], BF16)
    phi = sb("phi", [128, 2 * E2], BF16)
    gat = sb("gat", [128, 2 * D], BF16)
    scr2k = sb("scr2k", [128, E2], BF16)
    xnbf = scr2k
    qsA = sb("qsA", [128, D], F32)
    qsB = sb("qsB", [128, D], F32)
    hfin = sb("hfin", [128, 2 * E2], BF16)
    bmF = sb("bmF", [128, 64], F32)
    r6F = sb("r6F", [128, 64], F32)
    sdF = sb("sdF", [128, 64], F32)
    xqT = sb("xqT", [128, E2], BF16)
    hsl = sb("hsl", [128, 1024], F32)
    silu_t = sb("silu_t", [128, 1024], F32)
    hq = sb("hq", [128, 2 * FI], BF16)
    hqT = sb("hqT", [128, 2 * FI], BF16)
    gusl = sb("gusl", [128, 3 * 4096], BF16)
    dsl = sb("dsl", [128, 2 * E2], BF16)
    sgnb = sb("sgnb", [128, 1024], BF16)

    ISLW = [512, 512, 512, 512, 512, 256]

    fin_mark = [0]
    out_mark = [0]
    phi_mark = [0]
    phiw_mark = [0]
    xqT_mark = [0]
    PS_Abf = PS_A[:].bitcast(BF16)
    PS_Bbf = PS_B[:].bitcast(BF16)
    silu_bf = silu_t[:].bitcast(BF16)
    hqTf = hqT[:].bitcast(F32)

    def em_sy_head(g):
        xsl = E2 * (g % 2)

        def _sy(e):
            wge(e, sO, out_mark[0])
            for i in range(2):
                r = 128 * (2 * g + i)
                inc(e.dma_start(out=xbuf[:, xsl + 1024 * i:xsl + 1024 * (i + 1)],
                                in_=x_d[r:r + 128, :]), sX, 16)
        blk.sync(_sy)

    def em_sy_wb(g):
        def _sy(e):
            wge(e, sT, phi_mark[0])
            inc(e.dma_start(out=wbuf[:], in_=sc_w[:, 0:8 * E2]), sW, 16)
            psiw_mark[0] = sW.v
        blk.sync(_sy)

    def em_v_ln(g):
        xsl = E2 * (g % 2)

        def _v(e):
            wge(e, sX)
            x3 = xbuf[:, xsl:xsl + E2].rearrange("p (t c) -> p t c", c=1024)
            e.reduce_sum(stat[:, 0:2], x3, axis=AX)
            inc(e.drain(), sD)
            i = e.tensor_scalar(stat[:, 0:2], stat[:, 0:2], F(1.0 / 1024.0),
                                None, ALU.mult)
            inc(i, sV)
        blk.vector(_v)
        return sV.v

    def em_s_ln(g):
        xsl = E2 * (g % 2)

        def _s(e):
            wge(e, sS)
            wge(e, sX)
            wge(e, sV)
            for ti in range(2):
                e.activation(scr2k[:, 1024 * ti:1024 * (ti + 1)],
                             xbuf[:, xsl + 1024 * ti:xsl + 1024 * (ti + 1)],
                             ACT.Square, accum_out=stat[:, 4 + ti:5 + ti])
            inc(e.drain(), sS)
        blk.scalar(_s)
        return sS.v

    def em_v_ln2(g, sq_mark):
        xsl = E2 * (g % 2)

        def _va(e):
            wge(e, sS, sq_mark)
            wge(e, sV)
            var, sq = stat[:, 6:8], stat[:, 2:4]
            e.tensor_tensor(sq, stat[:, 0:2], stat[:, 0:2], ALU.mult)
            inc(e.drain(), sD)
            e.scalar_tensor_tensor(var, stat[:, 4:6], F(1.0 / 1024.0), sq,
                                   ALU.mult, ALU.subtract)
            inc(e.drain(), sD)
            i = e.tensor_scalar(var, var, F(1e-5), None, ALU.add)
            inc(i, sV)
        blk.vector(_va)

        def _sr(e):
            wge(e, sS)
            wge(e, sV)
            i = e.activation(stat[:, 10:12], stat[:, 6:8], ACT.Sqrt)
            inc(i, sS)
        blk.scalar(_sr)
        rsq_s = sS.v

        def _v(e):
            wge(e, sS, rsq_s)
            e.reciprocal(stat[:, 8:10], stat[:, 10:12])
            inc(e.drain(), sD)
            e.wait_ge(sD.h, sD.v)
            for ti in range(2):
                e.tensor_scalar(xn[:, 1024 * ti:1024 * (ti + 1)],
                                xbuf[:, xsl + 1024 * ti:xsl + 1024 * (ti + 1)],
                                stat[:, ti:ti + 1], stat[:, 8 + ti:9 + ti],
                                ALU.subtract, ALU.mult)
            inc(e.drain(), sD)
            e.wait_ge(sD.h, sD.v)
            e.tensor_copy(xnbf[:, 0:D], xn[:, 0:D])
            e.tensor_copy(xnbf[:, D:2 * D], xn[:, D:2 * D])
            inc(e.drain(), sV)
        blk.vector(_v)
        return sV.v

    def em_t_tr(g, ti, ln2_v, bank_free_s):
        def _t(e):
            e.wait_ge(sV.h, ln2_v)
            e.wait_ge(sS.h, bank_free_s)
            for k in range(8):
                i = e.transpose(PS_Abf[:, 128 * k:128 * (k + 1)],
                                xnbf[:, D * ti + 128 * k:D * ti + 128 * (k + 1)],
                                identb[:])
            inc(i, sT)
        blk.tensor(_t)
        return sT.v

    def em_s_tr(g, ti, t_mark):
        def _s(e):
            wge(e, sT, t_mark)
            i = e.activation(xnT[:, D * ti:D * (ti + 1)], PS_Abf[:, 0:1024],
                             ACT.Copy)
            inc(i, sS)
        blk.scalar(_s)
        return sS.v

    def em_t_psi(g, ti, s_mark):
        def _t(e):
            wge(e, sS, s_mark)
            e.wait_ge(sW.h, psiw_mark[0])
            for k in range(8):
                for j in range(4):
                    e.matmul(PS_A[:, 512 * j:512 * (j + 1)],
                             xnT[:, D * ti + 128 * k:D * ti + 128 * (k + 1)],
                             wbuf[:, E2 * k + 512 * j:E2 * k + 512 * (j + 1)],
                             start=(k == 0), stop=False)
            for j in range(4):
                i = e.matmul(PS_A[:, 512 * j:512 * (j + 1)], ones1b[:],
                             bpsi_b[0:1, 512 * j:512 * (j + 1)],
                             start=False, stop=True)
            inc(i, sT)
        blk.tensor(_t)
        return sT.v

    def em_s_psi(g, ti, t_mark):
        def _s(e):
            wge(e, sT, t_mark)
            i = e.activation(psi[:, E2 * ti:E2 * (ti + 1)], PS_A[:], ACT.Copy)
            inc(i, sS)
        blk.scalar(_s)
        return sS.v

    def em_t_gat(g, ti, s_mark):
        def _t(e):
            wge(e, sS, s_mark)
            for k in range(8):
                for j in range(2):
                    e.matmul(PS_A[:, 512 * j:512 * (j + 1)],
                             xnT[:, D * ti + 128 * k:D * ti + 128 * (k + 1)],
                             wgat_b[:, D * k + 512 * j:D * k + 512 * (j + 1)],
                             start=(k == 0), stop=False)
            for j in range(2):
                i = e.matmul(PS_A[:, 512 * j:512 * (j + 1)], ones1b[:],
                             bgat_b[0:1, 512 * j:512 * (j + 1)],
                             start=False, stop=True)
            inc(i, sT)
        blk.tensor(_t)
        return sT.v

    def em_s_gat(g, ti, t_mark):
        def _s(e):
            wge(e, sT, t_mark)
            i = e.activation(gat[:, D * ti:D * (ti + 1)], PS_A[:, 0:1024],
                             ACT.Sigmoid)
            inc(i, sS)
        blk.scalar(_s)
        return sS.v

    def em_sy_phi(g, psi1_t):
        def _sy(e):
            wge(e, sT, psi1_t)
            inc(e.dma_start(out=wbuf[:], in_=sc_w[:, 8 * E2:16 * E2]), sW, 16)
            phiw_mark[0] = sW.v
        blk.sync(_sy)

    def em_t_phi(g, ti, s_mark):
        def _t(e):
            wge(e, sS, s_mark)
            e.wait_ge(sW.h, phiw_mark[0])
            for k in range(8):
                for j in range(4):
                    e.matmul(PS_A[:, 512 * j:512 * (j + 1)],
                             xnT[:, D * ti + 128 * k:D * ti + 128 * (k + 1)],
                             wbuf[:, E2 * k + 512 * j:E2 * k + 512 * (j + 1)],
                             start=(k == 0), stop=False)
            for j in range(4):
                i = e.matmul(PS_A[:, 512 * j:512 * (j + 1)], ones1b[:],
                             bphi_b[0:1, 512 * j:512 * (j + 1)],
                             start=False, stop=True)
            inc(i, sT)
            if ti == 1:
                phi_mark[0] = sT.v
        blk.tensor(_t)
        return sT.v

    def em_s_phi(g, ti, t_mark):
        def _s(e):
            wge(e, sT, t_mark)
            i = e.activation(phi[:, E2 * ti:E2 * (ti + 1)], PS_A[:], ACT.Copy)
            inc(i, sS)
        blk.scalar(_s)
        return sS.v

    def em_v_a(g, ti, s_mark, t_mark, dst):
        def _v(e):
            wge(e, sS, s_mark)
            wge(e, sT, t_mark)
            pr = psi[:, E2 * ti:E2 * ti + D]
            pim = psi[:, E2 * ti + D:E2 * (ti + 1)]
            g_ = gat[:, D * ti:D * (ti + 1)]
            t1, t2 = xqT[:, 0:D], xqT[:, D:2 * D]
            e.tensor_tensor(t1, pr, wm_re[:], ALU.mult)
            i = e.tensor_tensor(t2, pim, wm_im[:], ALU.mult)
            inc(i, sD)
            e.wait_ge(sD.h, sD.v)
            i = e.tensor_tensor(dst[:, 0:D], t1, t2, ALU.subtract)
            inc(i, sD)
            e.wait_ge(sD.h, sD.v)
            e.tensor_tensor(t1, pr, wm_im[:], ALU.mult)
            i = e.tensor_tensor(t2, pim, wm_re[:], ALU.mult)
            inc(i, sD)
            e.wait_ge(sD.h, sD.v)
            i = e.tensor_tensor(dst[:, D:E2], t1, t2, ALU.add)
            inc(i, sD)
            e.wait_ge(sD.h, sD.v)
            e.tensor_tensor(dst[:, 0:D], dst[:, 0:D], g_, ALU.mult)
            e.tensor_tensor(dst[:, D:E2], dst[:, D:E2], g_, ALU.mult)
            inc(e.drain(), sV)
        blk.vector(_v)
        return sV.v

    def em_t_c(g, ti, v_mark, s_mark, src):
        tt = 2 * g + ti

        def _t(e):
            wge(e, sV, v_mark)
            wge(e, sS, s_mark)
            first = (tt % seq_tiles == 0)
            prev = E2 * ((tt + 1) % 2)
            for j in range(4):
                i = e.matmul(PS_A[:, 512 * j:512 * (j + 1)], ltb[:],
                             src[:, 512 * j:512 * (j + 1)],
                             start=True, stop=first)
            if not first:
                for j in range(4):
                    i = e.matmul(PS_A[:, 512 * j:512 * (j + 1)], e127b[:],
                                 hfin[:, prev + 512 * j:prev + 512 * (j + 1)],
                                 start=False, stop=True)
            inc(i, sT)
        blk.tensor(_t)
        return sT.v

    def em_s_pv(g, ti, t_mark, dst):
        def _s(e):
            wge(e, sT, t_mark)
            i = e.activation(dst[:], PS_A[:], ACT.Copy)
            inc(i, sS)
        blk.scalar(_s)
        return sS.v

    def em_v_p(g, ti, s_mark, src):
        tt = 2 * g + ti

        def _v(e):
            wge(e, sS, s_mark)
            cur = E2 * (tt % 2)
            t1, t2 = xqT[:, 0:D], xqT[:, D:2 * D]
            e.tensor_tensor(t1, src[:, 0:D], wp_re[:], ALU.mult)
            i = e.tensor_tensor(t2, src[:, D:E2], wp_im[:], ALU.mult)
            inc(i, sD)
            e.wait_ge(sD.h, sD.v)
            i = e.tensor_tensor(hfin[:, cur:cur + D], t1, t2, ALU.subtract)
            inc(i, sD)
            e.wait_ge(sD.h, sD.v)
            e.tensor_tensor(t1, src[:, 0:D], wp_im[:], ALU.mult)
            i = e.tensor_tensor(t2, src[:, D:E2], wp_re[:], ALU.mult)
            inc(i, sD)
            e.wait_ge(sD.h, sD.v)
            i = e.tensor_tensor(hfin[:, cur + D:cur + E2], t1, t2, ALU.add)
            inc(e.drain(), sV)
        blk.vector(_v)
        return sV.v

    def em_v_pr(g):
        def _v(e):
            h3 = hfin[:].rearrange("p (q c) -> p q c", c=1024)
            e.reduce_sum(stat[:, 0:4], h3, axis=AX)
            inc(e.drain(), sD)
            i = e.tensor_scalar(stat[:, 0:4], stat[:, 0:4], F(1.0 / 1024.0),
                                None, ALU.mult)
            inc(i, sV)
        blk.vector(_v)
        return sV.v

    def em_s_pr(g, p1_v):
        def _s(e):
            wge(e, sS)
            wge(e, sV, p1_v)
            dumps = [scr2k[:, 0:1024], scr2k[:, 1024:2048],
                     gat[:, 0:1024], gat[:, 1024:2048]]
            for q in range(4):
                i = e.activation(dumps[q], hfin[:, 1024 * q:1024 * (q + 1)],
                                 ACT.Square, accum_out=stat[:, 8 + q:9 + q])
            inc(i, sS)
        blk.scalar(_s)
        return sS.v

    def em_v_pr2(g, s_mark):
        def _v(e):
            wge(e, sS, s_mark)
            wge(e, sV)
            sq = stat[:, 4:8]
            e.tensor_tensor(sq, stat[:, 0:4], stat[:, 0:4], ALU.mult)
            inc(e.drain(), sD)
            i = e.scalar_tensor_tensor(stat[:, 8:12], stat[:, 8:12],
                                       F(1.0 / 1024.0), sq, ALU.mult,
                                       ALU.subtract)
            inc(i, sV)
        blk.vector(_v)
        return sV.v

    def em_s_pr2(g, v_mark):
        def _s(e):
            wge(e, sV, v_mark)
            i = e.sqrt(stat[:, 12:16], stat[:, 8:12])
            inc(i, sS)
        blk.scalar(_s)
        return sS.v

    def em_v_pr3(g, s_mark):
        xsl = E2 * (g % 2)

        def _v(e):
            wge(e, sS, s_mark)
            y, rr = stat[:, 12:16], stat[:, 4:8]
            e.tensor_scalar(y, y, F(1e-6), None, ALU.add)
            inc(e.drain(), sD)
            e.reciprocal(rr, y)
            inc(e.drain(), sD)
            for ti in range(2):
                for p_ in range(2):
                    q = 2 * ti + p_
                    e.tensor_scalar(psi[:, E2 * ti + 1024 * p_:
                                        E2 * ti + 1024 * (p_ + 1)],
                                    hfin[:, 1024 * q:1024 * (q + 1)],
                                    stat[:, q:q + 1], stat[:, 4 + q:5 + q],
                                    ALU.subtract, ALU.mult)
            inc(e.drain(), sD)
            e.wait_ge(sD.h, sD.v)
            for ti in range(2):
                e.tensor_tensor(psi[:, E2 * ti:E2 * ti + D],
                                psi[:, E2 * ti:E2 * ti + D],
                                phi[:, E2 * ti:E2 * ti + D], ALU.mult)
                e.tensor_tensor(psi[:, E2 * ti + D:E2 * (ti + 1)],
                                psi[:, E2 * ti + D:E2 * (ti + 1)],
                                phi[:, E2 * ti + D:E2 * (ti + 1)], ALU.mult)
            inc(e.drain(), sD)
            e.wait_ge(sD.h, sD.v)
            for ti in range(2):
                a_ap = xbuf[:, xsl + 1024 * ti:xsl + 1024 * (ti + 1)]
                e.tensor_tensor(a_ap, a_ap, psi[:, E2 * ti:E2 * ti + D], ALU.add)
                inc(e.drain(), sD)
                e.wait_ge(sD.h, sD.v)
                i = e.tensor_tensor(a_ap, a_ap,
                                    psi[:, E2 * ti + D:E2 * (ti + 1)], ALU.add)
                inc(i, sD)
            inc(e.drain(), sV)
        blk.vector(_v)
        return sV.v

    def em_s_xqp(g, a1_v):
        def _s(e):
            wge(e, sS)
            wge(e, sV, a1_v)
            i = e.activation(gat[:, 0:2 * D], xn[:, 0:2 * D], ACT.Sign)
            inc(i, sS)
        blk.scalar(_s)
        return sS.v

    def em_v_xq(g, ti, sgn_s, free_t):
        def _v(e):
            if free_t is not None:
                wge(e, sT, free_t)
            emit_quant(e, xn[:, 1024 * ti:1024 * (ti + 1)], D,
                       hq[:, 1024 * ti:1024 * (ti + 1)],
                       qsA[:], qsB[:], hsl[:],
                       bmF[:, 0:64], r6F[:, 0:64], sdF[:, 0:64],
                       sgn_ap=gat[:, D * ti:D * (ti + 1)], az_mark=sgn_s,
                       fuse=True, az_sep=True)
            inc(e.drain(), sV)
        blk.vector(_v)
        return sV.v

    def em_t_xqtr(g, v_mark, ps_free_s):
        def _t(e):
            wge(e, sV, v_mark)
            wge(e, sS, ps_free_s)
            for ti in range(2):
                dstp = PS_Abf[:, 1024 * ti:1024 * (ti + 1)]
                for k in range(8):
                    i = e.transpose(dstp[:, 128 * k:128 * (k + 1)],
                                    hq[:, 1024 * ti + 128 * k:
                                       1024 * ti + 128 * (k + 1)], identb[:])
            inc(i, sT)
        blk.tensor(_t)
        return sT.v

    def em_s_xqe(g, t_mark):
        def _s(e):
            wge(e, sT, t_mark)
            i = e.activation(xqT[:, 0:2 * D], PS_Abf[:, 0:2048], ACT.Copy)
            inc(i, sS)
            xqT_mark[0] = sS.v
        blk.scalar(_s)
        return sS.v

    def em_t_gu(g, s, usv, vb_h1, expect):
        w = ISLW[s]
        gj, uj = 2 * s, 2 * s + 1

        def _t(e):
            if s == 0:
                wge(e, sS, xqT_mark[0])
                wge(e, sV, fin_mark[0])
            else:
                e.wait_ge(sV.h, vb_h1[s - 1])
            e.wait_ge(sFg[gj % 3].h, usv[gj])
            e.wait_ge(sFg[uj % 3].h, usv[uj])
            gsl_ap = gusl[:, 4096 * (gj % 3):4096 * (gj % 3) + 8 * w]
            usl_ap = gusl[:, 4096 * (uj % 3):4096 * (uj % 3) + 8 * w]
            for k in range(8):
                for (ti, po) in ((0, 0), (1, 1024)):
                    xt = xqT[:, D * ti + 128 * k:D * ti + 128 * (k + 1)]
                    e.matmul(PS_B[:, po:po + w], xt,
                             gsl_ap[:, w * k:w * (k + 1)],
                             start=(k == 0), stop=(k == 7))
                    i = e.matmul(PS_B[:, po + 512:po + 512 + w], xt,
                                 usl_ap[:, w * k:w * (k + 1)],
                                 start=(k == 0), stop=(k == 7))
            inc(i, sT)
            assert sT.v == expect, (sT.v, expect)
        blk.tensor(_t)
        return sT.v

    def em_sy_gu(g, usv, gu_plan, j0, jhi=12):
        def _sy(e):
            for j in range(j0, jhi):
                mat, s = j % 2, j // 2
                w = ISLW[s]
                scr = sc_u if mat else sc_g
                slot = gusl[:, 4096 * (j % 3):4096 * (j % 3) + 8 * w]
                if j >= 3:
                    wge(e, sT, gu_plan[(j - 3) // 2])
                src_ = scr[:].rearrange("p (k c) -> p k c", c=FI)[:, :,
                                                                 512 * s:512 * s + w]
                inc(e.dma_start(out=slot.rearrange("p (k c) -> p k c", c=w),
                                in_=src_), sFg[j % 3], 16)
                assert sFg[j % 3].v == usv[j]
        blk.sync(_sy)

    pre_usv = {}

    def em_sy_gu_pre(nxt, gu_plan_cur):
        def _sy(e):
            for j in range(3):
                mat, s = j % 2, j // 2
                w = ISLW[s]
                scr = sc_u if mat else sc_g
                slot = gusl[:, 4096 * j:4096 * j + 8 * w]
                wge(e, sT, gu_plan_cur[(j + 9) // 2])
                src_ = scr[:].rearrange("p (k c) -> p k c", c=FI)[:, :,
                                                                 512 * s:512 * s + w]
                inc(e.dma_start(out=slot.rearrange("p (k c) -> p k c", c=w),
                                in_=src_), sFg[j], 16)
                pre_usv[j] = sFg[j].v
        blk.sync(_sy)

    def em_s_sig(g, s, gu_t):
        w = ISLW[s]

        def _s(e):
            wge(e, sS)
            wge(e, sT, gu_t)
            ps3 = PS_B[:].rearrange("p (t c) -> p t c", c=1024)[:, :, 0:w]
            sl3 = silu_t[:, 0:2 * w].rearrange("p (t c) -> p t c", c=w)
            i = e.activation(sl3, ps3, ACT.Silu)
            inc(i, sS)
        blk.scalar(_s)
        return sS.v

    def em_v_h1(g, s, sig_s, expect):
        w = ISLW[s]

        def _v(e):
            wge(e, sS, sig_s)
            sl3 = silu_t[:, 0:2 * w].rearrange("p (t c) -> p t c", c=w)
            u3 = PS_B[:].rearrange("p (t c) -> p t c", c=1024)[:, :, 512:512 + w]
            h3 = hsl[:, 0:2 * w].rearrange("p (t c) -> p t c", c=w)
            i = e.tensor_tensor(h3, sl3, u3, ALU.mult)
            inc(i, sV)
            assert sV.v == expect, (sV.v, expect)
        blk.vector(_v)
        return sV.v

    def em_s_sgn(g, s, h1_v):
        w = ISLW[s]

        def _s(e):
            wge(e, sS)
            wge(e, sV, h1_v)
            i = e.activation(sgnb[:, 0:2 * w], hsl[:, 0:2 * w], ACT.Sign)
            inc(i, sS)
        blk.scalar(_s)
        return sS.v

    def em_v_h2(g, s, sgn_s, expect):
        w = ISLW[s]
        off = 512 * s

        def _v(e):
            wge(e, sV)
            nb2 = 2 * w // 16
            hq3 = hq[:].rearrange("p (t c) -> p t c", c=FI)[:, :, off:off + w]
            emit_quant(e, hsl[:, 0:2 * w], 2 * w, hq3,
                       qsA[:, 0:2 * w], qsB[:, 0:2 * w], hqTf[:, 0:2 * w],
                       bmF[:, 0:nb2], r6F[:, 0:nb2], sdF[:, 0:nb2],
                       sgn_ap=sgnb[:, 0:2 * w], az_mark=sgn_s, fuse=True,
                       az_sep=True)
            inc(e.drain(), sV)
            assert sV.v == expect, (sV.v, expect)
        blk.vector(_v)
        return sV.v

    HTR_WAVES = [(ti, 128 * 8 * wv, min(128 * 8 * (wv + 1), FI))
                 for ti in range(2) for wv in range(3)]

    def em_t_htrw(g, q5_v, ebase):
        htr_marks = []
        for widx, (ti, lo, hi) in enumerate(HTR_WAVES):
            def _t(e, ti=ti, lo=lo, hi=hi, widx=widx):
                e.wait_ge(sS.h, ebase + max(0, widx - 1))
                if widx == 0:
                    wge(e, sV, q5_v)
                dstp = PS_Bbf[:, 0:1024] if widx % 2 == 0 else PS_Bbf[:, 1024:2048]
                nb = (hi - lo) // 128
                for b in range(nb):
                    i = e.transpose(dstp[:, 128 * b:128 * (b + 1)],
                                    hq[:, FI * ti + lo + 128 * b:
                                       FI * ti + lo + 128 * (b + 1)], identb[:])
                inc(i, sT)
                htr_marks.append(sT.v)
            blk.tensor(_t)
        return htr_marks

    def em_s_htr(g, htr_marks, ebase):
        def _s(e):
            for widx, (ti, lo, hi) in enumerate(HTR_WAVES):
                wge(e, sT, htr_marks[widx])
                srcp = PS_Bbf[:, 0:1024] if widx % 2 == 0 else PS_Bbf[:, 1024:2048]
                i = e.activation(hqT[:, FI * ti + lo:FI * ti + hi],
                                 srcp[:, 0:hi - lo], ACT.Copy)
                inc(i, sS)
                assert sS.v == ebase + widx + 1, (sS.v, ebase, widx)
        blk.scalar(_s)
        return sS.v

    def em_t_dn(g, s_mark, v_mark):
        dpair_freed = {}
        dpair_sem_val = {}
        for kp in range(11):
            dpair_sem_val[kp] = sFd[kp % 2].v + 16 * (kp // 2 + 1)

        def _t(e):
            wge(e, sS, s_mark)
            wge(e, sV, v_mark)
            for kp in range(11):
                e.wait_ge(sFd[kp % 2].h, dpair_sem_val[kp])
                for half in range(2):
                    kk = 2 * kp + half
                    for ti in range(2):
                        st_ = hqT[:, FI * ti + 128 * kk:FI * ti + 128 * (kk + 1)]
                        for j in range(2):
                            i = e.matmul(
                                PS_B[:, 1024 * ti + 512 * j:1024 * ti + 512 * (j + 1)],
                                st_,
                                dsl[:, E2 * (kp % 2) + D * half + 512 * j:
                                    E2 * (kp % 2) + D * half + 512 * (j + 1)],
                                start=(kk == 0), stop=(kk == 21))
                inc(i, sT)
                dpair_freed[kp] = sT.v
        blk.tensor(_t)

        def _sy(e):
            for kp in range(11):
                if kp >= 2:
                    wge(e, sT, dpair_freed[kp - 2])
                inc(e.dma_start(out=dsl[:, E2 * (kp % 2):E2 * (kp % 2) + E2],
                                in_=sc_d[:, E2 * kp:E2 * (kp + 1)]),
                    sFd[kp % 2], 16)
                assert sFd[kp % 2].v == dpair_sem_val[kp]
        blk.sync(_sy)
        return sT.v

    def em_v_fin(g, dn_t):
        xsl = E2 * (g % 2)

        def _v(e):
            wge(e, sT, dn_t)
            for ti in range(2):
                a_ap = xbuf[:, xsl + 1024 * ti:xsl + 1024 * (ti + 1)]
                e.tensor_tensor(a_ap, a_ap, PS_B[:, 1024 * ti:1024 * (ti + 1)],
                                ALU.add)
                inc(e.drain(), sD)
            inc(e.drain(), sV)
            fin_mark[0] = sV.v
        blk.vector(_v)
        return sV.v

    def em_sy_out(g):
        xsl = E2 * (g % 2)

        def _sy(e):
            wge(e, sV, fin_mark[0])
            for ti in range(2):
                r = 128 * (2 * g + ti)
                inc(e.dma_start(out=out_d[r:r + 128, :],
                                in_=xbuf[:, xsl + 1024 * ti:xsl + 1024 * (ti + 1)]),
                    sO, 16)
            out_mark[0] = sO.v
        blk.sync(_sy)

    def emit_attn0():
        em_sy_head(0)
        em_v_ln(0)
        m = em_s_ln(0)
        ln2 = em_v_ln2(0, m)
        m = em_t_tr(0, 0, ln2, xqT_mark[0])
        m = em_s_tr(0, 0, m)
        m = em_t_psi(0, 0, m)
        psi1 = None
        m = em_s_psi(0, 0, m)
        m = em_t_gat(0, 0, m)
        m = em_s_gat(0, 0, m)
        tr1m = em_t_tr(0, 1, ln2, m)
        m = em_s_tr(0, 1, tr1m)
        psi1 = em_t_psi(0, 1, m)
        em_sy_phi(0, psi1)
        m = em_s_psi(0, 1, psi1)
        m = em_t_gat(0, 1, m)
        sg1 = em_s_gat(0, 1, m)
        m = em_t_phi(0, 0, sg1)
        m = em_s_phi(0, 0, m)
        m = em_t_phi(0, 1, m)
        sphi1 = em_s_phi(0, 1, m)
        a0 = em_v_a(0, 0, sphi1, tr1m, scr2k)
        c0 = em_t_c(0, 0, a0, sphi1, scr2k)
        a1 = em_v_a(0, 1, sphi1, tr1m, silu_bf)
        pv0 = em_s_pv(0, 0, c0, scr2k)
        p0 = em_v_p(0, 0, pv0, scr2k)
        c1 = em_t_c(0, 1, p0, pv0, silu_bf)
        pv1 = em_s_pv(0, 1, c1, silu_bf)
        p1 = em_v_p(0, 1, pv1, silu_bf)
        em_v_pr(0)
        m = em_s_pr(0, p1)
        m = em_v_pr2(0, m)
        m = em_s_pr2(0, m)
        em_v_pr3(0, m)
        sgn_s = em_s_xqp(0, a1)
        em_v_xq(0, 0, sgn_s, None)
        xqv = em_v_xq(0, 1, sgn_s, None)
        m = em_t_xqtr(0, xqv, pv1)
        em_s_xqe(0, m)

    def emit_period(g):
        nxt = g + 1
        has_next = nxt < ng
        tbase = sT.v
        vbase = sV.v
        if has_next:
            gu_plan = {s: tbase + [1, 2, 5, 7, 9, 12][s] for s in range(6)}
            vb_h1 = {s: vbase + [1, 6, 8, 10, 12, 14][s] for s in range(6)}
            vb_q = {s: vbase + [2, 7, 9, 11, 13, 15][s] for s in range(6)}
            vb_a1 = vbase + 17
            vb_p1 = vbase + 21
        else:
            gu_plan = {s: tbase + s + 1 for s in range(6)}
            vb_h1 = {s: vbase + 2 * s + 1 for s in range(6)}
            vb_q = {s: vbase + 2 * s + 2 for s in range(6)}
        if g == 0:
            usv = {j: sFg[j % 3].v + 16 * (j // 3 + 1) for j in range(12)}
            j0 = 0
        else:
            usv = dict(pre_usv)
            for j in range(3, 12):
                usv[j] = sFg[j % 3].v + 16 * ((j - 3) // 3 + 1)
            j0 = 3

        if has_next:
            em_sy_head(nxt)
            em_sy_gu(g, usv, gu_plan, j0, 7)
            em_sy_wb(nxt)
            em_sy_gu(g, usv, gu_plan, 7, 12)
            em_sy_phi(nxt, tbase + 10)
        else:
            em_sy_gu(g, usv, gu_plan, j0)

        gu0 = em_t_gu(g, 0, usv, vb_h1, gu_plan[0])
        sig0 = em_s_sig(g, 0, gu0)
        h1v = em_v_h1(g, 0, sig0, vb_h1[0])
        sgn0 = em_s_sgn(g, 0, h1v)
        if has_next:
            mq = em_s_ln(nxt)
        em_v_h2(g, 0, sgn0, vb_q[0])

        gu1 = em_t_gu(g, 1, usv, vb_h1, gu_plan[1])
        sig1 = em_s_sig(g, 1, gu1)
        if has_next:
            em_v_ln(nxt)
            ln2 = em_v_ln2(nxt, mq)
            trm = em_t_tr(nxt, 0, ln2, xqT_mark[0])
            str0 = em_s_tr(nxt, 0, trm)
        h1v = em_v_h1(g, 1, sig1, vb_h1[1])
        sgn1 = em_s_sgn(g, 1, h1v)
        em_v_h2(g, 1, sgn1, vb_q[1])
        if has_next:
            tpsi0 = em_t_psi(nxt, 0, str0)

        gu2 = em_t_gu(g, 2, usv, vb_h1, gu_plan[2])
        sig2 = em_s_sig(g, 2, gu2)
        h1v = em_v_h1(g, 2, sig2, vb_h1[2])
        if has_next:
            spsi0 = em_s_psi(nxt, 0, tpsi0)
        sgn2 = em_s_sgn(g, 2, h1v)
        em_v_h2(g, 2, sgn2, vb_q[2])
        if has_next:
            tgat0 = em_t_gat(nxt, 0, spsi0)

        gu3 = em_t_gu(g, 3, usv, vb_h1, gu_plan[3])
        sig3 = em_s_sig(g, 3, gu3)
        h1v = em_v_h1(g, 3, sig3, vb_h1[3])
        if has_next:
            sgat0 = em_s_gat(nxt, 0, tgat0)
            trm1 = em_t_tr(nxt, 1, ln2, sgat0)
            str1 = em_s_tr(nxt, 1, trm1)
        sgn3 = em_s_sgn(g, 3, h1v)
        em_v_h2(g, 3, sgn3, vb_q[3])

        gu4 = em_t_gu(g, 4, usv, vb_h1, gu_plan[4])
        sig4 = em_s_sig(g, 4, gu4)
        h1v = em_v_h1(g, 4, sig4, vb_h1[4])
        if has_next:
            tpsi1 = em_t_psi(nxt, 1, str1)
            assert tpsi1 == tbase + 10, (tpsi1, tbase)
            spsi1 = em_s_psi(nxt, 1, tpsi1)
        sgn4 = em_s_sgn(g, 4, h1v)
        em_v_h2(g, 4, sgn4, vb_q[4])
        if has_next:
            tgat1 = em_t_gat(nxt, 1, spsi1)
            sgat1 = em_s_gat(nxt, 1, tgat1)

        gu5 = em_t_gu(g, 5, usv, vb_h1, gu_plan[5])
        sig5 = em_s_sig(g, 5, gu5)
        h1v = em_v_h1(g, 5, sig5, vb_h1[5])
        sgn5 = em_s_sgn(g, 5, h1v)
        q5 = em_v_h2(g, 5, sgn5, vb_q[5])
        if has_next:
            tphi0 = em_t_phi(nxt, 0, sgat1)
            sphi0 = em_s_phi(nxt, 0, tphi0)
            tphi1 = em_t_phi(nxt, 1, sphi0)
            sphi1 = em_s_phi(nxt, 1, tphi1)

        if has_next:
            em_sy_gu_pre(nxt, gu_plan)
        ebase = sS.v
        htr_marks = em_t_htrw(g, q5, ebase)
        shtr = em_s_htr(g, htr_marks, ebase)
        if has_next:
            sgn_s = em_s_xqp(nxt, vb_a1)
            a0 = em_v_a(nxt, 0, sgat1, trm1, scr2k)
            assert a0 == vbase + 16
            c0 = em_t_c(nxt, 0, a0, sphi1, scr2k)
            a1 = em_v_a(nxt, 1, sgat1, trm1, silu_bf)
            assert a1 == vb_a1
            em_v_xq(nxt, 0, sgn_s, htr_marks[-1])
            pv0 = em_s_pv(nxt, 0, c0, scr2k)
            dn_t = em_t_dn(g, shtr, vb_h1[5])
            p0 = em_v_p(nxt, 0, pv0, scr2k)
            c1 = em_t_c(nxt, 1, p0, pv0, silu_bf)
            xqv = em_v_xq(nxt, 1, sgn_s, None)
            pv1 = em_s_pv(nxt, 1, c1, silu_bf)
            p1 = em_v_p(nxt, 1, pv1, silu_bf)
            assert p1 == vb_p1
            em_v_pr(nxt)
            m = em_s_pr(nxt, vb_p1)
            m2 = em_v_pr2(nxt, m)
            m = em_s_pr2(nxt, m2)
            fv = em_v_fin(g, dn_t)
            assert fv == vbase + 24
            em_sy_out(g)
            em_v_pr3(nxt, m)
            mt = em_t_xqtr(nxt, vb_p1, pv1)
            em_s_xqe(nxt, mt)
        else:
            dn_t = em_t_dn(g, shtr, vb_h1[5])
            fv = em_v_fin(g, dn_t)
            assert fv == vbase + 13
            em_sy_out(g)

    with nc.Block() as blk:
        emit_attn0()
        for g in range(ng):
            emit_period(g)

        def _sy_end(e):
            wge(e, sO)
        blk.sync(_sy_end)

    ctx.close()
    return nc


def _shard_inputs(inputs, tpc=4096):
    x = np.asarray(inputs["L0_attn_x"])
    B = x.shape[0]
    per = tpc // x.shape[1]
    wpsiT = np.ascontiguousarray(np.asarray(inputs["L0_attn_W_psi"]).T)
    wphiT = np.ascontiguousarray(np.asarray(inputs["L0_attn_W_phi"]).T)
    wgateT = np.ascontiguousarray(np.asarray(inputs["L0_attn_W_gate"]).T)
    shared = {
        "wpsiT": wpsiT, "wphiT": wphiT, "wgateT": wgateT,
        "bpsi": np.ascontiguousarray(np.asarray(inputs["L0_attn_b_psi"]).reshape(1, -1)),
        "bphi": np.ascontiguousarray(np.asarray(inputs["L0_attn_b_phi"]).reshape(1, -1)),
        "bgate": np.ascontiguousarray(np.asarray(inputs["L0_attn_b_gate"]).reshape(1, -1)),
        "omega": np.ascontiguousarray(np.asarray(inputs["L0_attn_omega"]).reshape(1, -1)),
        "lgam": np.ascontiguousarray(np.asarray(inputs["L0_attn_log_gamma"]).reshape(1, -1)),
        "dt": np.ascontiguousarray(np.asarray(inputs["L0_attn_dt"]).reshape(1, 1)),
        "wg": np.ascontiguousarray(np.asarray(inputs["L0_ffn_w_gate"])),
        "wu": np.ascontiguousarray(np.asarray(inputs["L0_ffn_w_up"])),
        "wd": np.ascontiguousarray(np.asarray(inputs["L0_ffn_w_down"])),
    }
    in_maps = []
    ncores = B // per
    for c in range(ncores):
        m = dict(shared)
        m["x"] = np.ascontiguousarray(
            x[c * per:(c + 1) * per].reshape(tpc, D))
        in_maps.append(m)
    return in_maps, ncores


def kernel(**inputs):
    x = np.asarray(inputs["L0_attn_x"])
    B, L, _ = x.shape
    tpc = (B * L) // 8
    in_maps, ncores = _shard_inputs(inputs, tpc)
    nc = build_program(tpc=tpc, seq_tiles=L // 128)
    res = run_bass_kernel_spmd(nc, in_maps, core_ids=list(range(ncores)))
    outs = [np.asarray(r["out"]).reshape(-1, L, D) for r in res.results]
    return np.concatenate(outs, axis=0).astype(np.float32)
